# version 13
# baseline (speedup 1.0000x reference)
import numpy as np
import ml_dtypes
from contextlib import ExitStack

import concourse.bass as bass
import concourse.mybir as mybir
from concourse.bass_utils import run_bass_kernel_spmd

F32 = mybir.dt.float32
BF16 = mybir.dt.bfloat16
I32 = mybir.dt.int32
AF = mybir.ActivationFunctionType
ALU = mybir.AluOpType
AX = mybir.AxisListType

C = 1024
CT = 8
I_FFN = 2816
IT = 22
EPS = 1e-5
QMAX = 127.0

ENGINES = ("sync", "tensor", "vector", "scalar", "gpsimd")
DMA_RR = 4
DEBUG_OUTPUTS = ()
DEBUG_SCALES = False


class Sched:

    def __init__(self):
        self.groups = []
        self.sem_count = {}
        self.res = {}
        self.floor = {e: {} for e in ENGINES}
        self.max_waited = {}
        self.dma_rr = {}

    def _rec(self, name):
        return self.res.setdefault(name, {"w": None, "r": []})

    def _bump(self, sem, delta):
        self.sem_count[sem] = self.sem_count.get(sem, 0) + delta
        return self.sem_count[sem]

    def group(self, engine, reads=(), writes=(), sem=None, dma_sem=None, waits=(),
              serialize=False):
        sch = self
        want = {}
        if dma_sem is not None:
            idx = self.dma_rr.get(dma_sem, 0)
            self.dma_rr[dma_sem] = (idx + 1) % DMA_RR
            dma_sem = f"{dma_sem}{idx}"

        def add(sw):
            if sw is None:
                return
            s, v = sw
            if s == engine:
                return
            if want.get(s, 0) < v:
                want[s] = v

        for r in reads:
            add(self._rec(r)["w"])
        for w in writes:
            rec = self._rec(w)
            add(rec["w"])
            for sw in rec["r"]:
                add(sw)
        for sw in waits:
            add(sw)
        if dma_sem is not None:
            add((dma_sem, self.sem_count.get(dma_sem, 0)))
        elif sem and sem != engine:
            add((sem, self.sem_count.get(sem, 0)))
        for s, v in want.items():
            if self.max_waited.get(s, 0) < v:
                self.max_waited[s] = v
        fl = self.floor[engine]
        pruned = {s: v for s, v in want.items() if fl.get(s, 0) < v}
        for s, v in pruned.items():
            fl[s] = v

        class G:
            __slots__ = ("ops",)

            def __init__(g):
                g.ops = []

            def op(g, fn):
                g.ops.append(fn)
                return g

            def done(g):
                if dma_sem is not None:
                    incs = [(dma_sem, 16)] * len(g.ops)
                    val = sch._bump(dma_sem, 16 * len(g.ops))
                    mysem = dma_sem
                    mode = "dma"
                else:
                    s = sem or engine
                    n = len(g.ops) if serialize else 1
                    val = sch._bump(s, n)
                    incs = [(s, 1)] * n
                    mysem = s
                    mode = ("ser", val - n) if serialize else "eng"
                    if s == engine:
                        fl2 = sch.floor[engine]
                        fl2[s] = max(fl2.get(s, 0), val)
                sch.groups.append((engine, pruned, list(g.ops), incs, mode))
                for w in writes:
                    rec = sch._rec(w)
                    rec["w"] = (mysem, val)
                    rec["r"] = []
                for r in reads:
                    sch._rec(r)["r"].append((mysem, val))
                return (mysem, val)

        return G()

    def wait_only(self, engine, waits):
        fl = self.floor[engine]
        pruned = {}
        for s, v in waits:
            if s == engine or fl.get(s, 0) >= v:
                continue
            pruned[s] = max(pruned.get(s, 0), v)
            fl[s] = pruned[s]
            if self.max_waited.get(s, 0) < v:
                self.max_waited[s] = v
        if pruned:
            self.groups.append((engine, pruned, [], [], False))

    def queue_totals(self, base):
        return [(s, v) for s, v in self.sem_count.items()
                if s.startswith(base) and s[len(base):].isdigit()]

    def mark(self):
        return len(self.groups)

    def replay(self, engine, eng, sems, lo, hi):
        for (e, waits, ops, incs, mode) in self.groups[lo:hi]:
            if e != engine:
                continue
            for s, v in sorted(waits.items()):
                eng.wait_ge(sems[s], v)
            if not ops:
                continue
            if mode == "dma":
                for fn, (s, d) in zip(ops, incs):
                    fn(eng).then_inc(sems[s], d)
            elif isinstance(mode, tuple):
                s = incs[0][0]
                start = mode[1]
                for k, fn in enumerate(ops):
                    if k > 0:
                        eng.wait_ge(sems[s], start + k)
                    fn(eng).then_inc(sems[s], 1)
            else:
                last = None
                for fn in ops:
                    last = fn(eng)
                s, d = incs[0]
                last.then_inc(sems[s], d)


def build_program(T=4096, NA=256, NF=512):
    nCA = T // NA
    nCB = nCA
    nCF = T // NF
    nc = bass.Bass("TRN2", target_bir_lowering=False, detect_race_conditions=False)

    dram = {}

    def din(name, shape, dt=F32):
        dram[name] = nc.dram_tensor(name, shape, dt, kind="ExternalInput").ap()
        return dram[name]

    def dint(name, shape, dt=F32):
        kind = "ExternalOutput" if name in DEBUG_OUTPUTS else "Internal"
        dram[name] = nc.dram_tensor(name, shape, dt, kind=kind).ap()
        return dram[name]

    xT = din("xT", [C, T])
    WqT = din("WqT", [C, C]); WkT = din("WkT", [C, C]); WvT = din("WvT", [C, C])
    WgT = din("WgT", [C, C]); WoT = din("WoT", [C, C])
    W0 = din("W0", [C, C]); ATd = din("AT", [8, C]); Bmd = din("Bm", [8, C])
    log_lr = din("log_lr", [C]); ln_g = din("ln_g", [C]); ln_b = din("ln_b", [C]); ls_g = din("ls_g", [C])
    clip_x = din("clip_x", [1]); clip_down = din("clip_down", [1])
    wg_r = din("wg_r", [I_FFN, C]); wu_r = din("wu_r", [I_FFN, C]); wd_r = din("wd_r", [C, I_FFN])
    ps_row_in = din("ps_row", [1, T])
    mask8 = din("mask8", [8])
    outT = nc.dram_tensor("outT", [C, T], F32, kind="ExternalOutput").ap()

    xa_sp = dint("xa_sp", [C, T])
    hh_sp = dint("hh_sp", [C, T], BF16); hl_sp = dint("hl_sp", [C, T], BF16)
    y0_sp = dint("y0_sp", [C, T]); gs_sp = dint("gs_sp", [C, T])
    x1_sp = dint("x1_sp", [C, T]); y_sp = dint("y_sp", [C, T])
    z_sp = dint("z_sp", [I_FFN, T])
    wqg_sp = dint("wqg_sp", [I_FFN, C], BF16)
    wqu_sp = dint("wqu_sp", [I_FFN, C], BF16)
    wqd_sp = dint("wqd_sp", [C, I_FFN], BF16)
    cc_gin = dint("cc_gin", [1024]); cc_gout = dint("cc_gout", [8, 1024])
    cc_m1in = dint("cc_m1in", [128]); cc_m1out = dint("cc_m1out", [1024])
    cc_m2in = dint("cc_m2in", [128]); cc_m2out = dint("cc_m2out", [1024])
    off_hbm = dint("off_hbm", [1024]); cg_hbm = dint("cg_hbm", [1024])

    sch = Sched()
    estack = ExitStack()
    sem_names = list(ENGINES) + ["cc"] + [f"{b}{i}" for b in ("qs", "qg", "qs2")
                                           for i in range(DMA_RR)]
    sems = {n: estack.enter_context(nc.semaphore(n)) for n in sem_names}

    def sb(name, shape, dt=F32):
        return nc.alloc_sbuf_tensor(name, shape, dt)

    lr_t = sb("lr_t", [128, CT])
    g_t = sb("g_t", [128, CT]); b_t = sb("b_t", [128, CT]); ls_t = sb("ls_t", [128, CT])
    off_t = sb("off_t", [128, CT])
    scg = sb("scg", [128, IT]); scu = sb("scu", [128, IT]); scd = sb("scd", [128, CT])
    sgq = sb("sgq", [128, IT]); suq = sb("suq", [128, IT]); sdq = sb("sdq", [128, CT])
    s1v = sb("s1v", [128, 1]); s2v = sb("s2v", [128, 1])
    s1p = sb("s1p", [128, 1]); s2p = sb("s2p", [128, 1])
    rmax = sb("rmax", [128, 16 * CT])
    rmax2 = sb("rmax2", [128, 8 * IT])

    ones1 = sb("ones1", [128, 1]); onesk1 = sb("onesk1", [1, 128])
    ginv_t = sb("ginv_t", [128, CT]); nb_t = sb("nb_t", [128, CT])
    bg_t = sb("bg_t", [128, CT]); ngate_t = sb("ngate_t", [128, CT])
    mask_t = sb("mask_t", [8, 1])
    sc1x1 = sb("sc1x1", [1, 8])
    clipt = sb("clipt", [1, 2])
    redt = sb("redt", [128, 1])
    wmax = sb("wmax", [128, 1])
    NB = 256
    NS = NB
    stv_buf = sb("stv", [1, 13 * NS])
    _SLOT = {"r1": 0, "mrn": 1, "r12": 2, "hc2": 3,
             "r1b": 0, "mrnb": 1, "psc": 2,
             "mu": 4, "var": 5, "te": 6, "te2": 7, "s0": 8, "s2": 9, "r0": 12,
             "r2": 10, "tmp1": 11, "tmp2": 12,
             "e2": 12, "a": 11, "b2": 11, "cc2": 11, "rte": 12, "v1r": 12}

    def SV(n):
        i = _SLOT[n]
        return stv_buf.ap()[:, i * NS:(i + 1) * NS]
    psb = [nc.alloc_psum_tensor(f"ps{i}", [128, 512], F32) for i in range(8)]
    RG = [list(range(8))]

    def hbm3(t, t0, N):
        return t.rearrange("(k p) t -> p k t", p=128)[:, :, t0:t0 + N]

    def sbuf3(buf, N, nt=CT):
        return buf.ap().rearrange("p (k n) -> p k n", k=nt)

    with ExitStack() as p1stack:
        _w = p1stack.enter_context
        wAh = _w(nc.sbuf_tensor("wAh", [128, 8 * C], BF16))
        wAl = _w(nc.sbuf_tensor("wAl", [128, 8 * C], BF16))
        wBh = _w(nc.sbuf_tensor("wBh", [128, 8 * C], BF16))
        wBl = _w(nc.sbuf_tensor("wBl", [128, 8 * C], BF16))
        wCh = _w(nc.sbuf_tensor("wCh", [128, 8 * C], BF16))
        wCl = _w(nc.sbuf_tensor("wCl", [128, 8 * C], BF16))
        p1a = ExitStack()
        _e = p1a.enter_context
        m1row = stv_buf
        x_b0 = _e(nc.sbuf_tensor("x_b0", [128, CT * NA], F32))
        x_b1 = _e(nc.sbuf_tensor("x_b1", [128, CT * NA], F32))
        h32_b = _e(nc.sbuf_tensor("h32_b", [128, CT * NA], F32))
        hh0 = _e(nc.sbuf_tensor("hh0", [128, CT * NA], BF16))
        hh1 = _e(nc.sbuf_tensor("hh1", [128, CT * NA], BF16))
        hl0 = _e(nc.sbuf_tensor("hl0", [128, CT * NA], BF16))
        hl1 = _e(nc.sbuf_tensor("hl1", [128, CT * NA], BF16))
        Kh_b = _e(nc.sbuf_tensor("Kh_b", [128, CT * NA], BF16))
        Kl_b = _e(nc.sbuf_tensor("Kl_b", [128, CT * NA], BF16))
        y0_b0 = _e(nc.sbuf_tensor("y0_b0", [128, CT * NA], F32))
        K_b = _e(nc.sbuf_tensor("K_b", [128, CT * NA], F32))
        G_b = K_b
        gse_b = _e(nc.sbuf_tensor("gse_b", [128, CT * (NA + 1)], F32))
        gse_c = _e(nc.sbuf_tensor("gse_c", [128, CT * (NA + 1)], F32))
        xsq_b = h32_b
        x_b = [x_b0, x_b1]
        hh_p = [hh0, hh1]
        hl_p = [hl0, hl1]
        y0_p = [y0_b0, y0_b0]
        gse_p = [gse_b, gse_c]
        scB = _e(nc.sbuf_tensor("scB", [128, NA], F32))
        wstr0 = _e(nc.sbuf_tensor("wstr0", [128, I_FFN], F32))
        wfst = wstr0
        wit = _e(nc.sbuf_tensor("wit", [128, I_FFN // 8], I32))
        wqb0 = _e(nc.sbuf_tensor("wqb0", [128, I_FFN // 8], BF16))
        wqb1 = _e(nc.sbuf_tensor("wqb1", [128, I_FFN // 8], BF16))
        winv = _e(nc.sbuf_tensor("winv", [128, 1], F32))

        class _V:
            def __init__(v, ap):
                v._ap = ap
            def ap(v):
                return v._ap
        gath = _V(gse_b.ap()[0:8, 0:1024])
        AT_s = _V(gse_b.ap()[0:8, 0:C])
        Bm_s = _V(h32_b.ap()[0:8, 0:C])
        wstr = [wstr0, wstr0]

        def xs(buf, m, N=NA, dt=4):
            return buf.ap()[:, m * N:(m + 1) * N]

        def wsl(w, kk, m):
            return w.ap()[:, kk * C + m * 128: kk * C + m * 128 + 128]

        g = sch.group("sync", writes=["const", "gath", "h32"], dma_sem="qs")
        for kk in range(CT):
            g.op(lambda e, kk=kk: e.dma_start(out=lr_t.ap()[:, kk:kk + 1], in_=log_lr[128 * kk:128 * kk + 128][:, None]))
            g.op(lambda e, kk=kk: e.dma_start(out=g_t.ap()[:, kk:kk + 1], in_=ln_g[128 * kk:128 * kk + 128][:, None]))
            g.op(lambda e, kk=kk: e.dma_start(out=b_t.ap()[:, kk:kk + 1], in_=ln_b[128 * kk:128 * kk + 128][:, None]))
            g.op(lambda e, kk=kk: e.dma_start(out=ls_t.ap()[:, kk:kk + 1], in_=ls_g[128 * kk:128 * kk + 128][:, None]))
        g.op(lambda e: e.dma_start(out=mask_t.ap(), in_=mask8[:, None]))
        g.op(lambda e: e.dma_start(out=clipt.ap()[0:1, 0:1], in_=clip_x[:, None]))
        g.op(lambda e: e.dma_start(out=clipt.ap()[0:1, 1:2], in_=clip_down[:, None]))
        g.op(lambda e: e.dma_start(out=AT_s.ap(), in_=ATd))
        g.op(lambda e: e.dma_start(out=Bm_s.ap(), in_=Bmd))
        g.done()
        g = sch.group("gpsimd", writes=["const2", "off", "rm", "rm2"])
        g.op(lambda e: e.memset(ones1.ap(), 1.0))
        g.op(lambda e: e.memset(onesk1.ap(), 1.0))
        g.op(lambda e: e.memset(off_t.ap(), 0.0))
        g.done()
        g = sch.group("scalar", reads=["const"], writes=["lr1"])
        g.op(lambda e: e.activation(lr_t.ap(), lr_t.ap(), AF.Exp))
        g.done()
        g = sch.group("vector", reads=["lr1", "const"], writes=["lr", "gder"])
        g.op(lambda e: e.tensor_scalar_min(lr_t.ap(), lr_t.ap(), 1.0))
        g.op(lambda e: e.reciprocal(ginv_t.ap(), g_t.ap()))
        g.op(lambda e: e.tensor_scalar_mul(nb_t.ap(), b_t.ap(), -1.0))
        g.op(lambda e: e.tensor_mul(bg_t.ap(), b_t.ap(), ginv_t.ap()))
        g.done()

        def load_split(src_hbm, dsth, dstl, hres, lres, negate=False, weff=False,
                       stage=None, stage_res=None, colscale=None, gacc=False):
            stg = stage if stage is not None else wfst
            sres = stage_res if stage_res is not None else ("wf0", "wf1")
            for kk in range(CT):
                rg = kk % 2
                r0, r1 = rg * C, rg * C + C
                g = sch.group("sync", writes=[sres[rg]], dma_sem="qs")
                g.op(lambda e, kk=kk, r0=r0, r1=r1, src_hbm=src_hbm: e.dma_start(
                    out=stg.ap()[:, r0:r1], in_=src_hbm[128 * kk:128 * kk + 128, :]))
                g.done()
                if colscale is not None:
                    g = sch.group("vector", reads=[sres[rg], "gder"], writes=[sres[rg]])
                    g.op(lambda e, kk=kk, r0=r0, r1=r1: e.tensor_scalar(
                        stg.ap()[:, r0:r1], stg.ap()[:, r0:r1],
                        colscale.ap()[:, kk:kk + 1], None, ALU.mult))
                    g.done()
                if gacc:
                    g = sch.group("tensor", reads=[sres[rg], "gder"], writes=["ps_cg"])
                    for half in range(2):
                        g.op(lambda e, kk=kk, r0=r0, half=half: e.matmul(
                            psb[6].ap()[32 * half:32 * half + 1, 0:512],
                            bg_t.ap()[:, kk:kk + 1],
                            stg.ap()[:, r0 + half * 512:r0 + half * 512 + 512],
                            start=(kk == 0), stop=(kk == CT - 1)))
                    g.done()
                if weff:
                    g = sch.group("tensor", reads=["gath", "h32"], writes=["psm0", "psm1"])
                    for half in range(2):
                        g.op(lambda e, kk=kk, half=half: e.matmul(
                            psb[half].ap()[:, 0:512], AT_s.ap()[:, kk * 128:kk * 128 + 128],
                            Bm_s.ap()[:, half * 512:half * 512 + 512], start=True, stop=True))
                    g.done()
                    g = sch.group("vector", reads=["psm0", "psm1", sres[rg]], writes=[sres[rg]])
                    for half in range(2):
                        g.op(lambda e, r0=r0, half=half: e.tensor_add(
                            stg.ap()[:, r0 + half * 512:r0 + half * 512 + 512],
                            stg.ap()[:, r0 + half * 512:r0 + half * 512 + 512],
                            psb[half].ap()[:, 0:512]))
                    g.done()
                if negate:
                    g = sch.group("scalar", reads=[sres[rg]], writes=[sres[rg]])
                    g.op(lambda e, r0=r0, r1=r1: e.mul(stg.ap()[:, r0:r1], stg.ap()[:, r0:r1], -1.0))
                    g.done()
                g = sch.group("scalar", reads=[sres[rg]], writes=[hres])
                g.op(lambda e, kk=kk, r0=r0, r1=r1, dsth=dsth: e.copy(
                    xs(dsth, kk, C), stg.ap()[:, r0:r1]))
                g.done()
                g = sch.group("gpsimd", reads=[sres[rg], hres], writes=[lres])
                g.op(lambda e, kk=kk, r0=r0, r1=r1, dsth=dsth, dstl=dstl: e.tensor_sub(
                    xs(dstl, kk, C), stg.ap()[:, r0:r1], xs(dsth, kk, C)))
                g.done()

        load_split(WkT, wAh, wAl, "wAh", "wAl")
        load_split(W0, wBh, wBl, "wBh", "wBl", weff=True)
        load_split(WvT, wCh, wCl, "wCh", "wCl", negate=True)
        g = sch.group("gpsimd", reads=["gath", "h32"], writes=["gse0", "gse1"])
        g.op(lambda e: e.memset(gse_b.ap(), 0.0))
        g.op(lambda e: e.memset(gse_c.ap(), 0.0))
        g.done()


        def quant_strip(w_in, wq_out, sc_tile, sc_res, s_i, cols):
            p = 0
            SEG = I_FFN // 8
            wqbs = [wqb0, wqb1]
            g = sch.group("scalar", writes=[f"wstr{p}", "wf0", "wf1"], dma_sem="qs2")
            g.op(lambda e, s_i=s_i, p=p, cols=cols, w_in=w_in: e.dma_start(
                out=wstr[p].ap()[:, 0:cols], in_=w_in[128 * s_i:128 * s_i + 128, :]))
            g.done()
            segs = [(c0, min(SEG, cols - c0)) for c0 in range(0, cols, SEG)]
            for pi in range(0, len(segs), 2):
                pair = segs[pi:pi + 2]
                if pi == 0:
                    g = sch.group("vector", reads=[f"wstr{p}"],
                                  writes=[sc_res, "winv", "wmax", "wit"], serialize=True)
                    g.op(lambda e, p=p, cols=cols: e.tensor_reduce(
                        wmax.ap(), wstr[p].ap()[:, 0:cols], AX.X, ALU.max,
                        apply_absolute_value=True))
                    g.op(lambda e, s_i=s_i, sc_tile=sc_tile: e.tensor_scalar_mul(
                        sc_tile.ap()[:, s_i:s_i + 1], wmax.ap(), 1.0 / QMAX))
                    g.op(lambda e, s_i=s_i, sc_tile=sc_tile: e.reciprocal(
                        winv.ap(), sc_tile.ap()[:, s_i:s_i + 1]))
                    g.op(lambda e, p=p, cw=pair[0][1]: e.tensor_scalar(
                        wit.ap()[:, 0:cw], wstr[p].ap()[:, 0:cw], winv.ap(), None, ALU.mult))
                    g.done()
                g = sch.group("vector", reads=[f"wstr{p}", "winv"],
                              writes=["wit", "wqb0", "wqb1"])
                for bi, (c0, cw) in enumerate(pair):
                    if not (pi == 0 and bi == 0):
                        g.op(lambda e, p=p, c0=c0, cw=cw: e.tensor_scalar(
                            wit.ap()[:, 0:cw], wstr[p].ap()[:, c0:c0 + cw], winv.ap(),
                            None, ALU.mult))
                    g.op(lambda e, bi=bi, cw=cw, wqbs=wqbs: e.tensor_scalar(
                        wqbs[bi].ap()[:, 0:cw], wit.ap()[:, 0:cw], -128.0, 127.0,
                        ALU.max, ALU.min))
                g.done()
                for bi, (c0, cw) in enumerate(pair):
                    g = sch.group("gpsimd", reads=[f"wqb{bi}"], dma_sem="qg")
                    g.op(lambda e, s_i=s_i, c0=c0, cw=cw, bi=bi, wq_out=wq_out,
                         wqbs=wqbs: e.dma_start(
                        out=wq_out[128 * s_i:128 * s_i + 128, c0:c0 + cw],
                        in_=wqbs[bi].ap()[:, 0:cw]))
                    g.done()

        def strip_gen():
            for s_i in range(IT):
                yield (wg_r, wqg_sp, scg, "wscg", s_i, C)
            for s_i in range(IT):
                yield (wu_r, wqu_sp, scu, "wscu", s_i, C)
            for s_i in range(CT):
                yield (wd_r, wqd_sp, scd, "wscd", s_i, I_FFN)

        strips = strip_gen()

        def pump_strips(n):
            for _ in range(n):
                try:
                    quant_strip(*next(strips))
                except StopIteration:
                    return

        def mm3(g, bank, wh, wl, rhit, rlot, m, N):
            passes = [(wh, rhit), (wh, rlot), (wl, rhit)]
            for pi, (ww, rr) in enumerate(passes):
                for kk in range(CT):
                    g.op(lambda e, ww=ww, rr=rr, m=m, kk=kk, bank=bank, N=N,
                         st=(pi == 0 and kk == 0), sp=(pi == 2 and kk == CT - 1): e.matmul(
                        psb[bank].ap()[:, 0:N], wsl(ww, kk, m), xs(rr, kk, N, dt=2),
                        start=st, stop=sp))

        def stats_mm(buf, res, N, xq_buf=None, xq_res="h32", xq_w=None):
            xqb = xq_buf if xq_buf is not None else xsq_b
            xqw = xq_w if xq_w is not None else NA
            g = sch.group("scalar", reads=[res], writes=[xq_res])
            for kk in range(CT):
                g.op(lambda e, kk=kk, buf=buf, N=N: e.activation(
                    xs(xqb, kk, xqw)[:, 0:N], xs(buf, kk, N), AF.Square))
            g.done()
            g = sch.group("tensor", reads=[res, xq_res, "const2"], writes=["ps_st"])
            for kk in range(CT):
                g.op(lambda e, kk=kk, buf=buf, N=N: e.matmul(
                    psb[2].ap()[0:1, 0:N], ones1.ap(), xs(buf, kk, N),
                    start=(kk == 0), stop=(kk == CT - 1)))
            for kk in range(CT):
                g.op(lambda e, kk=kk, N=N: e.matmul(
                    psb[2].ap()[32:33, 0:N], ones1.ap(), xs(xqb, kk, xqw)[:, 0:N],
                    start=(kk == 0), stop=(kk == CT - 1)))
            g.done()

        def stats_math(N, r1_slot, mrn_slot, second=True):
            V = lambda n: SV(n)[:, 0:N]
            g = sch.group("vector", reads=["ps_st"], writes=["stv"])
            g.op(lambda e: e.tensor_scalar_mul(V("mu"), psb[2].ap()[0:1, 0:N], 1.0 / C))
            g.op(lambda e: e.tensor_scalar_mul(V("e2"), psb[2].ap()[32:33, 0:N], 1.0 / C))
            g.op(lambda e: e.tensor_mul(V("a"), V("mu"), V("mu")))
            g.op(lambda e: e.tensor_sub(V("var"), V("e2"), V("a")))
            g.op(lambda e: e.tensor_scalar_add(V("te"), V("var"), EPS))
            if second:
                g.op(lambda e: e.reciprocal(V("rte"), V("te")))
                g.op(lambda e: e.tensor_mul(V("v1r"), V("var"), V("rte")))
                g.op(lambda e: e.tensor_scalar_add(V("te2"), V("v1r"), EPS))
            g.done()
            g = sch.group("scalar", reads=["stv"], writes=["stvs"])
            g.op(lambda e: e.activation(V("s0"), V("te"), AF.Sqrt))
            if second:
                g.op(lambda e: e.activation(V("s2"), V("te2"), AF.Sqrt))
            g.done()
            g = sch.group("vector", reads=["stvs"], writes=["stv"])

            def newton(s_in, t_in, r_out):
                g.op(lambda e, s_in=s_in: e.reciprocal(V("r0"), V(s_in)))
                for srcn, dstn in (("r0", r_out), (r_out, r_out)):
                    g.op(lambda e, srcn=srcn: e.tensor_mul(V("a"), V(srcn), V(srcn)))
                    g.op(lambda e, t_in=t_in: e.tensor_mul(V("b2"), V("a"), V(t_in)))
                    g.op(lambda e: e.tensor_scalar(V("cc2"), V("b2"), -0.5, 1.5, ALU.mult, ALU.add))
                    g.op(lambda e, srcn=srcn, dstn=dstn: e.tensor_mul(V(dstn), V(srcn), V("cc2")))

            newton("s0", "te", r1_slot)
            if second:
                newton("s2", "te2", "r2")
                g.op(lambda e, r1_slot=r1_slot: e.tensor_mul(V("r12"), V(r1_slot), V("r2")))
            g.op(lambda e, r1_slot=r1_slot, mrn_slot=mrn_slot: e.scalar_tensor_tensor(
                V(mrn_slot), V("mu"), -1.0, V(r1_slot), ALU.mult, ALU.mult))
            if second:
                g.op(lambda e, mrn_slot=mrn_slot: e.tensor_mul(V("hc2"), V(mrn_slot), V("r2")))
            g.done()

        def a_front(j):
            px = j % 2
            g = sch.group("sync", writes=[f"x{px}"], dma_sem="qs")
            g.op(lambda e, j=j, px=px: e.dma_start(out=sbuf3(x_b[px], NA),
                                                   in_=hbm3(xT, j * NA, NA)))
            g.done()
            stats_mm(x_b[px], f"x{px}", NA)

        def a_mid(j):
            stats_math(NA, "r1", "mrn", second=True)

        def a_bcast(j):
            g = sch.group("tensor", reads=["stv", "const2"], writes=["ps_bc"])
            for qq, sl in enumerate(("r12", "hc2")):
                g.op(lambda e, qq=qq, sl=sl: e.matmul(
                    psb[3].ap()[:, qq * NA:(qq + 1) * NA], onesk1.ap(),
                    SV(sl)[:, 0:NA], start=True, stop=True))
            g.done()

        def a_xah(j):
            px = j % 2
            g = sch.group("vector", reads=["ps_bc", f"x{px}", "const"],
                          writes=["h32"])
            for kk in range(CT):
                g.op(lambda e, kk=kk, px=px: e.tensor_mul(
                    xs(h32_b, kk), xs(x_b[px], kk), psb[3].ap()[:, 0:NA]))
            for kk in range(CT):
                g.op(lambda e, kk=kk: e.tensor_add(
                    xs(h32_b, kk), xs(h32_b, kk), psb[3].ap()[:, NA:2 * NA]))
            for kk in range(CT):
                g.op(lambda e, kk=kk: e.tensor_scalar(
                    xs(h32_b, kk), xs(h32_b, kk), g_t.ap()[:, kk:kk + 1],
                    b_t.ap()[:, kk:kk + 1], ALU.mult, ALU.add))
            g.done()
            g = sch.group("scalar", reads=["h32"], writes=[f"hh{px}"])
            g.op(lambda e, px=px: e.copy(hh_p[px].ap(), h32_b.ap()))
            g.done()
            g = sch.group("gpsimd", reads=["h32", f"hh{px}"], writes=[f"hl{px}"])
            g.op(lambda e, px=px: e.tensor_sub(hl_p[px].ap(), h32_b.ap(), hh_p[px].ap()))
            g.done()
            g = sch.group("gpsimd", reads=[f"hh{px}", f"hl{px}"], dma_sem="qg")
            g.op(lambda e, j=j, px=px: e.dma_start(out=hbm3(hh_sp, j * NA, NA),
                                                   in_=sbuf3(hh_p[px], NA)))
            g.op(lambda e, j=j, px=px: e.dma_start(out=hbm3(hl_sp, j * NA, NA),
                                                   in_=sbuf3(hl_p[px], NA)))
            g.done()

        def a_K(i):
            px = i % 2
            for m in range(CT):
                pm = f"psm{m % 2}"
                g = sch.group("tensor", reads=[f"hh{px}", f"hl{px}", "wAh", "wAl"], writes=[pm])
                mm3(g, m % 2, wAh, wAl, hh_p[px], hl_p[px], m, NA)
                g.done()
                g = sch.group("scalar", reads=[pm], writes=["K"])
                g.op(lambda e, m=m: e.copy(xs(K_b, m), psb[m % 2].ap()[:, 0:NA]))
                g.done()
            g = sch.group("scalar", reads=["K"], writes=["Kh"])
            g.op(lambda e: e.copy(Kh_b.ap(), K_b.ap()))
            g.done()
            g = sch.group("gpsimd", reads=["K", "Kh"], writes=["Kl"])
            g.op(lambda e: e.tensor_sub(Kl_b.ap(), K_b.ap(), Kh_b.ap()))
            g.done()

        def a_tail(i):
            px = i % 2
            for m in range(CT):
                pm = f"psm{m % 2}"
                g = sch.group("tensor", reads=["Kh", "Kl", "wBh", "wBl"], writes=[pm])
                mm3(g, m % 2, wBh, wBl, Kh_b, Kl_b, m, NA)
                g.done()
                g = sch.group("scalar", reads=[pm], writes=["y0A"])
                g.op(lambda e, m=m, px=px: e.copy(xs(y0_p[px], m), psb[m % 2].ap()[:, 0:NA]))
                g.done()
                g = sch.group("tensor", reads=[f"hh{px}", f"hl{px}", "wCh", "wCl", "y0A"],
                              writes=[pm])
                mm3(g, m % 2, wCh, wCl, hh_p[px], hl_p[px], m, NA)
                g.done()
                g = sch.group("vector", reads=[pm, "y0A"], writes=["K"])
                g.op(lambda e, m=m, px=px: e.tensor_add(scB.ap()[:, 0:NA], xs(y0_p[px], m),
                                                        psb[m % 2].ap()[:, 0:NA]))
                g.op(lambda e, m=m: e.tensor_mul(xs(G_b, m), xs(K_b, m), scB.ap()[:, 0:NA]))
                g.done()
            g = sch.group("gpsimd", reads=["y0A"], dma_sem="qg")
            g.op(lambda e, i=i, px=px: e.dma_start(out=hbm3(y0_sp, i * NA, NA),
                                                   in_=sbuf3(y0_p[px], NA)))
            g.done()
            pg = i % 2
            gcur, gprv = gse_p[pg], gse_p[1 - pg]
            g = sch.group("vector", reads=["K", f"gse{1 - pg}"], writes=[f"gse{pg}"])
            for m in range(CT):
                base = m * (NA + 1)
                g.op(lambda e, base=base, gcur=gcur, gprv=gprv: e.tensor_copy(
                    gcur.ap()[:, base:base + 1], gprv.ap()[:, base + NA:base + NA + 1]))
                g.op(lambda e, base=base, m=m, gcur=gcur, gprv=gprv: e.tensor_tensor_scan(
                    gcur.ap()[:, base + 1:base + 1 + NA], xs(G_b, m), xs(G_b, m),
                    gprv.ap()[:, base + NA:base + NA + 1], ALU.add, ALU.bypass))
            g.done()
            g = sch.group("gpsimd", reads=[f"gse{pg}"], dma_sem="qg")
            g.op(lambda e, i=i, gcur=gcur: e.dma_start(
                out=hbm3(gs_sp, i * NA, NA),
                in_=gcur.ap().rearrange("p (k n) -> p k n", k=CT)[:, :, 0:NA]))
            g.done()

        for s in range(nCA + 1):
            if s >= 1:
                a_K(s - 1)
            if s < nCA:
                a_front(s)
                a_mid(s)
            if s >= 1:
                a_tail(s - 1)
            if s < nCA:
                a_bcast(s)
                a_xah(s)
            pump_strips(4 if s > 0 else 0)
        pump_strips(64)

        pg_last = (nCA - 1) % 2
        glast = gse_b if pg_last == 0 else gse_c
        g = sch.group("gpsimd", reads=[f"gse{pg_last}"], dma_sem="qg")
        for m in range(CT):
            g.op(lambda e, m=m, glast=glast: e.dma_start(
                out=cc_gin[m * 128:m * 128 + 128][:, None],
                in_=glast.ap()[:, m * (NA + 1) + NA:m * (NA + 1) + NA + 1]))
        g.done()
        g = sch.group("gpsimd", writes=["ccg"], sem="cc", waits=sch.queue_totals("qg"))
        g.op(lambda e: e.collective_compute("AllGather", ALU.bypass, replica_groups=RG,
                                            ins=[cc_gin.opt()], outs=[cc_gout.opt()]))
        g.done()
        g = sch.group("sync", reads=["ccg"], writes=["gath", "gse0"], dma_sem="qs")
        g.op(lambda e: e.dma_start(out=gath.ap(), in_=cc_gout))
        g.done()
        for half in range(2):
            g = sch.group("tensor", reads=["gath", "const"], writes=["ps_bc"])
            g.op(lambda e, half=half: e.matmul(
                psb[3].ap()[0:1, 0:512], mask_t.ap(),
                gath.ap()[:, half * 512:half * 512 + 512], start=True, stop=True))
            g.done()
            g = sch.group("scalar", reads=["ps_bc"], writes=["offrow", "stv"])
            g.op(lambda e, half=half: e.copy(m1row.ap()[0:1, half * 512:half * 512 + 512],
                                             psb[3].ap()[0:1, 0:512]))
            g.done()
        g = sch.group("gpsimd", reads=["offrow"], dma_sem="qg")
        g.op(lambda e: e.dma_start(out=off_hbm[None, :], in_=m1row.ap()[0:1, 0:1024]))
        g.done()
        g = sch.group("sync", writes=["off"], dma_sem="qs", waits=sch.queue_totals("qg"))
        for m in range(CT):
            g.op(lambda e, m=m: e.dma_start(out=off_t.ap()[:, m:m + 1],
                                            in_=off_hbm[m * 128:m * 128 + 128][:, None]))
        g.done()

        snapAB = dict(sch.sem_count)
        for e in ENGINES:
            sch.wait_only(e, [(s, v) for s, v in snapAB.items()])
        p1a.close()

        p1b = ExitStack()
        _b = p1b.enter_context
        hhB = [_b(nc.sbuf_tensor(f"hhB{p}", [128, CT * NB], BF16)) for p in range(2)]
        hlB = [_b(nc.sbuf_tensor(f"hlB{p}", [128, CT * NB], BF16)) for p in range(2)]
        x1B = [_b(nc.sbuf_tensor(f"x1B{p}", [128, CT * NB], F32)) for p in range(2)]
        y0B = [_b(nc.sbuf_tensor(f"y0B{p}", [128, CT * NB], F32)) for p in range(2)]
        gseB = [_b(nc.sbuf_tensor(f"gseB{p}", [128, CT * (NB + 1)], F32)) for p in range(2)]
        xahB = _b(nc.sbuf_tensor("xahB", [128, CT * NB], BF16))
        xalB = _b(nc.sbuf_tensor("xalB", [128, CT * NB], BF16))
        h32B = _b(nc.sbuf_tensor("h32B", [128, CT * NB], F32))
        scAB = _b(nc.sbuf_tensor("scAB", [128, NB], F32))
        scBB = _b(nc.sbuf_tensor("scBB", [128, NB], F32))
        sresB = ("h32B0", "h32B1")

        load_split(WqT, wAh, wAl, "wAh", "wAl", stage=h32B, stage_res=sresB)
        load_split(WgT, wBh, wBl, "wBh", "wBl", stage=h32B, stage_res=sresB,
                   colscale=ginv_t, gacc=True)
        load_split(WoT, wCh, wCl, "wCh", "wCl", stage=h32B, stage_res=sresB)
        g = sch.group("scalar", reads=["ps_cg"], writes=["offrow", "stv"])
        for half in range(2):
            g.op(lambda e, half=half: e.mul(
                m1row.ap()[0:1, half * 512:half * 512 + 512],
                psb[6].ap()[32 * half:32 * half + 1, 0:512], -1.0))
        g.done()
        g = sch.group("gpsimd", reads=["offrow"], dma_sem="qg")
        g.op(lambda e: e.dma_start(out=cg_hbm[None, :], in_=m1row.ap()[0:1, 0:1024]))
        g.done()
        g = sch.group("sync", writes=["cgt"], dma_sem="qs", waits=sch.queue_totals("qg"))
        for m in range(CT):
            g.op(lambda e, m=m: e.dma_start(out=ngate_t.ap()[:, m:m + 1],
                                            in_=cg_hbm[m * 128:m * 128 + 128][:, None]))
        g.done()

        def b_loads(j):
            px = j % 2
            for dst, res_, srcp in [(hhB[px], f"hh{px}", hh_sp), (hlB[px], f"hl{px}", hl_sp),
                                    (y0B[px], f"y0{px}", y0_sp)]:
                g = sch.group("sync", writes=[res_], dma_sem="qs")
                g.op(lambda e, j=j, dst=dst, srcp=srcp: e.dma_start(
                    out=sbuf3(dst, NB), in_=hbm3(srcp, j * NB, NB)))
                g.done()
            g = sch.group("sync", writes=[f"gse{px}"], dma_sem="qs")
            g.op(lambda e, j=j, px=px: e.dma_start(
                out=gseB[px].ap().rearrange("p (k n) -> p k n", k=CT)[:, :, 0:NB],
                in_=hbm3(gs_sp, j * NB, NB)))
            g.done()
            psl = SV("psc") if px == 0 else SV("hc2")
            g = sch.group("sync", writes=[f"psc{px}"], dma_sem="qs")
            g.op(lambda e, j=j, psl=psl: e.dma_start(out=psl[:, 0:NB],
                                                     in_=ps_row_in[0:1, j * NB:j * NB + NB]))
            g.done()

        def b_main(i):
            px = i % 2
            psl = SV("psc") if px == 0 else SV("hc2")
            g = sch.group("tensor", reads=[f"psc{px}", "const2"], writes=["ps_ps"])
            g.op(lambda e, psl=psl: e.matmul(psb[7].ap()[:, 0:NB], onesk1.ap(), psl[:, 0:NB],
                                             start=True, stop=True))
            g.done()
            for m in range(CT):
                pm = f"psm{m % 2}"
                pgm = f"psg{m % 2}"
                g = sch.group("tensor", reads=[f"hh{px}", f"hl{px}", "wAh", "wAl"], writes=[pm])
                mm3(g, m % 2, wAh, wAl, hhB[px], hlB[px], m, NB)
                g.done()
                g = sch.group("tensor", reads=[f"hh{px}", f"hl{px}", "wBh", "wBl"], writes=[pgm])
                mm3(g, 4 + m % 2, wBh, wBl, hhB[px], hlB[px], m, NB)
                g.done()
                g = sch.group("scalar", reads=[pgm, "cgt"], writes=["h32B"])
                g.op(lambda e, m=m: e.activation(xs(h32B, m, NB), psb[4 + m % 2].ap()[:, 0:NB],
                                                 AF.Sigmoid, bias=ngate_t.ap()[:, m:m + 1]))
                g.done()
                g = sch.group("vector",
                              reads=[f"gse{px}", "off", "ps_ps", pm, f"y0{px}", "h32B", "lr"],
                              writes=["h32B"])
                base = m * (NB + 1)
                g.op(lambda e, base=base, m=m, px=px: e.scalar_tensor_tensor(
                    scAB.ap(), gseB[px].ap()[:, base:base + NB], off_t.ap()[:, m:m + 1],
                    psb[7].ap()[:, 0:NB], ALU.add, ALU.mult))
                g.op(lambda e, m=m: e.scalar_tensor_tensor(
                    scBB.ap(), psb[m % 2].ap()[:, 0:NB], lr_t.ap()[:, m:m + 1], scAB.ap(),
                    ALU.mult, ALU.mult))
                g.op(lambda e, m=m, px=px: e.tensor_sub(scAB.ap(), xs(y0B[px], m, NB),
                                                        scBB.ap()))
                g.op(lambda e, m=m: e.tensor_mul(xs(h32B, m, NB), scAB.ap(), xs(h32B, m, NB)))
                g.done()
            g = sch.group("scalar", reads=["h32B"], writes=["xah"])
            g.op(lambda e: e.copy(xahB.ap(), h32B.ap()))
            g.done()
            g = sch.group("gpsimd", reads=["h32B", "xah"], writes=["xal"])
            g.op(lambda e: e.tensor_sub(xalB.ap(), h32B.ap(), xahB.ap()))
            g.done()
            g = sch.group("vector", reads=[f"hh{px}", f"hl{px}", "gder", "const"],
                          writes=[f"x1{px}"])
            for kk in range(CT):
                g.op(lambda e, kk=kk, px=px: e.tensor_add(
                    scAB.ap(), xs(hhB[px], kk, NB), xs(hlB[px], kk, NB)))
                g.op(lambda e, kk=kk: e.tensor_scalar(
                    scAB.ap(), scAB.ap(), nb_t.ap()[:, kk:kk + 1],
                    ginv_t.ap()[:, kk:kk + 1], ALU.add, ALU.mult))
                g.op(lambda e, kk=kk, px=px: e.tensor_add(
                    xs(x1B[px], kk, NB), xs(x1B[px], kk, NB), scAB.ap()))
            g.done()
            for m in range(CT):
                pm = f"psm{m % 2}"
                g = sch.group("tensor", reads=["xah", "xal", "wCh", "wCl"], writes=[pm])
                mm3(g, m % 2, wCh, wCl, xahB, xalB, m, NB)
                g.done()
                g = sch.group("vector", reads=[pm, "const"], writes=[f"x1{px}"])
                g.op(lambda e, m=m, px=px: e.scalar_tensor_tensor(
                    xs(x1B[px], m, NB), psb[m % 2].ap()[:, 0:NB], ls_t.ap()[:, m:m + 1],
                    xs(x1B[px], m, NB), ALU.mult, ALU.add))
                g.done()
            g = sch.group("gpsimd", reads=[f"x1{px}"], dma_sem="qg")
            g.op(lambda e, i=i, px=px: e.dma_start(out=hbm3(x1_sp, i * NB, NB),
                                                   in_=sbuf3(x1B[px], NB)))
            g.done()
            stats_mm(x1B[px], f"x1{px}", NB, xq_buf=h32B, xq_res="h32B", xq_w=NB)
            stats_math(NB, "r1b", "mrnb", second=False)

        def b_y(i):
            px = i % 2
            g = sch.group("tensor", reads=["stv", "const2"], writes=["ps_bc"])
            g.op(lambda e: e.matmul(psb[3].ap()[:, 0:2 * NB], onesk1.ap(),
                                    stv_buf.ap()[:, 0:2 * NB], start=True, stop=True))
            g.done()
            g = sch.group("vector", reads=["ps_bc", f"x1{px}"], writes=["h32B", "rm"])
            for kk in range(CT):
                g.op(lambda e, kk=kk, px=px: e.tensor_mul(
                    xs(h32B, kk, NB), xs(x1B[px], kk, NB), psb[3].ap()[:, 0:NB]))
            for kk in range(CT):
                g.op(lambda e, kk=kk: e.tensor_add(xs(h32B, kk, NB), xs(h32B, kk, NB),
                                                   psb[3].ap()[:, NB:2 * NB]))
            for kk in range(CT):
                g.op(lambda e, kk=kk, i=i: e.tensor_reduce(
                    rmax.ap()[:, i * CT + kk:i * CT + kk + 1], xs(h32B, kk, NB), AX.X,
                    ALU.max, apply_absolute_value=True))
            g.done()
            g = sch.group("gpsimd", reads=["h32B"], dma_sem="qg")
            g.op(lambda e, i=i: e.dma_start(out=hbm3(y_sp, i * NB, NB), in_=sbuf3(h32B, NB)))
            g.done()

        def b_loadx(j):
            px = j % 2
            g = sch.group("sync", writes=[f"x1{px}"], dma_sem="qs")
            g.op(lambda e, j=j, px=px: e.dma_start(out=sbuf3(x1B[px], NB),
                                                   in_=hbm3(xT, j * NB, NB)))
            g.done()

        nCBB = T // NB
        b_loadx(0)
        for s in range(nCBB + 2):
            if s >= 2:
                b_y(s - 2)
            if s < nCBB:
                b_loads(s)
                if s >= 1:
                    b_loadx(s)
            if s >= 1 and s - 1 < nCBB:
                b_main(s - 1)

        part1 = sch.mark()
        with nc.Block() as blk:
            for e in ENGINES:
                def closure(eng, e=e):
                    sch.replay(e, eng, sems, 0, part1)
                getattr(blk, e)(closure)
        p1b.close()

    with ExitStack() as p2stack:
        _e2 = p2stack.enter_context
        wqgT = _e2(nc.sbuf_tensor("wqgT", [128, CT * I_FFN], BF16))
        wquT = _e2(nc.sbuf_tensor("wquT", [128, CT * I_FFN], BF16))
        wqdT = _e2(nc.sbuf_tensor("wqdT", [128, IT * C], BF16))
        y_bf0 = _e2(nc.sbuf_tensor("y_bf0", [128, CT * NF], F32))
        y_bf1 = y_bf0
        xq_b = _e2(nc.sbuf_tensor("xq_b", [128, CT * NF], BF16))
        itq = _e2(nc.sbuf_tensor("itq", [128, NF], I32))
        silu0 = _e2(nc.sbuf_tensor("silu0", [128, NF], F32))
        silu1 = _e2(nc.sbuf_tensor("silu1", [128, NF], F32))
        z_t0 = _e2(nc.sbuf_tensor("z_t0", [128, NF], F32))
        z_t1 = _e2(nc.sbuf_tensor("z_t1", [128, NF], F32))
        zq_t0 = _e2(nc.sbuf_tensor("zq_t0", [128, NF], BF16))
        zq_t1 = _e2(nc.sbuf_tensor("zq_t1", [128, NF], BF16))
        out_b = _e2(nc.sbuf_tensor("out_b", [128, CT * NF], F32))
        m2row = _e2(nc.sbuf_tensor("m2row", [1, 1024], F32))
        dbgbufS = _e2(nc.sbuf_tensor("dbgbufS", [128, 64], F32))

        y_bufs = [y_bf0, y_bf1]
        silu_t = [silu0, silu1]
        z_t = [z_t0, z_t1]
        zq_t = [zq_t0, zq_t1]

        snap = dict(sch.sem_count)
        for e in ENGINES:
            sch.wait_only(e, [(s, v) for s, v in snap.items()])

        g = sch.group("sync", writes=["wqg", "wqu", "wqd"], dma_sem="qs")
        for kk in range(CT):
            g.op(lambda e, kk=kk: e.dma_start_transpose(
                wqgT.ap()[:, kk * I_FFN:(kk + 1) * I_FFN], wqg_sp[:, 128 * kk:128 * kk + 128]))
            g.op(lambda e, kk=kk: e.dma_start_transpose(
                wquT.ap()[:, kk * I_FFN:(kk + 1) * I_FFN], wqu_sp[:, 128 * kk:128 * kk + 128]))
        for kk in range(IT):
            g.op(lambda e, kk=kk: e.dma_start_transpose(
                wqdT.ap()[:, kk * C:(kk + 1) * C], wqd_sp[:, 128 * kk:128 * kk + 128]))
        g.done()

        g = sch.group("vector", reads=["rm"], writes=["redm"])
        g.op(lambda e: e.tensor_reduce(redt.ap(), rmax.ap()[:, 0:(T // NB) * CT], AX.X, ALU.max))
        g.done()
        g = sch.group("gpsimd", reads=["redm"], dma_sem="qg")
        g.op(lambda e: e.dma_start(out=cc_m1in[:, None], in_=redt.ap()))
        g.done()
        g = sch.group("gpsimd", writes=["ccm1"], sem="cc", waits=sch.queue_totals("qg"))
        g.op(lambda e: e.collective_compute("AllGather", ALU.bypass, replica_groups=RG,
                                            ins=[cc_m1in.opt()], outs=[cc_m1out.opt()]))
        g.done()
        g = sch.group("sync", reads=["ccm1"], writes=["m1row"], dma_sem="qs")
        g.op(lambda e: e.dma_start(out=m2row.ap(), in_=cc_m1out[None, :]))
        g.done()
        g = sch.group("scalar", reads=["const"], writes=["sig1"])
        g.op(lambda e: e.activation(sc1x1.ap()[0:1, 0:1], clipt.ap()[0:1, 0:1], AF.Sigmoid))
        g.done()
        g = sch.group("vector", reads=["m1row", "sig1"], writes=["s1s"], serialize=True)
        g.op(lambda e: e.tensor_reduce(sc1x1.ap()[0:1, 1:2], m2row.ap(), AX.X, ALU.max))
        g.op(lambda e: e.tensor_mul(sc1x1.ap()[0:1, 2:3], sc1x1.ap()[0:1, 1:2], sc1x1.ap()[0:1, 0:1]))
        g.op(lambda e: e.tensor_scalar_mul(sc1x1.ap()[0:1, 2:3], sc1x1.ap()[0:1, 2:3], 1.0 / QMAX))
        g.op(lambda e: e.reciprocal(sc1x1.ap()[0:1, 3:4], sc1x1.ap()[0:1, 2:3]))
        g.done()
        g = sch.group("tensor", reads=["s1s", "const2"], writes=["ps_bc"])
        g.op(lambda e: e.matmul(psb[3].ap()[:, 0:2], onesk1.ap(), sc1x1.ap()[0:1, 2:4],
                                start=True, stop=True))
        g.done()
        g = sch.group("scalar", reads=["ps_bc"], writes=["s1v"])
        g.op(lambda e: e.copy(s1p.ap(), psb[3].ap()[:, 0:1]))
        g.op(lambda e: e.copy(s1v.ap(), psb[3].ap()[:, 1:2]))
        g.done()
        g = sch.group("vector", reads=["s1v", "wscg", "wscu"], writes=["sq1"])
        g.op(lambda e: e.tensor_scalar(sgq.ap(), scg.ap(), s1p.ap(), None, ALU.mult))
        g.op(lambda e: e.tensor_scalar(suq.ap(), scu.ap(), s1p.ap(), None, ALU.mult))
        g.done()

        for i in range(nCF):
            t0 = i * NF
            p = i % 2
            ybr = "yb"
            g = sch.group("sync", writes=[ybr], dma_sem="qs")
            g.op(lambda e, t0=t0, p=p: e.dma_start(out=sbuf3(y_bufs[p], NF), in_=hbm3(y_sp, t0, NF)))
            g.done()
            g = sch.group("vector", reads=[ybr, "s1v"], writes=["xq"])
            for kk in range(CT):
                g.op(lambda e, kk=kk, p=p: e.tensor_scalar(
                    itq.ap(), y_bufs[p].ap()[:, kk * NF:(kk + 1) * NF], s1v.ap(), None, ALU.mult))
                g.op(lambda e, kk=kk: e.tensor_scalar(
                    xq_b.ap()[:, kk * NF:(kk + 1) * NF], itq.ap(), -128.0, 127.0, ALU.max, ALU.min))
            g.done()
            for kb in range(IT):
                pp = kb % 2
                g = sch.group("tensor", reads=["xq", "wqg"], writes=[f"psf{pp}"])
                for kk in range(CT):
                    g.op(lambda e, kb=kb, kk=kk, pp=pp: e.matmul(
                        psb[pp].ap()[:, 0:NF],
                        wqgT.ap()[:, kk * I_FFN + kb * 128: kk * I_FFN + kb * 128 + 128],
                        xq_b.ap()[:, kk * NF:(kk + 1) * NF], start=(kk == 0), stop=(kk == CT - 1)))
                g.done()
                g = sch.group("scalar", reads=[f"psf{pp}", "sq1"], writes=[f"sil{pp}"])
                g.op(lambda e, kb=kb, pp=pp: e.activation(
                    silu_t[pp].ap(), psb[pp].ap()[:, 0:NF], AF.Silu, scale=sgq.ap()[:, kb:kb + 1]))
                g.done()
                g = sch.group("tensor", reads=["xq", "wqu"], writes=[f"psf{2 + pp}"])
                for kk in range(CT):
                    g.op(lambda e, kb=kb, kk=kk, pp=pp: e.matmul(
                        psb[2 + pp].ap()[:, 0:NF],
                        wquT.ap()[:, kk * I_FFN + kb * 128: kk * I_FFN + kb * 128 + 128],
                        xq_b.ap()[:, kk * NF:(kk + 1) * NF], start=(kk == 0), stop=(kk == CT - 1)))
                g.done()
                g = sch.group("vector", reads=[f"psf{2 + pp}", f"sil{pp}", "sq1"],
                              writes=[f"zt{pp}", "rm2"])
                g.op(lambda e, kb=kb, pp=pp: e.scalar_tensor_tensor(
                    z_t[pp].ap(), silu_t[pp].ap(), suq.ap()[:, kb:kb + 1], psb[2 + pp].ap()[:, 0:NF],
                    ALU.mult, ALU.mult))
                g.op(lambda e, kb=kb, pp=pp, i=i: e.tensor_reduce(
                    rmax2.ap()[:, i * IT + kb:i * IT + kb + 1], z_t[pp].ap(), AX.X, ALU.max,
                    apply_absolute_value=True))
                g.done()
                g = sch.group("gpsimd", reads=[f"zt{pp}"], dma_sem="qg")
                g.op(lambda e, t0=t0, kb=kb, pp=pp: e.dma_start(
                    out=z_sp[kb * 128:kb * 128 + 128, t0:t0 + NF], in_=z_t[pp].ap()))
                g.done()

        g = sch.group("vector", reads=["rm2"], writes=["redm2"])
        g.op(lambda e, nCF=nCF: e.tensor_reduce(redt.ap(), rmax2.ap()[:, 0:nCF * IT], AX.X, ALU.max))
        g.done()
        g = sch.group("gpsimd", reads=["redm2"], dma_sem="qg")
        g.op(lambda e: e.dma_start(out=cc_m2in[:, None], in_=redt.ap()))
        g.done()
        g = sch.group("gpsimd", writes=["ccm2"], sem="cc", waits=sch.queue_totals("qg"))
        g.op(lambda e: e.collective_compute("AllGather", ALU.bypass, replica_groups=RG,
                                            ins=[cc_m2in.opt()], outs=[cc_m2out.opt()]))
        g.done()
        g = sch.group("sync", reads=["ccm2"], writes=["m2row"], dma_sem="qs")
        g.op(lambda e: e.dma_start(out=m2row.ap(), in_=cc_m2out[None, :]))
        g.done()
        g = sch.group("scalar", reads=["const"], writes=["sig2"])
        g.op(lambda e: e.activation(sc1x1.ap()[0:1, 4:5], clipt.ap()[0:1, 1:2], AF.Sigmoid))
        g.done()
        g = sch.group("vector", reads=["m2row", "sig2"], writes=["s2s"], serialize=True)
        g.op(lambda e: e.tensor_reduce(sc1x1.ap()[0:1, 5:6], m2row.ap(), AX.X, ALU.max))
        g.op(lambda e: e.tensor_mul(sc1x1.ap()[0:1, 6:7], sc1x1.ap()[0:1, 5:6], sc1x1.ap()[0:1, 4:5]))
        g.op(lambda e: e.tensor_scalar_mul(sc1x1.ap()[0:1, 6:7], sc1x1.ap()[0:1, 6:7], 1.0 / QMAX))
        g.op(lambda e: e.reciprocal(sc1x1.ap()[0:1, 7:8], sc1x1.ap()[0:1, 6:7]))
        g.done()
        g = sch.group("tensor", reads=["s2s", "const2"], writes=["ps_bc2"])
        g.op(lambda e: e.matmul(psb[6].ap()[:, 0:2], onesk1.ap(), sc1x1.ap()[0:1, 6:8],
                                start=True, stop=True))
        g.done()
        g = sch.group("scalar", reads=["ps_bc2"], writes=["s2v"])
        g.op(lambda e: e.copy(s2p.ap(), psb[6].ap()[:, 0:1]))
        g.op(lambda e: e.copy(s2v.ap(), psb[6].ap()[:, 1:2]))
        g.done()
        g = sch.group("vector", reads=["s2v", "wscd"], writes=["sqd"])
        g.op(lambda e: e.tensor_scalar(sdq.ap(), scd.ap(), s2p.ap(), None, ALU.mult))
        g.done()

        for i in range(nCF):
            t0 = i * NF
            p = i % 2
            ybr = "yb"
            g = sch.group("sync", writes=[ybr], dma_sem="qs")
            g.op(lambda e, t0=t0, p=p: e.dma_start(out=sbuf3(y_bufs[p], NF), in_=hbm3(x1_sp, t0, NF)))
            g.done()
            for kb in range(IT):
                pp = kb % 2
                g = sch.group("sync", writes=[f"zt{pp}"], dma_sem="qs")
                g.op(lambda e, t0=t0, kb=kb, pp=pp: e.dma_start(
                    out=z_t[pp].ap(), in_=z_sp[kb * 128:kb * 128 + 128, t0:t0 + NF]))
                g.done()
                g = sch.group("vector", reads=[f"zt{pp}", "s2v"], writes=[f"zq{pp}"])
                g.op(lambda e, pp=pp: e.tensor_scalar(itq.ap(), z_t[pp].ap(), s2v.ap(), None, ALU.mult))
                g.op(lambda e, pp=pp: e.tensor_scalar(zq_t[pp].ap(), itq.ap(), -128.0, 127.0,
                                                      ALU.max, ALU.min))
                g.done()
                psw = ["psd"] if not (i == 0 and kb == 0) else [
                    "psd", "psf0", "psf1", "psf2", "psf3", "ps_bc2"]
                g = sch.group("tensor", reads=[f"zq{pp}", "wqd"], writes=psw)
                for mm in range(CT):
                    g.op(lambda e, kb=kb, mm=mm, pp=pp: e.matmul(
                        psb[mm].ap()[:, 0:NF],
                        wqdT.ap()[:, kb * C + mm * 128: kb * C + mm * 128 + 128],
                        zq_t[pp].ap(), start=(kb == 0), stop=(kb == IT - 1)))
                g.done()
            g = sch.group("scalar", reads=["psd", "sqd"], writes=["outb"])
            for mm in range(CT):
                g.op(lambda e, mm=mm: e.activation(
                    out_b.ap()[:, mm * NF:(mm + 1) * NF], psb[mm].ap()[:, 0:NF],
                    AF.Copy, scale=sdq.ap()[:, mm:mm + 1]))
            g.done()
            g = sch.group("vector", reads=["outb", ybr], writes=["outb"])
            for kk in range(CT):
                g.op(lambda e, kk=kk, p=p: e.tensor_add(
                    out_b.ap()[:, kk * NF:(kk + 1) * NF], out_b.ap()[:, kk * NF:(kk + 1) * NF],
                    y_bufs[p].ap()[:, kk * NF:(kk + 1) * NF]))
            g.done()
            g = sch.group("sync", reads=["outb"], dma_sem="qs")
            g.op(lambda e, t0=t0: e.dma_start(out=hbm3(outT, t0, NF), in_=sbuf3(out_b, NF)))
            g.done()

        if DEBUG_SCALES:
            dbg_sc = dint("dbg_sc", [128, 64])
            g = sch.group("vector", reads=["wscg", "wscu", "wscd", "sq1", "sqd", "s1v", "s2v"],
                          writes=["dbgsc"])
            g.op(lambda e: e.tensor_copy(dbgbufS.ap()[:, 0:IT], scg.ap()))
            g.op(lambda e: e.tensor_copy(dbgbufS.ap()[:, 22:22 + IT], scu.ap()))
            g.op(lambda e: e.tensor_copy(dbgbufS.ap()[:, 44:44 + CT], scd.ap()))
            g.op(lambda e: e.tensor_copy(dbgbufS.ap()[:, 52:53], s1v.ap()))
            g.op(lambda e: e.tensor_copy(dbgbufS.ap()[:, 53:54], s2v.ap()))
            g.op(lambda e: e.tensor_copy(dbgbufS.ap()[:, 54:55], s1p.ap()))
            g.op(lambda e: e.tensor_copy(dbgbufS.ap()[:, 55:56], s2p.ap()))
            g.done()
            g = sch.group("gpsimd", reads=["dbgsc"], dma_sem="qg")
            g.op(lambda e: e.dma_start(out=dbg_sc, in_=dbgbufS.ap()))
            g.done()
        sch.wait_only("sync", sch.queue_totals("qs"))
        sch.wait_only("gpsimd", sch.queue_totals("qg"))

        part2 = sch.mark()
        with nc.Block() as blk2:
            for e in ENGINES:
                def closure2(eng, e=e):
                    sch.replay(e, eng, sems, part1, part2)
                getattr(blk2, e)(closure2)

    estack.close()
    return nc


_T = 4096


def _prep_core_inputs(core, inputs, T):
    b, q = core // 4, core % 4
    t0 = q * T
    d = {}
    d["xT"] = np.ascontiguousarray(inputs["L0_attn_x"][b, t0:t0 + T, :].T)
    d["WqT"] = np.ascontiguousarray(inputs["L0_attn_Wq"].T)
    d["WkT"] = np.ascontiguousarray(inputs["L0_attn_Wk"].T)
    d["WvT"] = np.ascontiguousarray(inputs["L0_attn_Wv"].T)
    d["WgT"] = np.ascontiguousarray(inputs["L0_attn_Wg"].T)
    d["WoT"] = np.ascontiguousarray(inputs["L0_attn_Wo"].T)
    d["W0"] = np.ascontiguousarray(inputs["L0_attn_W0"])
    d["AT"] = np.ascontiguousarray(inputs["L0_attn_A_init"].T)
    d["Bm"] = np.ascontiguousarray(inputs["L0_attn_B_init"])
    d["log_lr"] = inputs["L0_attn_log_lr"]
    d["ln_g"] = inputs["L0_attn_ln_g"]
    d["ln_b"] = inputs["L0_attn_ln_b"]
    d["ls_g"] = inputs["L0_attn_ls_g"]
    d["clip_x"] = inputs["L0_ffn_clip_x"]
    d["clip_down"] = inputs["L0_ffn_clip_down"]
    d["wg_r"] = inputs["L0_ffn_w_gate"]
    d["wu_r"] = inputs["L0_ffn_w_up"]
    d["wd_r"] = inputs["L0_ffn_w_down"]
    pos = np.arange(t0 + 1, t0 + T + 1, dtype=np.float64)
    d["ps_row"] = (1.0 / (1.0 + 0.1 * np.log(pos))).astype(np.float32)[None, :]
    m = np.zeros(8, dtype=np.float32)
    m[b * 4:b * 4 + q] = 1.0
    d["mask8"] = m
    return d


def kernel(**inputs):
    T = _T
    nc = build_program(T=T)
    in_maps = [_prep_core_inputs(c, inputs, T) for c in range(8)]
    res = run_bass_kernel_spmd(nc, in_maps, core_ids=list(range(8)))
    B, Tfull, _ = inputs["L0_attn_x"].shape
    out = np.empty((B, Tfull, C), dtype=np.float32)
    for c in range(8):
        b, q = c // 4, c % 4
        out[b, q * T:(q + 1) * T, :] = np.asarray(res.results[c]["outT"]).T
    return out


# version 14
# speedup vs baseline: 1.0241x; 1.0241x over previous
import numpy as np
import ml_dtypes
from contextlib import ExitStack

import concourse.bass as bass
import concourse.mybir as mybir
from concourse.bass_utils import run_bass_kernel_spmd

F32 = mybir.dt.float32
BF16 = mybir.dt.bfloat16
I32 = mybir.dt.int32
AF = mybir.ActivationFunctionType
ALU = mybir.AluOpType
AX = mybir.AxisListType

C = 1024
CT = 8
I_FFN = 2816
IT = 22
EPS = 1e-5
QMAX = 127.0

ENGINES = ("sync", "tensor", "vector", "scalar", "gpsimd")
DMA_RR = 4
DEBUG_OUTPUTS = ()
DEBUG_SCALES = False


class Sched:

    def __init__(self):
        self.groups = []
        self.sem_count = {}
        self.res = {}
        self.floor = {e: {} for e in ENGINES}
        self.max_waited = {}
        self.dma_rr = {}

    def _rec(self, name):
        return self.res.setdefault(name, {"w": None, "r": []})

    def _bump(self, sem, delta):
        self.sem_count[sem] = self.sem_count.get(sem, 0) + delta
        return self.sem_count[sem]

    def group(self, engine, reads=(), writes=(), sem=None, dma_sem=None, waits=(),
              serialize=False):
        sch = self
        want = {}
        if dma_sem is not None:
            idx = self.dma_rr.get(dma_sem, 0)
            self.dma_rr[dma_sem] = (idx + 1) % DMA_RR
            dma_sem = f"{dma_sem}{idx}"

        def add(sw):
            if sw is None:
                return
            s, v = sw
            if s == engine:
                return
            if want.get(s, 0) < v:
                want[s] = v

        for r in reads:
            add(self._rec(r)["w"])
        for w in writes:
            rec = self._rec(w)
            add(rec["w"])
            for sw in rec["r"]:
                add(sw)
        for sw in waits:
            add(sw)
        if dma_sem is not None:
            add((dma_sem, self.sem_count.get(dma_sem, 0)))
        elif sem and sem != engine:
            add((sem, self.sem_count.get(sem, 0)))
        for s, v in want.items():
            if self.max_waited.get(s, 0) < v:
                self.max_waited[s] = v
        fl = self.floor[engine]
        pruned = {s: v for s, v in want.items() if fl.get(s, 0) < v}
        for s, v in pruned.items():
            fl[s] = v

        class G:
            __slots__ = ("ops",)

            def __init__(g):
                g.ops = []

            def op(g, fn):
                g.ops.append(fn)
                return g

            def done(g):
                if dma_sem is not None:
                    incs = [(dma_sem, 16)] * len(g.ops)
                    val = sch._bump(dma_sem, 16 * len(g.ops))
                    mysem = dma_sem
                    mode = "dma"
                else:
                    s = sem or engine
                    n = len(g.ops) if serialize else 1
                    val = sch._bump(s, n)
                    incs = [(s, 1)] * n
                    mysem = s
                    mode = ("ser", val - n) if serialize else "eng"
                    if s == engine:
                        fl2 = sch.floor[engine]
                        fl2[s] = max(fl2.get(s, 0), val)
                sch.groups.append((engine, pruned, list(g.ops), incs, mode))
                for w in writes:
                    rec = sch._rec(w)
                    rec["w"] = (mysem, val)
                    rec["r"] = []
                for r in reads:
                    sch._rec(r)["r"].append((mysem, val))
                return (mysem, val)

        return G()

    def wait_only(self, engine, waits):
        fl = self.floor[engine]
        pruned = {}
        for s, v in waits:
            if s == engine or fl.get(s, 0) >= v:
                continue
            pruned[s] = max(pruned.get(s, 0), v)
            fl[s] = pruned[s]
            if self.max_waited.get(s, 0) < v:
                self.max_waited[s] = v
        if pruned:
            self.groups.append((engine, pruned, [], [], False))

    def queue_totals(self, base):
        return [(s, v) for s, v in self.sem_count.items()
                if s.startswith(base) and s[len(base):].isdigit()]

    def mark(self):
        return len(self.groups)

    def replay(self, engine, eng, sems, lo, hi):
        for (e, waits, ops, incs, mode) in self.groups[lo:hi]:
            if e != engine:
                continue
            for s, v in sorted(waits.items()):
                eng.wait_ge(sems[s], v)
            if not ops:
                continue
            if mode == "dma":
                for fn, (s, d) in zip(ops, incs):
                    fn(eng).then_inc(sems[s], d)
            elif isinstance(mode, tuple):
                s = incs[0][0]
                start = mode[1]
                for k, fn in enumerate(ops):
                    if k > 0:
                        eng.wait_ge(sems[s], start + k)
                    fn(eng).then_inc(sems[s], 1)
            else:
                last = None
                for fn in ops:
                    last = fn(eng)
                s, d = incs[0]
                last.then_inc(sems[s], d)


def build_program(T=4096, NA=256, NF=512):
    nCA = T // NA
    nCB = nCA
    nCF = T // NF
    nc = bass.Bass("TRN2", target_bir_lowering=False, detect_race_conditions=False)

    dram = {}

    def din(name, shape, dt=F32):
        dram[name] = nc.dram_tensor(name, shape, dt, kind="ExternalInput").ap()
        return dram[name]

    def dint(name, shape, dt=F32):
        kind = "ExternalOutput" if name in DEBUG_OUTPUTS else "Internal"
        dram[name] = nc.dram_tensor(name, shape, dt, kind=kind).ap()
        return dram[name]

    xT = din("xT", [C, T])
    WqT = din("WqT", [C, C]); WkT = din("WkT", [C, C]); WvT = din("WvT", [C, C])
    WgT = din("WgT", [C, C]); WoT = din("WoT", [C, C])
    W0 = din("W0", [C, C]); ATd = din("AT", [8, C]); Bmd = din("Bm", [8, C])
    log_lr = din("log_lr", [C]); ln_g = din("ln_g", [C]); ln_b = din("ln_b", [C]); ls_g = din("ls_g", [C])
    clip_x = din("clip_x", [1]); clip_down = din("clip_down", [1])
    wg_r = din("wg_r", [I_FFN, C]); wu_r = din("wu_r", [I_FFN, C]); wd_r = din("wd_r", [C, I_FFN])
    ps_row_in = din("ps_row", [1, T])
    mask8 = din("mask8", [8])
    outT = nc.dram_tensor("outT", [C, T], F32, kind="ExternalOutput").ap()

    xa_sp = dint("xa_sp", [C, T])
    hh_sp = dint("hh_sp", [C, T], BF16); hl_sp = dint("hl_sp", [C, T], BF16)
    y0_sp = dint("y0_sp", [C, T]); gs_sp = dint("gs_sp", [C, T])
    x1_sp = dint("x1_sp", [C, T]); y_sp = dint("y_sp", [C, T])
    z_sp = dint("z_sp", [I_FFN, T])
    wqg_sp = dint("wqg_sp", [I_FFN, C], BF16)
    wqu_sp = dint("wqu_sp", [I_FFN, C], BF16)
    wqd_sp = dint("wqd_sp", [C, I_FFN], BF16)
    cc_gin = dint("cc_gin", [1024]); cc_gout = dint("cc_gout", [8, 1024])
    cc_m1in = dint("cc_m1in", [128]); cc_m1out = dint("cc_m1out", [1024])
    cc_m2in = dint("cc_m2in", [128]); cc_m2out = dint("cc_m2out", [1024])
    off_hbm = dint("off_hbm", [1024]); cg_hbm = dint("cg_hbm", [1024])

    sch = Sched()
    estack = ExitStack()
    sem_names = list(ENGINES) + ["cc"] + [f"{b}{i}" for b in ("qs", "qg", "qs2")
                                           for i in range(DMA_RR)]
    sems = {n: estack.enter_context(nc.semaphore(n)) for n in sem_names}

    def sb(name, shape, dt=F32):
        return nc.alloc_sbuf_tensor(name, shape, dt)

    lr_t = sb("lr_t", [128, CT])
    g_t = sb("g_t", [128, CT]); b_t = sb("b_t", [128, CT]); ls_t = sb("ls_t", [128, CT])
    off_t = sb("off_t", [128, CT])
    scg = sb("scg", [128, IT]); scu = sb("scu", [128, IT]); scd = sb("scd", [128, CT])
    sgq = sb("sgq", [128, IT]); suq = sb("suq", [128, IT]); sdq = sb("sdq", [128, CT])
    s1v = sb("s1v", [128, 1]); s2v = sb("s2v", [128, 1])
    s1p = sb("s1p", [128, 1]); s2p = sb("s2p", [128, 1])
    rmax = sb("rmax", [128, 16 * CT])
    rmax2 = sb("rmax2", [128, 8 * IT])

    ones1 = sb("ones1", [128, 1]); onesk1 = sb("onesk1", [1, 128])
    ginv_t = sb("ginv_t", [128, CT]); nb_t = sb("nb_t", [128, CT])
    bg_t = sb("bg_t", [128, CT]); ngate_t = sb("ngate_t", [128, CT])
    mask_t = sb("mask_t", [8, 1])
    sc1x1 = sb("sc1x1", [1, 8])
    clipt = sb("clipt", [1, 2])
    redt = sb("redt", [128, 1])
    wmax = sb("wmax", [128, 1])
    NB = 256
    NS = NB
    stv_buf = sb("stv", [1, 13 * NS])
    _SLOT = {"r1": 0, "mrn": 1, "r12": 2, "hc2": 3,
             "r1b": 0, "mrnb": 1, "psc": 2,
             "mu": 4, "var": 5, "te": 6, "te2": 7, "s0": 8, "s2": 9, "r0": 12,
             "r2": 10, "tmp1": 11, "tmp2": 12,
             "e2": 12, "a": 11, "b2": 11, "cc2": 11, "rte": 12, "v1r": 12}

    def SV(n):
        i = _SLOT[n]
        return stv_buf.ap()[:, i * NS:(i + 1) * NS]
    psb = [nc.alloc_psum_tensor(f"ps{i}", [128, 512], F32) for i in range(8)]
    RG = [list(range(8))]

    def hbm3(t, t0, N):
        return t.rearrange("(k p) t -> p k t", p=128)[:, :, t0:t0 + N]

    def sbuf3(buf, N, nt=CT):
        return buf.ap().rearrange("p (k n) -> p k n", k=nt)

    with ExitStack() as p1stack:
        _w = p1stack.enter_context
        wAh = _w(nc.sbuf_tensor("wAh", [128, 8 * C], BF16))
        wAl = _w(nc.sbuf_tensor("wAl", [128, 8 * C], BF16))
        wBh = _w(nc.sbuf_tensor("wBh", [128, 8 * C], BF16))
        wBl = _w(nc.sbuf_tensor("wBl", [128, 8 * C], BF16))
        wCh = _w(nc.sbuf_tensor("wCh", [128, 8 * C], BF16))
        wCl = _w(nc.sbuf_tensor("wCl", [128, 8 * C], BF16))
        p1a = ExitStack()
        _e = p1a.enter_context
        m1row = stv_buf
        x_b0 = _e(nc.sbuf_tensor("x_b0", [128, CT * NA], F32))
        x_b1 = _e(nc.sbuf_tensor("x_b1", [128, CT * NA], F32))
        h32_b = _e(nc.sbuf_tensor("h32_b", [128, CT * NA], F32))
        hh0 = _e(nc.sbuf_tensor("hh0", [128, CT * NA], BF16))
        hh1 = _e(nc.sbuf_tensor("hh1", [128, CT * NA], BF16))
        hl0 = _e(nc.sbuf_tensor("hl0", [128, CT * NA], BF16))
        hl1 = _e(nc.sbuf_tensor("hl1", [128, CT * NA], BF16))
        Kh_b = _e(nc.sbuf_tensor("Kh_b", [128, CT * NA], BF16))
        Kl_b = _e(nc.sbuf_tensor("Kl_b", [128, CT * NA], BF16))
        y0_b0 = _e(nc.sbuf_tensor("y0_b0", [128, CT * NA], F32))
        K_b = _e(nc.sbuf_tensor("K_b", [128, CT * NA], F32))
        G_b = K_b
        gse_b = _e(nc.sbuf_tensor("gse_b", [128, CT * (NA + 1)], F32))
        gse_c = _e(nc.sbuf_tensor("gse_c", [128, CT * (NA + 1)], F32))
        xsq_b = h32_b
        x_b = [x_b0, x_b1]
        hh_p = [hh0, hh1]
        hl_p = [hl0, hl1]
        y0_p = [y0_b0, y0_b0]
        gse_p = [gse_b, gse_c]
        scB = _e(nc.sbuf_tensor("scB", [128, NA], F32))
        wstr0 = _e(nc.sbuf_tensor("wstr0", [128, I_FFN], F32))
        wfst = wstr0
        wit = _e(nc.sbuf_tensor("wit", [128, I_FFN // 8], I32))
        wqb0 = _e(nc.sbuf_tensor("wqb0", [128, I_FFN // 8], BF16))
        wqb1 = _e(nc.sbuf_tensor("wqb1", [128, I_FFN // 8], BF16))
        winv = _e(nc.sbuf_tensor("winv", [128, 1], F32))

        class _V:
            def __init__(v, ap):
                v._ap = ap
            def ap(v):
                return v._ap
        gath = _V(gse_b.ap()[0:8, 0:1024])
        AT_s = _V(gse_b.ap()[0:8, 0:C])
        Bm_s = _V(h32_b.ap()[0:8, 0:C])
        wstr = [wstr0, wstr0]

        def xs(buf, m, N=NA, dt=4):
            return buf.ap()[:, m * N:(m + 1) * N]

        def wsl(w, kk, m):
            return w.ap()[:, kk * C + m * 128: kk * C + m * 128 + 128]

        g = sch.group("sync", writes=["const", "gath", "h32"], dma_sem="qs")
        for kk in range(CT):
            g.op(lambda e, kk=kk: e.dma_start(out=lr_t.ap()[:, kk:kk + 1], in_=log_lr[128 * kk:128 * kk + 128][:, None]))
            g.op(lambda e, kk=kk: e.dma_start(out=g_t.ap()[:, kk:kk + 1], in_=ln_g[128 * kk:128 * kk + 128][:, None]))
            g.op(lambda e, kk=kk: e.dma_start(out=b_t.ap()[:, kk:kk + 1], in_=ln_b[128 * kk:128 * kk + 128][:, None]))
            g.op(lambda e, kk=kk: e.dma_start(out=ls_t.ap()[:, kk:kk + 1], in_=ls_g[128 * kk:128 * kk + 128][:, None]))
        g.op(lambda e: e.dma_start(out=mask_t.ap(), in_=mask8[:, None]))
        g.op(lambda e: e.dma_start(out=clipt.ap()[0:1, 0:1], in_=clip_x[:, None]))
        g.op(lambda e: e.dma_start(out=clipt.ap()[0:1, 1:2], in_=clip_down[:, None]))
        g.op(lambda e: e.dma_start(out=AT_s.ap(), in_=ATd))
        g.op(lambda e: e.dma_start(out=Bm_s.ap(), in_=Bmd))
        g.done()
        g = sch.group("gpsimd", writes=["const2", "off", "rm", "rm2"])
        g.op(lambda e: e.memset(ones1.ap(), 1.0))
        g.op(lambda e: e.memset(onesk1.ap(), 1.0))
        g.op(lambda e: e.memset(off_t.ap(), 0.0))
        g.done()
        g = sch.group("scalar", reads=["const"], writes=["lr1"])
        g.op(lambda e: e.activation(lr_t.ap(), lr_t.ap(), AF.Exp))
        g.done()
        g = sch.group("vector", reads=["lr1", "const"], writes=["lr", "gder"])
        g.op(lambda e: e.tensor_scalar_min(lr_t.ap(), lr_t.ap(), 1.0))
        g.op(lambda e: e.reciprocal(ginv_t.ap(), g_t.ap()))
        g.op(lambda e: e.tensor_scalar_mul(nb_t.ap(), b_t.ap(), -1.0))
        g.op(lambda e: e.tensor_mul(bg_t.ap(), b_t.ap(), ginv_t.ap()))
        g.done()

        def load_split(src_hbm, dsth, dstl, hres, lres, negate=False, weff=False,
                       stage=None, stage_res=None, colscale=None, gacc=False):
            stg = stage if stage is not None else wfst
            sres = stage_res if stage_res is not None else ("wf0", "wf1")
            for kk in range(CT):
                rg = kk % 2
                r0, r1 = rg * C, rg * C + C
                g = sch.group("sync", writes=[sres[rg]], dma_sem="qs")
                g.op(lambda e, kk=kk, r0=r0, r1=r1, src_hbm=src_hbm: e.dma_start(
                    out=stg.ap()[:, r0:r1], in_=src_hbm[128 * kk:128 * kk + 128, :]))
                g.done()
                if colscale is not None:
                    g = sch.group("vector", reads=[sres[rg], "gder"], writes=[sres[rg]])
                    g.op(lambda e, kk=kk, r0=r0, r1=r1: e.tensor_scalar(
                        stg.ap()[:, r0:r1], stg.ap()[:, r0:r1],
                        colscale.ap()[:, kk:kk + 1], None, ALU.mult))
                    g.done()
                if gacc:
                    g = sch.group("tensor", reads=[sres[rg], "gder"], writes=["ps_cg"])
                    for half in range(2):
                        g.op(lambda e, kk=kk, r0=r0, half=half: e.matmul(
                            psb[6].ap()[32 * half:32 * half + 1, 0:512],
                            bg_t.ap()[:, kk:kk + 1],
                            stg.ap()[:, r0 + half * 512:r0 + half * 512 + 512],
                            start=(kk == 0), stop=(kk == CT - 1)))
                    g.done()
                if weff:
                    g = sch.group("tensor", reads=["gath", "h32"], writes=["psm0", "psm1"])
                    for half in range(2):
                        g.op(lambda e, kk=kk, half=half: e.matmul(
                            psb[half].ap()[:, 0:512], AT_s.ap()[:, kk * 128:kk * 128 + 128],
                            Bm_s.ap()[:, half * 512:half * 512 + 512], start=True, stop=True))
                    g.done()
                    g = sch.group("vector", reads=["psm0", "psm1", sres[rg]], writes=[sres[rg]])
                    for half in range(2):
                        g.op(lambda e, r0=r0, half=half: e.tensor_add(
                            stg.ap()[:, r0 + half * 512:r0 + half * 512 + 512],
                            stg.ap()[:, r0 + half * 512:r0 + half * 512 + 512],
                            psb[half].ap()[:, 0:512]))
                    g.done()
                if negate:
                    g = sch.group("scalar", reads=[sres[rg]], writes=[sres[rg]])
                    g.op(lambda e, r0=r0, r1=r1: e.mul(stg.ap()[:, r0:r1], stg.ap()[:, r0:r1], -1.0))
                    g.done()
                g = sch.group("scalar", reads=[sres[rg]], writes=[hres])
                g.op(lambda e, kk=kk, r0=r0, r1=r1, dsth=dsth: e.copy(
                    xs(dsth, kk, C), stg.ap()[:, r0:r1]))
                g.done()
                g = sch.group("gpsimd", reads=[sres[rg], hres], writes=[lres])
                g.op(lambda e, kk=kk, r0=r0, r1=r1, dsth=dsth, dstl=dstl: e.tensor_sub(
                    xs(dstl, kk, C), stg.ap()[:, r0:r1], xs(dsth, kk, C)))
                g.done()

        load_split(WkT, wAh, wAl, "wAh", "wAl")
        load_split(W0, wBh, wBl, "wBh", "wBl", weff=True)
        load_split(WvT, wCh, wCl, "wCh", "wCl", negate=True)
        g = sch.group("gpsimd", reads=["gath", "h32"], writes=["gse0", "gse1"])
        g.op(lambda e: e.memset(gse_b.ap(), 0.0))
        g.op(lambda e: e.memset(gse_c.ap(), 0.0))
        g.done()


        def quant_strip(w_in, wq_out, sc_tile, sc_res, s_i, cols, half=0):
            SEG = I_FFN // 8
            wqbs = [wqb0, wqb1]
            base = 1024 * half if cols <= 1024 else 0
            sres_q = [f"wf{half}"] if cols <= 1024 else ["wf0", "wf1"]
            g = sch.group("scalar", writes=sres_q, dma_sem="qs2")
            g.op(lambda e, s_i=s_i, base=base, cols=cols, w_in=w_in: e.dma_start(
                out=wstr0.ap()[:, base:base + cols], in_=w_in[128 * s_i:128 * s_i + 128, :]))
            g.done()
            segs = [(c0, min(SEG, cols - c0)) for c0 in range(0, cols, SEG)]
            for pi in range(0, len(segs), 2):
                pair = segs[pi:pi + 2]
                if pi == 0:
                    g = sch.group("vector", reads=sres_q,
                                  writes=[sc_res, "winv", "wmax", "wit"], serialize=True)
                    g.op(lambda e, base=base, cols=cols: e.tensor_reduce(
                        wmax.ap(), wstr0.ap()[:, base:base + cols], AX.X, ALU.max,
                        apply_absolute_value=True))
                    g.op(lambda e, s_i=s_i, sc_tile=sc_tile: e.tensor_scalar_mul(
                        sc_tile.ap()[:, s_i:s_i + 1], wmax.ap(), 1.0 / QMAX))
                    g.op(lambda e, s_i=s_i, sc_tile=sc_tile: e.reciprocal(
                        winv.ap(), sc_tile.ap()[:, s_i:s_i + 1]))
                    g.op(lambda e, base=base, cw=pair[0][1]: e.tensor_scalar(
                        wit.ap()[:, 0:cw], wstr0.ap()[:, base:base + cw], winv.ap(),
                        None, ALU.mult))
                    g.done()
                g = sch.group("vector", reads=sres_q + ["winv"],
                              writes=["wit", "wqb0", "wqb1"])
                for bi, (c0, cw) in enumerate(pair):
                    if not (pi == 0 and bi == 0):
                        g.op(lambda e, base=base, c0=c0, cw=cw: e.tensor_scalar(
                            wit.ap()[:, 0:cw], wstr0.ap()[:, base + c0:base + c0 + cw],
                            winv.ap(), None, ALU.mult))
                    g.op(lambda e, bi=bi, cw=cw, wqbs=wqbs: e.tensor_scalar(
                        wqbs[bi].ap()[:, 0:cw], wit.ap()[:, 0:cw], -128.0, 127.0,
                        ALU.max, ALU.min))
                g.done()
                for bi, (c0, cw) in enumerate(pair):
                    g = sch.group("gpsimd", reads=[f"wqb{bi}"], dma_sem="qg")
                    g.op(lambda e, s_i=s_i, c0=c0, cw=cw, bi=bi, wq_out=wq_out,
                         wqbs=wqbs: e.dma_start(
                        out=wq_out[128 * s_i:128 * s_i + 128, c0:c0 + cw],
                        in_=wqbs[bi].ap()[:, 0:cw]))
                    g.done()

        def strip_gen():
            n = 0
            for s_i in range(IT):
                yield (wg_r, wqg_sp, scg, "wscg", s_i, C, n % 2); n += 1
            for s_i in range(IT):
                yield (wu_r, wqu_sp, scu, "wscu", s_i, C, n % 2); n += 1
            for s_i in range(CT):
                yield (wd_r, wqd_sp, scd, "wscd", s_i, I_FFN, 0)

        strips = strip_gen()

        def pump_strips(n):
            for _ in range(n):
                try:
                    quant_strip(*next(strips))
                except StopIteration:
                    return

        def mm3(g, bank, wh, wl, rhit, rlot, m, N):
            passes = [(wh, rhit), (wh, rlot), (wl, rhit)]
            for pi, (ww, rr) in enumerate(passes):
                for kk in range(CT):
                    g.op(lambda e, ww=ww, rr=rr, m=m, kk=kk, bank=bank, N=N,
                         st=(pi == 0 and kk == 0), sp=(pi == 2 and kk == CT - 1): e.matmul(
                        psb[bank].ap()[:, 0:N], wsl(ww, kk, m), xs(rr, kk, N, dt=2),
                        start=st, stop=sp))

        def stats_mm(buf, res, N, xq_buf=None, xq_res="h32", xq_w=None):
            xqb = xq_buf if xq_buf is not None else xsq_b
            xqw = xq_w if xq_w is not None else NA
            g = sch.group("scalar", reads=[res], writes=[xq_res])
            for kk in range(CT):
                g.op(lambda e, kk=kk, buf=buf, N=N: e.activation(
                    xs(xqb, kk, xqw)[:, 0:N], xs(buf, kk, N), AF.Square))
            g.done()
            g = sch.group("tensor", reads=[res, xq_res, "const2"], writes=["ps_st"])
            for kk in range(CT):
                g.op(lambda e, kk=kk, buf=buf, N=N: e.matmul(
                    psb[2].ap()[0:1, 0:N], ones1.ap(), xs(buf, kk, N),
                    start=(kk == 0), stop=(kk == CT - 1)))
            for kk in range(CT):
                g.op(lambda e, kk=kk, N=N: e.matmul(
                    psb[2].ap()[32:33, 0:N], ones1.ap(), xs(xqb, kk, xqw)[:, 0:N],
                    start=(kk == 0), stop=(kk == CT - 1)))
            g.done()

        def stats_math(N, r1_slot, mrn_slot, second=True):
            V = lambda n: SV(n)[:, 0:N]
            g = sch.group("vector", reads=["ps_st"], writes=["stv"])
            g.op(lambda e: e.tensor_scalar_mul(V("mu"), psb[2].ap()[0:1, 0:N], 1.0 / C))
            g.op(lambda e: e.tensor_scalar_mul(V("e2"), psb[2].ap()[32:33, 0:N], 1.0 / C))
            g.op(lambda e: e.tensor_mul(V("a"), V("mu"), V("mu")))
            g.op(lambda e: e.tensor_sub(V("var"), V("e2"), V("a")))
            g.op(lambda e: e.tensor_scalar_add(V("te"), V("var"), EPS))
            if second:
                g.op(lambda e: e.reciprocal(V("rte"), V("te")))
                g.op(lambda e: e.tensor_mul(V("v1r"), V("var"), V("rte")))
                g.op(lambda e: e.tensor_scalar_add(V("te2"), V("v1r"), EPS))
            g.done()
            g = sch.group("scalar", reads=["stv"], writes=["stvs"])
            g.op(lambda e: e.activation(V("s0"), V("te"), AF.Sqrt))
            if second:
                g.op(lambda e: e.activation(V("s2"), V("te2"), AF.Sqrt))
            g.done()
            g = sch.group("vector", reads=["stvs"], writes=["stv"])

            def newton(s_in, t_in, r_out):
                g.op(lambda e, s_in=s_in: e.reciprocal(V("r0"), V(s_in)))
                for srcn, dstn in (("r0", r_out), (r_out, r_out)):
                    g.op(lambda e, srcn=srcn: e.tensor_mul(V("a"), V(srcn), V(srcn)))
                    g.op(lambda e, t_in=t_in: e.tensor_mul(V("b2"), V("a"), V(t_in)))
                    g.op(lambda e: e.tensor_scalar(V("cc2"), V("b2"), -0.5, 1.5, ALU.mult, ALU.add))
                    g.op(lambda e, srcn=srcn, dstn=dstn: e.tensor_mul(V(dstn), V(srcn), V("cc2")))

            newton("s0", "te", r1_slot)
            if second:
                newton("s2", "te2", "r2")
                g.op(lambda e, r1_slot=r1_slot: e.tensor_mul(V("r12"), V(r1_slot), V("r2")))
            g.op(lambda e, r1_slot=r1_slot, mrn_slot=mrn_slot: e.scalar_tensor_tensor(
                V(mrn_slot), V("mu"), -1.0, V(r1_slot), ALU.mult, ALU.mult))
            if second:
                g.op(lambda e, mrn_slot=mrn_slot: e.tensor_mul(V("hc2"), V(mrn_slot), V("r2")))
            g.done()

        def a_front(j):
            px = j % 2
            g = sch.group("sync", writes=[f"x{px}"], dma_sem="qs")
            g.op(lambda e, j=j, px=px: e.dma_start(out=sbuf3(x_b[px], NA),
                                                   in_=hbm3(xT, j * NA, NA)))
            g.done()
            stats_mm(x_b[px], f"x{px}", NA)

        def a_mid(j):
            stats_math(NA, "r1", "mrn", second=True)

        def a_bcast(j):
            g = sch.group("tensor", reads=["stv", "const2"], writes=["ps_bc"])
            for qq, sl in enumerate(("r12", "hc2")):
                g.op(lambda e, qq=qq, sl=sl: e.matmul(
                    psb[3].ap()[:, qq * NA:(qq + 1) * NA], onesk1.ap(),
                    SV(sl)[:, 0:NA], start=True, stop=True))
            g.done()

        def a_xah(j):
            px = j % 2
            g = sch.group("vector", reads=["ps_bc", f"x{px}", "const"],
                          writes=["h32"])
            for kk in range(CT):
                g.op(lambda e, kk=kk, px=px: e.tensor_mul(
                    xs(h32_b, kk), xs(x_b[px], kk), psb[3].ap()[:, 0:NA]))
            for kk in range(CT):
                g.op(lambda e, kk=kk: e.tensor_add(
                    xs(h32_b, kk), xs(h32_b, kk), psb[3].ap()[:, NA:2 * NA]))
            for kk in range(CT):
                g.op(lambda e, kk=kk: e.tensor_scalar(
                    xs(h32_b, kk), xs(h32_b, kk), g_t.ap()[:, kk:kk + 1],
                    b_t.ap()[:, kk:kk + 1], ALU.mult, ALU.add))
            g.done()
            g = sch.group("scalar", reads=["h32"], writes=[f"hh{px}"])
            g.op(lambda e, px=px: e.copy(hh_p[px].ap(), h32_b.ap()))
            g.done()
            g = sch.group("gpsimd", reads=["h32", f"hh{px}"], writes=[f"hl{px}"])
            g.op(lambda e, px=px: e.tensor_sub(hl_p[px].ap(), h32_b.ap(), hh_p[px].ap()))
            g.done()
            g = sch.group("gpsimd", reads=[f"hh{px}", f"hl{px}"], dma_sem="qg")
            g.op(lambda e, j=j, px=px: e.dma_start(out=hbm3(hh_sp, j * NA, NA),
                                                   in_=sbuf3(hh_p[px], NA)))
            g.op(lambda e, j=j, px=px: e.dma_start(out=hbm3(hl_sp, j * NA, NA),
                                                   in_=sbuf3(hl_p[px], NA)))
            g.done()

        def a_K(i):
            px = i % 2
            for m in range(CT):
                pm = f"psm{m % 2}"
                g = sch.group("tensor", reads=[f"hh{px}", f"hl{px}", "wAh", "wAl"], writes=[pm])
                mm3(g, m % 2, wAh, wAl, hh_p[px], hl_p[px], m, NA)
                g.done()
                g = sch.group("scalar", reads=[pm], writes=["K"])
                g.op(lambda e, m=m: e.copy(xs(K_b, m), psb[m % 2].ap()[:, 0:NA]))
                g.done()
            g = sch.group("scalar", reads=["K"], writes=["Kh"])
            g.op(lambda e: e.copy(Kh_b.ap(), K_b.ap()))
            g.done()
            g = sch.group("gpsimd", reads=["K", "Kh"], writes=["Kl"])
            g.op(lambda e: e.tensor_sub(Kl_b.ap(), K_b.ap(), Kh_b.ap()))
            g.done()

        def a_tail(i):
            px = i % 2
            for m in range(CT):
                pm = f"psm{m % 2}"
                g = sch.group("tensor", reads=["Kh", "Kl", "wBh", "wBl"], writes=[pm])
                mm3(g, m % 2, wBh, wBl, Kh_b, Kl_b, m, NA)
                g.done()
                g = sch.group("scalar", reads=[pm], writes=["y0A"])
                g.op(lambda e, m=m, px=px: e.copy(xs(y0_p[px], m), psb[m % 2].ap()[:, 0:NA]))
                g.done()
                g = sch.group("tensor", reads=[f"hh{px}", f"hl{px}", "wCh", "wCl", "y0A"],
                              writes=[pm])
                mm3(g, m % 2, wCh, wCl, hh_p[px], hl_p[px], m, NA)
                g.done()
                g = sch.group("vector", reads=[pm, "y0A"], writes=["K"])
                g.op(lambda e, m=m, px=px: e.tensor_add(scB.ap()[:, 0:NA], xs(y0_p[px], m),
                                                        psb[m % 2].ap()[:, 0:NA]))
                g.op(lambda e, m=m: e.tensor_mul(xs(G_b, m), xs(K_b, m), scB.ap()[:, 0:NA]))
                g.done()
            g = sch.group("gpsimd", reads=["y0A"], dma_sem="qg")
            g.op(lambda e, i=i, px=px: e.dma_start(out=hbm3(y0_sp, i * NA, NA),
                                                   in_=sbuf3(y0_p[px], NA)))
            g.done()
            pg = i % 2
            gcur, gprv = gse_p[pg], gse_p[1 - pg]
            g = sch.group("vector", reads=["K", f"gse{1 - pg}"], writes=[f"gse{pg}"])
            for m in range(CT):
                base = m * (NA + 1)
                g.op(lambda e, base=base, gcur=gcur, gprv=gprv: e.tensor_copy(
                    gcur.ap()[:, base:base + 1], gprv.ap()[:, base + NA:base + NA + 1]))
                g.op(lambda e, base=base, m=m, gcur=gcur, gprv=gprv: e.tensor_tensor_scan(
                    gcur.ap()[:, base + 1:base + 1 + NA], xs(G_b, m), xs(G_b, m),
                    gprv.ap()[:, base + NA:base + NA + 1], ALU.add, ALU.bypass))
            g.done()
            g = sch.group("gpsimd", reads=[f"gse{pg}"], dma_sem="qg")
            g.op(lambda e, i=i, gcur=gcur: e.dma_start(
                out=hbm3(gs_sp, i * NA, NA),
                in_=gcur.ap().rearrange("p (k n) -> p k n", k=CT)[:, :, 0:NA]))
            g.done()

        for s in range(nCA + 1):
            if s >= 1:
                a_K(s - 1)
            if s < nCA:
                a_front(s)
                a_mid(s)
            if s >= 1:
                a_tail(s - 1)
            if s < nCA:
                a_bcast(s)
                a_xah(s)
            pump_strips(4 if s > 0 else 0)
        pump_strips(64)

        pg_last = (nCA - 1) % 2
        glast = gse_b if pg_last == 0 else gse_c
        g = sch.group("gpsimd", reads=[f"gse{pg_last}"], dma_sem="qg")
        for m in range(CT):
            g.op(lambda e, m=m, glast=glast: e.dma_start(
                out=cc_gin[m * 128:m * 128 + 128][:, None],
                in_=glast.ap()[:, m * (NA + 1) + NA:m * (NA + 1) + NA + 1]))
        g.done()
        g = sch.group("gpsimd", writes=["ccg"], sem="cc", waits=sch.queue_totals("qg"))
        g.op(lambda e: e.collective_compute("AllGather", ALU.bypass, replica_groups=RG,
                                            ins=[cc_gin.opt()], outs=[cc_gout.opt()]))
        g.done()
        g = sch.group("sync", reads=["ccg"], writes=["gath", "gse0"], dma_sem="qs")
        g.op(lambda e: e.dma_start(out=gath.ap(), in_=cc_gout))
        g.done()
        for half in range(2):
            g = sch.group("tensor", reads=["gath", "const"], writes=["ps_bc"])
            g.op(lambda e, half=half: e.matmul(
                psb[3].ap()[0:1, 0:512], mask_t.ap(),
                gath.ap()[:, half * 512:half * 512 + 512], start=True, stop=True))
            g.done()
            g = sch.group("scalar", reads=["ps_bc"], writes=["offrow", "stv"])
            g.op(lambda e, half=half: e.copy(m1row.ap()[0:1, half * 512:half * 512 + 512],
                                             psb[3].ap()[0:1, 0:512]))
            g.done()
        g = sch.group("gpsimd", reads=["offrow"], dma_sem="qg")
        g.op(lambda e: e.dma_start(out=off_hbm[None, :], in_=m1row.ap()[0:1, 0:1024]))
        g.done()
        g = sch.group("sync", writes=["off"], dma_sem="qs", waits=sch.queue_totals("qg"))
        for m in range(CT):
            g.op(lambda e, m=m: e.dma_start(out=off_t.ap()[:, m:m + 1],
                                            in_=off_hbm[m * 128:m * 128 + 128][:, None]))
        g.done()

        snapAB = dict(sch.sem_count)
        for e in ENGINES:
            sch.wait_only(e, [(s, v) for s, v in snapAB.items()])
        p1a.close()

        p1b = ExitStack()
        _b = p1b.enter_context
        hhB = [_b(nc.sbuf_tensor(f"hhB{p}", [128, CT * NB], BF16)) for p in range(2)]
        hlB = [_b(nc.sbuf_tensor(f"hlB{p}", [128, CT * NB], BF16)) for p in range(2)]
        x1B = [_b(nc.sbuf_tensor(f"x1B{p}", [128, CT * NB], F32)) for p in range(2)]
        y0B = [_b(nc.sbuf_tensor(f"y0B{p}", [128, CT * NB], F32)) for p in range(2)]
        gseB = [_b(nc.sbuf_tensor(f"gseB{p}", [128, CT * (NB + 1)], F32)) for p in range(2)]
        xahB = _b(nc.sbuf_tensor("xahB", [128, CT * NB], BF16))
        xalB = _b(nc.sbuf_tensor("xalB", [128, CT * NB], BF16))
        h32B = _b(nc.sbuf_tensor("h32B", [128, CT * NB], F32))
        scAB = _b(nc.sbuf_tensor("scAB", [128, NB], F32))
        scBB = _b(nc.sbuf_tensor("scBB", [128, NB], F32))
        sresB = ("h32B0", "h32B1")

        load_split(WqT, wAh, wAl, "wAh", "wAl", stage=h32B, stage_res=sresB)
        load_split(WgT, wBh, wBl, "wBh", "wBl", stage=h32B, stage_res=sresB,
                   colscale=ginv_t, gacc=True)
        load_split(WoT, wCh, wCl, "wCh", "wCl", stage=h32B, stage_res=sresB)
        g = sch.group("scalar", reads=["ps_cg"], writes=["offrow", "stv"])
        for half in range(2):
            g.op(lambda e, half=half: e.mul(
                m1row.ap()[0:1, half * 512:half * 512 + 512],
                psb[6].ap()[32 * half:32 * half + 1, 0:512], -1.0))
        g.done()
        g = sch.group("gpsimd", reads=["offrow"], dma_sem="qg")
        g.op(lambda e: e.dma_start(out=cg_hbm[None, :], in_=m1row.ap()[0:1, 0:1024]))
        g.done()
        g = sch.group("sync", writes=["cgt"], dma_sem="qs", waits=sch.queue_totals("qg"))
        for m in range(CT):
            g.op(lambda e, m=m: e.dma_start(out=ngate_t.ap()[:, m:m + 1],
                                            in_=cg_hbm[m * 128:m * 128 + 128][:, None]))
        g.done()

        def b_loads(j):
            px = j % 2
            for dst, res_, srcp in [(hhB[px], f"hh{px}", hh_sp), (hlB[px], f"hl{px}", hl_sp),
                                    (y0B[px], f"y0{px}", y0_sp)]:
                g = sch.group("sync", writes=[res_], dma_sem="qs")
                g.op(lambda e, j=j, dst=dst, srcp=srcp: e.dma_start(
                    out=sbuf3(dst, NB), in_=hbm3(srcp, j * NB, NB)))
                g.done()
            g = sch.group("sync", writes=[f"gse{px}"], dma_sem="qs")
            g.op(lambda e, j=j, px=px: e.dma_start(
                out=gseB[px].ap().rearrange("p (k n) -> p k n", k=CT)[:, :, 0:NB],
                in_=hbm3(gs_sp, j * NB, NB)))
            g.done()
            psl = SV("psc") if px == 0 else SV("hc2")
            g = sch.group("sync", writes=[f"psc{px}"], dma_sem="qs")
            g.op(lambda e, j=j, psl=psl: e.dma_start(out=psl[:, 0:NB],
                                                     in_=ps_row_in[0:1, j * NB:j * NB + NB]))
            g.done()

        def b_main(i):
            px = i % 2
            psl = SV("psc") if px == 0 else SV("hc2")
            g = sch.group("tensor", reads=[f"psc{px}", "const2"], writes=["ps_ps"])
            g.op(lambda e, psl=psl: e.matmul(psb[7].ap()[:, 0:NB], onesk1.ap(), psl[:, 0:NB],
                                             start=True, stop=True))
            g.done()
            for m in range(CT):
                pm = f"psm{m % 2}"
                pgm = f"psg{m % 2}"
                g = sch.group("tensor", reads=[f"hh{px}", f"hl{px}", "wAh", "wAl"], writes=[pm])
                mm3(g, m % 2, wAh, wAl, hhB[px], hlB[px], m, NB)
                g.done()
                g = sch.group("tensor", reads=[f"hh{px}", f"hl{px}", "wBh", "wBl"], writes=[pgm])
                mm3(g, 4 + m % 2, wBh, wBl, hhB[px], hlB[px], m, NB)
                g.done()
                g = sch.group("scalar", reads=[pgm, "cgt"], writes=["h32B"])
                g.op(lambda e, m=m: e.activation(xs(h32B, m, NB), psb[4 + m % 2].ap()[:, 0:NB],
                                                 AF.Sigmoid, bias=ngate_t.ap()[:, m:m + 1]))
                g.done()
                g = sch.group("vector",
                              reads=[f"gse{px}", "off", "ps_ps", pm, f"y0{px}", "h32B", "lr"],
                              writes=["h32B"])
                base = m * (NB + 1)
                g.op(lambda e, base=base, m=m, px=px: e.scalar_tensor_tensor(
                    scAB.ap(), gseB[px].ap()[:, base:base + NB], off_t.ap()[:, m:m + 1],
                    psb[7].ap()[:, 0:NB], ALU.add, ALU.mult))
                g.op(lambda e, m=m: e.scalar_tensor_tensor(
                    scBB.ap(), psb[m % 2].ap()[:, 0:NB], lr_t.ap()[:, m:m + 1], scAB.ap(),
                    ALU.mult, ALU.mult))
                g.op(lambda e, m=m, px=px: e.tensor_sub(scAB.ap(), xs(y0B[px], m, NB),
                                                        scBB.ap()))
                g.op(lambda e, m=m: e.tensor_mul(xs(h32B, m, NB), scAB.ap(), xs(h32B, m, NB)))
                g.done()
            g = sch.group("scalar", reads=["h32B"], writes=["xah"])
            g.op(lambda e: e.copy(xahB.ap(), h32B.ap()))
            g.done()
            g = sch.group("gpsimd", reads=["h32B", "xah"], writes=["xal"])
            g.op(lambda e: e.tensor_sub(xalB.ap(), h32B.ap(), xahB.ap()))
            g.done()
            g = sch.group("vector", reads=[f"hh{px}", f"hl{px}", "gder", "const"],
                          writes=[f"x1{px}"])
            for kk in range(CT):
                g.op(lambda e, kk=kk, px=px: e.tensor_add(
                    scAB.ap(), xs(hhB[px], kk, NB), xs(hlB[px], kk, NB)))
                g.op(lambda e, kk=kk: e.tensor_scalar(
                    scAB.ap(), scAB.ap(), nb_t.ap()[:, kk:kk + 1],
                    ginv_t.ap()[:, kk:kk + 1], ALU.add, ALU.mult))
                g.op(lambda e, kk=kk, px=px: e.tensor_add(
                    xs(x1B[px], kk, NB), xs(x1B[px], kk, NB), scAB.ap()))
            g.done()
            for m in range(CT):
                pm = f"psm{m % 2}"
                g = sch.group("tensor", reads=["xah", "xal", "wCh", "wCl"], writes=[pm])
                mm3(g, m % 2, wCh, wCl, xahB, xalB, m, NB)
                g.done()
                g = sch.group("vector", reads=[pm, "const"], writes=[f"x1{px}"])
                g.op(lambda e, m=m, px=px: e.scalar_tensor_tensor(
                    xs(x1B[px], m, NB), psb[m % 2].ap()[:, 0:NB], ls_t.ap()[:, m:m + 1],
                    xs(x1B[px], m, NB), ALU.mult, ALU.add))
                g.done()
            g = sch.group("gpsimd", reads=[f"x1{px}"], dma_sem="qg")
            g.op(lambda e, i=i, px=px: e.dma_start(out=hbm3(x1_sp, i * NB, NB),
                                                   in_=sbuf3(x1B[px], NB)))
            g.done()
            stats_mm(x1B[px], f"x1{px}", NB, xq_buf=h32B, xq_res="h32B", xq_w=NB)
            stats_math(NB, "r1b", "mrnb", second=False)

        def b_y(i):
            px = i % 2
            g = sch.group("tensor", reads=["stv", "const2"], writes=["ps_bc"])
            g.op(lambda e: e.matmul(psb[3].ap()[:, 0:2 * NB], onesk1.ap(),
                                    stv_buf.ap()[:, 0:2 * NB], start=True, stop=True))
            g.done()
            g = sch.group("vector", reads=["ps_bc", f"x1{px}"], writes=["h32B", "rm"])
            for kk in range(CT):
                g.op(lambda e, kk=kk, px=px: e.tensor_mul(
                    xs(h32B, kk, NB), xs(x1B[px], kk, NB), psb[3].ap()[:, 0:NB]))
            for kk in range(CT):
                g.op(lambda e, kk=kk: e.tensor_add(xs(h32B, kk, NB), xs(h32B, kk, NB),
                                                   psb[3].ap()[:, NB:2 * NB]))
            for kk in range(CT):
                g.op(lambda e, kk=kk, i=i: e.tensor_reduce(
                    rmax.ap()[:, i * CT + kk:i * CT + kk + 1], xs(h32B, kk, NB), AX.X,
                    ALU.max, apply_absolute_value=True))
            g.done()
            g = sch.group("gpsimd", reads=["h32B"], dma_sem="qg")
            g.op(lambda e, i=i: e.dma_start(out=hbm3(y_sp, i * NB, NB), in_=sbuf3(h32B, NB)))
            g.done()

        def b_loadx(j):
            px = j % 2
            g = sch.group("sync", writes=[f"x1{px}"], dma_sem="qs")
            g.op(lambda e, j=j, px=px: e.dma_start(out=sbuf3(x1B[px], NB),
                                                   in_=hbm3(xT, j * NB, NB)))
            g.done()

        nCBB = T // NB
        b_loadx(0)
        for s in range(nCBB + 2):
            if s >= 2:
                b_y(s - 2)
            if s < nCBB:
                b_loads(s)
                if s >= 1:
                    b_loadx(s)
            if s >= 1 and s - 1 < nCBB:
                b_main(s - 1)

        part1 = sch.mark()
        with nc.Block() as blk:
            for e in ENGINES:
                def closure(eng, e=e):
                    sch.replay(e, eng, sems, 0, part1)
                getattr(blk, e)(closure)
        p1b.close()

    with ExitStack() as p2stack:
        _e2 = p2stack.enter_context
        wqgT = _e2(nc.sbuf_tensor("wqgT", [128, CT * I_FFN], BF16))
        wquT = _e2(nc.sbuf_tensor("wquT", [128, CT * I_FFN], BF16))
        wqdT = _e2(nc.sbuf_tensor("wqdT", [128, IT * C], BF16))
        y_bf0 = _e2(nc.sbuf_tensor("y_bf0", [128, CT * NF], F32))
        y_bf1 = y_bf0
        xq_b = _e2(nc.sbuf_tensor("xq_b", [128, CT * NF], BF16))
        itq = _e2(nc.sbuf_tensor("itq", [128, NF], I32))
        silu0 = _e2(nc.sbuf_tensor("silu0", [128, NF], F32))
        silu1 = _e2(nc.sbuf_tensor("silu1", [128, NF], F32))
        z_t0 = _e2(nc.sbuf_tensor("z_t0", [128, NF], F32))
        z_t1 = _e2(nc.sbuf_tensor("z_t1", [128, NF], F32))
        zq_t0 = _e2(nc.sbuf_tensor("zq_t0", [128, NF], BF16))
        zq_t1 = _e2(nc.sbuf_tensor("zq_t1", [128, NF], BF16))
        out_b = _e2(nc.sbuf_tensor("out_b", [128, CT * NF], F32))
        m2row = _e2(nc.sbuf_tensor("m2row", [1, 1024], F32))
        dbgbufS = _e2(nc.sbuf_tensor("dbgbufS", [128, 64], F32))

        y_bufs = [y_bf0, y_bf1]
        silu_t = [silu0, silu1]
        z_t = [z_t0, z_t1]
        zq_t = [zq_t0, zq_t1]

        snap = dict(sch.sem_count)
        for e in ENGINES:
            sch.wait_only(e, [(s, v) for s, v in snap.items()])

        g = sch.group("sync", writes=["wqg", "wqu", "wqd"], dma_sem="qs")
        for kk in range(CT):
            g.op(lambda e, kk=kk: e.dma_start_transpose(
                wqgT.ap()[:, kk * I_FFN:(kk + 1) * I_FFN], wqg_sp[:, 128 * kk:128 * kk + 128]))
            g.op(lambda e, kk=kk: e.dma_start_transpose(
                wquT.ap()[:, kk * I_FFN:(kk + 1) * I_FFN], wqu_sp[:, 128 * kk:128 * kk + 128]))
        for kk in range(IT):
            g.op(lambda e, kk=kk: e.dma_start_transpose(
                wqdT.ap()[:, kk * C:(kk + 1) * C], wqd_sp[:, 128 * kk:128 * kk + 128]))
        g.done()

        g = sch.group("vector", reads=["rm"], writes=["redm"])
        g.op(lambda e: e.tensor_reduce(redt.ap(), rmax.ap()[:, 0:(T // NB) * CT], AX.X, ALU.max))
        g.done()
        g = sch.group("gpsimd", reads=["redm"], dma_sem="qg")
        g.op(lambda e: e.dma_start(out=cc_m1in[:, None], in_=redt.ap()))
        g.done()
        g = sch.group("gpsimd", writes=["ccm1"], sem="cc", waits=sch.queue_totals("qg"))
        g.op(lambda e: e.collective_compute("AllGather", ALU.bypass, replica_groups=RG,
                                            ins=[cc_m1in.opt()], outs=[cc_m1out.opt()]))
        g.done()
        g = sch.group("sync", reads=["ccm1"], writes=["m1row"], dma_sem="qs")
        g.op(lambda e: e.dma_start(out=m2row.ap(), in_=cc_m1out[None, :]))
        g.done()
        g = sch.group("scalar", reads=["const"], writes=["sig1"])
        g.op(lambda e: e.activation(sc1x1.ap()[0:1, 0:1], clipt.ap()[0:1, 0:1], AF.Sigmoid))
        g.done()
        g = sch.group("vector", reads=["m1row", "sig1"], writes=["s1s"], serialize=True)
        g.op(lambda e: e.tensor_reduce(sc1x1.ap()[0:1, 1:2], m2row.ap(), AX.X, ALU.max))
        g.op(lambda e: e.tensor_mul(sc1x1.ap()[0:1, 2:3], sc1x1.ap()[0:1, 1:2], sc1x1.ap()[0:1, 0:1]))
        g.op(lambda e: e.tensor_scalar_mul(sc1x1.ap()[0:1, 2:3], sc1x1.ap()[0:1, 2:3], 1.0 / QMAX))
        g.op(lambda e: e.reciprocal(sc1x1.ap()[0:1, 3:4], sc1x1.ap()[0:1, 2:3]))
        g.done()
        g = sch.group("tensor", reads=["s1s", "const2"], writes=["ps_bc"])
        g.op(lambda e: e.matmul(psb[3].ap()[:, 0:2], onesk1.ap(), sc1x1.ap()[0:1, 2:4],
                                start=True, stop=True))
        g.done()
        g = sch.group("scalar", reads=["ps_bc"], writes=["s1v"])
        g.op(lambda e: e.copy(s1p.ap(), psb[3].ap()[:, 0:1]))
        g.op(lambda e: e.copy(s1v.ap(), psb[3].ap()[:, 1:2]))
        g.done()
        g = sch.group("vector", reads=["s1v", "wscg", "wscu"], writes=["sq1"])
        g.op(lambda e: e.tensor_scalar(sgq.ap(), scg.ap(), s1p.ap(), None, ALU.mult))
        g.op(lambda e: e.tensor_scalar(suq.ap(), scu.ap(), s1p.ap(), None, ALU.mult))
        g.done()

        for i in range(nCF):
            t0 = i * NF
            p = i % 2
            ybr = "yb"
            g = sch.group("sync", writes=[ybr], dma_sem="qs")
            g.op(lambda e, t0=t0, p=p: e.dma_start(out=sbuf3(y_bufs[p], NF), in_=hbm3(y_sp, t0, NF)))
            g.done()
            g = sch.group("vector", reads=[ybr, "s1v"], writes=["xq"])
            for kk in range(CT):
                g.op(lambda e, kk=kk, p=p: e.tensor_scalar(
                    itq.ap(), y_bufs[p].ap()[:, kk * NF:(kk + 1) * NF], s1v.ap(), None, ALU.mult))
                g.op(lambda e, kk=kk: e.tensor_scalar(
                    xq_b.ap()[:, kk * NF:(kk + 1) * NF], itq.ap(), -128.0, 127.0, ALU.max, ALU.min))
            g.done()
            for kb in range(IT):
                pp = kb % 2
                g = sch.group("tensor", reads=["xq", "wqg"], writes=[f"psf{pp}"])
                for kk in range(CT):
                    g.op(lambda e, kb=kb, kk=kk, pp=pp: e.matmul(
                        psb[pp].ap()[:, 0:NF],
                        wqgT.ap()[:, kk * I_FFN + kb * 128: kk * I_FFN + kb * 128 + 128],
                        xq_b.ap()[:, kk * NF:(kk + 1) * NF], start=(kk == 0), stop=(kk == CT - 1)))
                g.done()
                g = sch.group("scalar", reads=[f"psf{pp}", "sq1"], writes=[f"sil{pp}"])
                g.op(lambda e, kb=kb, pp=pp: e.activation(
                    silu_t[pp].ap(), psb[pp].ap()[:, 0:NF], AF.Silu, scale=sgq.ap()[:, kb:kb + 1]))
                g.done()
                g = sch.group("tensor", reads=["xq", "wqu"], writes=[f"psf{2 + pp}"])
                for kk in range(CT):
                    g.op(lambda e, kb=kb, kk=kk, pp=pp: e.matmul(
                        psb[2 + pp].ap()[:, 0:NF],
                        wquT.ap()[:, kk * I_FFN + kb * 128: kk * I_FFN + kb * 128 + 128],
                        xq_b.ap()[:, kk * NF:(kk + 1) * NF], start=(kk == 0), stop=(kk == CT - 1)))
                g.done()
                g = sch.group("vector", reads=[f"psf{2 + pp}", f"sil{pp}", "sq1"],
                              writes=[f"zt{pp}", "rm2"])
                g.op(lambda e, kb=kb, pp=pp: e.scalar_tensor_tensor(
                    z_t[pp].ap(), silu_t[pp].ap(), suq.ap()[:, kb:kb + 1], psb[2 + pp].ap()[:, 0:NF],
                    ALU.mult, ALU.mult))
                g.op(lambda e, kb=kb, pp=pp, i=i: e.tensor_reduce(
                    rmax2.ap()[:, i * IT + kb:i * IT + kb + 1], z_t[pp].ap(), AX.X, ALU.max,
                    apply_absolute_value=True))
                g.done()
                g = sch.group("gpsimd", reads=[f"zt{pp}"], dma_sem="qg")
                g.op(lambda e, t0=t0, kb=kb, pp=pp: e.dma_start(
                    out=z_sp[kb * 128:kb * 128 + 128, t0:t0 + NF], in_=z_t[pp].ap()))
                g.done()

        g = sch.group("vector", reads=["rm2"], writes=["redm2"])
        g.op(lambda e, nCF=nCF: e.tensor_reduce(redt.ap(), rmax2.ap()[:, 0:nCF * IT], AX.X, ALU.max))
        g.done()
        g = sch.group("gpsimd", reads=["redm2"], dma_sem="qg")
        g.op(lambda e: e.dma_start(out=cc_m2in[:, None], in_=redt.ap()))
        g.done()
        g = sch.group("gpsimd", writes=["ccm2"], sem="cc", waits=sch.queue_totals("qg"))
        g.op(lambda e: e.collective_compute("AllGather", ALU.bypass, replica_groups=RG,
                                            ins=[cc_m2in.opt()], outs=[cc_m2out.opt()]))
        g.done()
        g = sch.group("sync", reads=["ccm2"], writes=["m2row"], dma_sem="qs")
        g.op(lambda e: e.dma_start(out=m2row.ap(), in_=cc_m2out[None, :]))
        g.done()
        g = sch.group("scalar", reads=["const"], writes=["sig2"])
        g.op(lambda e: e.activation(sc1x1.ap()[0:1, 4:5], clipt.ap()[0:1, 1:2], AF.Sigmoid))
        g.done()
        g = sch.group("vector", reads=["m2row", "sig2"], writes=["s2s"], serialize=True)
        g.op(lambda e: e.tensor_reduce(sc1x1.ap()[0:1, 5:6], m2row.ap(), AX.X, ALU.max))
        g.op(lambda e: e.tensor_mul(sc1x1.ap()[0:1, 6:7], sc1x1.ap()[0:1, 5:6], sc1x1.ap()[0:1, 4:5]))
        g.op(lambda e: e.tensor_scalar_mul(sc1x1.ap()[0:1, 6:7], sc1x1.ap()[0:1, 6:7], 1.0 / QMAX))
        g.op(lambda e: e.reciprocal(sc1x1.ap()[0:1, 7:8], sc1x1.ap()[0:1, 6:7]))
        g.done()
        g = sch.group("tensor", reads=["s2s", "const2"], writes=["ps_bc2"])
        g.op(lambda e: e.matmul(psb[6].ap()[:, 0:2], onesk1.ap(), sc1x1.ap()[0:1, 6:8],
                                start=True, stop=True))
        g.done()
        g = sch.group("scalar", reads=["ps_bc2"], writes=["s2v"])
        g.op(lambda e: e.copy(s2p.ap(), psb[6].ap()[:, 0:1]))
        g.op(lambda e: e.copy(s2v.ap(), psb[6].ap()[:, 1:2]))
        g.done()
        g = sch.group("vector", reads=["s2v", "wscd"], writes=["sqd"])
        g.op(lambda e: e.tensor_scalar(sdq.ap(), scd.ap(), s2p.ap(), None, ALU.mult))
        g.done()

        for i in range(nCF):
            t0 = i * NF
            p = i % 2
            ybr = "yb"
            g = sch.group("sync", writes=[ybr], dma_sem="qs")
            g.op(lambda e, t0=t0, p=p: e.dma_start(out=sbuf3(y_bufs[p], NF), in_=hbm3(x1_sp, t0, NF)))
            g.done()
            for kb in range(IT):
                pp = kb % 2
                g = sch.group("sync", writes=[f"zt{pp}"], dma_sem="qs")
                g.op(lambda e, t0=t0, kb=kb, pp=pp: e.dma_start(
                    out=z_t[pp].ap(), in_=z_sp[kb * 128:kb * 128 + 128, t0:t0 + NF]))
                g.done()
                g = sch.group("vector", reads=[f"zt{pp}", "s2v"], writes=[f"zq{pp}"])
                g.op(lambda e, pp=pp: e.tensor_scalar(itq.ap(), z_t[pp].ap(), s2v.ap(), None, ALU.mult))
                g.op(lambda e, pp=pp: e.tensor_scalar(zq_t[pp].ap(), itq.ap(), -128.0, 127.0,
                                                      ALU.max, ALU.min))
                g.done()
                psw = ["psd"] if not (i == 0 and kb == 0) else [
                    "psd", "psf0", "psf1", "psf2", "psf3", "ps_bc2"]
                g = sch.group("tensor", reads=[f"zq{pp}", "wqd"], writes=psw)
                for mm in range(CT):
                    g.op(lambda e, kb=kb, mm=mm, pp=pp: e.matmul(
                        psb[mm].ap()[:, 0:NF],
                        wqdT.ap()[:, kb * C + mm * 128: kb * C + mm * 128 + 128],
                        zq_t[pp].ap(), start=(kb == 0), stop=(kb == IT - 1)))
                g.done()
            g = sch.group("scalar", reads=["psd", "sqd"], writes=["outb"])
            for mm in range(CT):
                g.op(lambda e, mm=mm: e.activation(
                    out_b.ap()[:, mm * NF:(mm + 1) * NF], psb[mm].ap()[:, 0:NF],
                    AF.Copy, scale=sdq.ap()[:, mm:mm + 1]))
            g.done()
            g = sch.group("vector", reads=["outb", ybr], writes=["outb"])
            for kk in range(CT):
                g.op(lambda e, kk=kk, p=p: e.tensor_add(
                    out_b.ap()[:, kk * NF:(kk + 1) * NF], out_b.ap()[:, kk * NF:(kk + 1) * NF],
                    y_bufs[p].ap()[:, kk * NF:(kk + 1) * NF]))
            g.done()
            g = sch.group("sync", reads=["outb"], dma_sem="qs")
            g.op(lambda e, t0=t0: e.dma_start(out=hbm3(outT, t0, NF), in_=sbuf3(out_b, NF)))
            g.done()

        if DEBUG_SCALES:
            dbg_sc = dint("dbg_sc", [128, 64])
            g = sch.group("vector", reads=["wscg", "wscu", "wscd", "sq1", "sqd", "s1v", "s2v"],
                          writes=["dbgsc"])
            g.op(lambda e: e.tensor_copy(dbgbufS.ap()[:, 0:IT], scg.ap()))
            g.op(lambda e: e.tensor_copy(dbgbufS.ap()[:, 22:22 + IT], scu.ap()))
            g.op(lambda e: e.tensor_copy(dbgbufS.ap()[:, 44:44 + CT], scd.ap()))
            g.op(lambda e: e.tensor_copy(dbgbufS.ap()[:, 52:53], s1v.ap()))
            g.op(lambda e: e.tensor_copy(dbgbufS.ap()[:, 53:54], s2v.ap()))
            g.op(lambda e: e.tensor_copy(dbgbufS.ap()[:, 54:55], s1p.ap()))
            g.op(lambda e: e.tensor_copy(dbgbufS.ap()[:, 55:56], s2p.ap()))
            g.done()
            g = sch.group("gpsimd", reads=["dbgsc"], dma_sem="qg")
            g.op(lambda e: e.dma_start(out=dbg_sc, in_=dbgbufS.ap()))
            g.done()
        sch.wait_only("sync", sch.queue_totals("qs"))
        sch.wait_only("gpsimd", sch.queue_totals("qg"))

        part2 = sch.mark()
        with nc.Block() as blk2:
            for e in ENGINES:
                def closure2(eng, e=e):
                    sch.replay(e, eng, sems, part1, part2)
                getattr(blk2, e)(closure2)

    estack.close()
    return nc


_T = 4096


def _prep_core_inputs(core, inputs, T):
    b, q = core // 4, core % 4
    t0 = q * T
    d = {}
    d["xT"] = np.ascontiguousarray(inputs["L0_attn_x"][b, t0:t0 + T, :].T)
    d["WqT"] = np.ascontiguousarray(inputs["L0_attn_Wq"].T)
    d["WkT"] = np.ascontiguousarray(inputs["L0_attn_Wk"].T)
    d["WvT"] = np.ascontiguousarray(inputs["L0_attn_Wv"].T)
    d["WgT"] = np.ascontiguousarray(inputs["L0_attn_Wg"].T)
    d["WoT"] = np.ascontiguousarray(inputs["L0_attn_Wo"].T)
    d["W0"] = np.ascontiguousarray(inputs["L0_attn_W0"])
    d["AT"] = np.ascontiguousarray(inputs["L0_attn_A_init"].T)
    d["Bm"] = np.ascontiguousarray(inputs["L0_attn_B_init"])
    d["log_lr"] = inputs["L0_attn_log_lr"]
    d["ln_g"] = inputs["L0_attn_ln_g"]
    d["ln_b"] = inputs["L0_attn_ln_b"]
    d["ls_g"] = inputs["L0_attn_ls_g"]
    d["clip_x"] = inputs["L0_ffn_clip_x"]
    d["clip_down"] = inputs["L0_ffn_clip_down"]
    d["wg_r"] = inputs["L0_ffn_w_gate"]
    d["wu_r"] = inputs["L0_ffn_w_up"]
    d["wd_r"] = inputs["L0_ffn_w_down"]
    pos = np.arange(t0 + 1, t0 + T + 1, dtype=np.float64)
    d["ps_row"] = (1.0 / (1.0 + 0.1 * np.log(pos))).astype(np.float32)[None, :]
    m = np.zeros(8, dtype=np.float32)
    m[b * 4:b * 4 + q] = 1.0
    d["mask8"] = m
    return d


def kernel(**inputs):
    T = _T
    nc = build_program(T=T)
    in_maps = [_prep_core_inputs(c, inputs, T) for c in range(8)]
    res = run_bass_kernel_spmd(nc, in_maps, core_ids=list(range(8)))
    B, Tfull, _ = inputs["L0_attn_x"].shape
    out = np.empty((B, Tfull, C), dtype=np.float32)
    for c in range(8):
        b, q = c // 4, c % 4
        out[b, q * T:(q + 1) * T, :] = np.asarray(res.results[c]["outT"]).T
    return out


# version 15
# speedup vs baseline: 1.0375x; 1.0132x over previous
import numpy as np
import ml_dtypes
from contextlib import ExitStack

import concourse.bass as bass
import concourse.mybir as mybir
from concourse.bass_utils import run_bass_kernel_spmd

F32 = mybir.dt.float32
BF16 = mybir.dt.bfloat16
I32 = mybir.dt.int32
AF = mybir.ActivationFunctionType
ALU = mybir.AluOpType
AX = mybir.AxisListType

C = 1024
CT = 8
I_FFN = 2816
IT = 22
EPS = 1e-5
QMAX = 127.0

ENGINES = ("sync", "tensor", "vector", "scalar", "gpsimd")
DMA_RR = 4
DEBUG_OUTPUTS = ()
DEBUG_SCALES = False


class Sched:

    def __init__(self):
        self.groups = []
        self.sem_count = {}
        self.res = {}
        self.floor = {e: {} for e in ENGINES}
        self.max_waited = {}
        self.dma_rr = {}

    def _rec(self, name):
        return self.res.setdefault(name, {"w": None, "r": []})

    def _bump(self, sem, delta):
        self.sem_count[sem] = self.sem_count.get(sem, 0) + delta
        return self.sem_count[sem]

    def group(self, engine, reads=(), writes=(), sem=None, dma_sem=None, waits=(),
              serialize=False):
        sch = self
        want = {}
        if dma_sem is not None:
            idx = self.dma_rr.get(dma_sem, 0)
            self.dma_rr[dma_sem] = (idx + 1) % DMA_RR
            dma_sem = f"{dma_sem}{idx}"

        def add(sw):
            if sw is None:
                return
            s, v = sw
            if s == engine:
                return
            if want.get(s, 0) < v:
                want[s] = v

        for r in reads:
            add(self._rec(r)["w"])
        for w in writes:
            rec = self._rec(w)
            add(rec["w"])
            for sw in rec["r"]:
                add(sw)
        for sw in waits:
            add(sw)
        if dma_sem is not None:
            add((dma_sem, self.sem_count.get(dma_sem, 0)))
        elif sem and sem != engine:
            add((sem, self.sem_count.get(sem, 0)))
        for s, v in want.items():
            if self.max_waited.get(s, 0) < v:
                self.max_waited[s] = v
        fl = self.floor[engine]
        pruned = {s: v for s, v in want.items() if fl.get(s, 0) < v}
        for s, v in pruned.items():
            fl[s] = v

        class G:
            __slots__ = ("ops",)

            def __init__(g):
                g.ops = []

            def op(g, fn):
                g.ops.append(fn)
                return g

            def done(g):
                if dma_sem is not None:
                    incs = [(dma_sem, 16)] * len(g.ops)
                    val = sch._bump(dma_sem, 16 * len(g.ops))
                    mysem = dma_sem
                    mode = "dma"
                else:
                    s = sem or engine
                    n = len(g.ops) if serialize else 1
                    val = sch._bump(s, n)
                    incs = [(s, 1)] * n
                    mysem = s
                    mode = ("ser", val - n) if serialize else "eng"
                    if s == engine:
                        fl2 = sch.floor[engine]
                        fl2[s] = max(fl2.get(s, 0), val)
                sch.groups.append((engine, pruned, list(g.ops), incs, mode))
                for w in writes:
                    rec = sch._rec(w)
                    rec["w"] = (mysem, val)
                    rec["r"] = []
                for r in reads:
                    sch._rec(r)["r"].append((mysem, val))
                return (mysem, val)

        return G()

    def wait_only(self, engine, waits):
        fl = self.floor[engine]
        pruned = {}
        for s, v in waits:
            if s == engine or fl.get(s, 0) >= v:
                continue
            pruned[s] = max(pruned.get(s, 0), v)
            fl[s] = pruned[s]
            if self.max_waited.get(s, 0) < v:
                self.max_waited[s] = v
        if pruned:
            self.groups.append((engine, pruned, [], [], False))

    def queue_totals(self, base):
        return [(s, v) for s, v in self.sem_count.items()
                if s.startswith(base) and s[len(base):].isdigit()]

    def mark(self):
        return len(self.groups)

    def replay(self, engine, eng, sems, lo, hi):
        for (e, waits, ops, incs, mode) in self.groups[lo:hi]:
            if e != engine:
                continue
            for s, v in sorted(waits.items()):
                eng.wait_ge(sems[s], v)
            if not ops:
                continue
            if mode == "dma":
                for fn, (s, d) in zip(ops, incs):
                    fn(eng).then_inc(sems[s], d)
            elif isinstance(mode, tuple):
                s = incs[0][0]
                start = mode[1]
                for k, fn in enumerate(ops):
                    if k > 0:
                        eng.wait_ge(sems[s], start + k)
                    fn(eng).then_inc(sems[s], 1)
            else:
                last = None
                for fn in ops:
                    last = fn(eng)
                s, d = incs[0]
                last.then_inc(sems[s], d)


def build_program(T=4096, NA=256, NF=512):
    nCA = T // NA
    nCB = nCA
    nCF = T // NF
    nc = bass.Bass("TRN2", target_bir_lowering=False, detect_race_conditions=False)

    dram = {}

    def din(name, shape, dt=F32):
        dram[name] = nc.dram_tensor(name, shape, dt, kind="ExternalInput").ap()
        return dram[name]

    def dint(name, shape, dt=F32):
        kind = "ExternalOutput" if name in DEBUG_OUTPUTS else "Internal"
        dram[name] = nc.dram_tensor(name, shape, dt, kind=kind).ap()
        return dram[name]

    xT = din("xT", [C, T])
    WqT = din("WqT", [C, C]); WkT = din("WkT", [C, C]); WvT = din("WvT", [C, C])
    WgT = din("WgT", [C, C]); WoT = din("WoT", [C, C])
    W0 = din("W0", [C, C]); ATd = din("AT", [8, C]); Bmd = din("Bm", [8, C])
    log_lr = din("log_lr", [C]); ln_g = din("ln_g", [C]); ln_b = din("ln_b", [C]); ls_g = din("ls_g", [C])
    clip_x = din("clip_x", [1]); clip_down = din("clip_down", [1])
    wg_r = din("wg_r", [I_FFN, C]); wu_r = din("wu_r", [I_FFN, C]); wd_r = din("wd_r", [C, I_FFN])
    ps_row_in = din("ps_row", [1, T])
    mask8 = din("mask8", [8])
    outT = nc.dram_tensor("outT", [C, T], F32, kind="ExternalOutput").ap()

    xa_sp = dint("xa_sp", [C, T])
    hh_sp = dint("hh_sp", [C, T], BF16); hl_sp = dint("hl_sp", [C, T], BF16)
    y0_sp = dint("y0_sp", [C, T]); gs_sp = dint("gs_sp", [C, T])
    x1_sp = dint("x1_sp", [C, T]); y_sp = dint("y_sp", [C, T])
    z_sp = dint("z_sp", [I_FFN, T])
    wqg_sp = dint("wqg_sp", [I_FFN, C], BF16)
    wqu_sp = dint("wqu_sp", [I_FFN, C], BF16)
    wqd_sp = dint("wqd_sp", [C, I_FFN], BF16)
    cc_gin = dint("cc_gin", [1024]); cc_gout = dint("cc_gout", [8, 1024])
    cc_m1in = dint("cc_m1in", [128]); cc_m1out = dint("cc_m1out", [1024])
    cc_m2in = dint("cc_m2in", [128]); cc_m2out = dint("cc_m2out", [1024])
    off_hbm = dint("off_hbm", [1024]); cg_hbm = dint("cg_hbm", [1024])

    sch = Sched()
    estack = ExitStack()
    sem_names = list(ENGINES) + ["cc"] + [f"{b}{i}" for b in ("qs", "qg", "qs2")
                                           for i in range(DMA_RR)]
    sems = {n: estack.enter_context(nc.semaphore(n)) for n in sem_names}

    def sb(name, shape, dt=F32):
        return nc.alloc_sbuf_tensor(name, shape, dt)

    lr_t = sb("lr_t", [128, CT])
    g_t = sb("g_t", [128, CT]); b_t = sb("b_t", [128, CT]); ls_t = sb("ls_t", [128, CT])
    off_t = sb("off_t", [128, CT])
    scg = sb("scg", [128, IT]); scu = sb("scu", [128, IT]); scd = sb("scd", [128, CT])
    sgq = sb("sgq", [128, IT]); suq = sb("suq", [128, IT]); sdq = sb("sdq", [128, CT])
    s1v = sb("s1v", [128, 1]); s2v = sb("s2v", [128, 1])
    s1p = sb("s1p", [128, 1]); s2p = sb("s2p", [128, 1])
    rmax = sb("rmax", [128, 16 * CT])
    rmax2 = sb("rmax2", [128, 8 * IT])

    ones1 = sb("ones1", [128, 1]); onesk1 = sb("onesk1", [1, 128])
    ginv_t = sb("ginv_t", [128, CT]); nb_t = sb("nb_t", [128, CT])
    bg_t = sb("bg_t", [128, CT]); ngate_t = sb("ngate_t", [128, CT])
    mask_t = sb("mask_t", [8, 1])
    sc1x1 = sb("sc1x1", [1, 8])
    clipt = sb("clipt", [1, 2])
    redt = sb("redt", [128, 1])
    wmax = sb("wmax", [128, 1])
    NB = 256
    NS = NB
    stv_buf = sb("stv", [1, 13 * NS])
    _SLOT = {"r1": 0, "mrn": 1, "r12": 2, "hc2": 3,
             "r1b": 0, "mrnb": 1, "psc": 2,
             "mu": 4, "var": 5, "te": 6, "te2": 7, "s0": 8, "s2": 9, "r0": 12,
             "r2": 10, "tmp1": 11, "tmp2": 12,
             "e2": 12, "a": 11, "b2": 11, "cc2": 11, "rte": 12, "v1r": 12}

    def SV(n):
        i = _SLOT[n]
        return stv_buf.ap()[:, i * NS:(i + 1) * NS]
    psb = [nc.alloc_psum_tensor(f"ps{i}", [128, 512], F32) for i in range(8)]
    RG = [list(range(8))]

    def hbm3(t, t0, N):
        return t.rearrange("(k p) t -> p k t", p=128)[:, :, t0:t0 + N]

    def sbuf3(buf, N, nt=CT):
        return buf.ap().rearrange("p (k n) -> p k n", k=nt)

    with ExitStack() as p1stack:
        _w = p1stack.enter_context
        wAh = _w(nc.sbuf_tensor("wAh", [128, 8 * C], BF16))
        wAl = _w(nc.sbuf_tensor("wAl", [128, 8 * C], BF16))
        wBh = _w(nc.sbuf_tensor("wBh", [128, 8 * C], BF16))
        wBl = _w(nc.sbuf_tensor("wBl", [128, 8 * C], BF16))
        wCh = _w(nc.sbuf_tensor("wCh", [128, 8 * C], BF16))
        wCl = _w(nc.sbuf_tensor("wCl", [128, 8 * C], BF16))
        p1a = ExitStack()
        _e = p1a.enter_context
        m1row = stv_buf
        x_b0 = _e(nc.sbuf_tensor("x_b0", [128, CT * NA], F32))
        x_b1 = _e(nc.sbuf_tensor("x_b1", [128, CT * NA], F32))
        h32_b = _e(nc.sbuf_tensor("h32_b", [128, CT * NA], F32))
        hh0 = _e(nc.sbuf_tensor("hh0", [128, CT * NA], BF16))
        hh1 = _e(nc.sbuf_tensor("hh1", [128, CT * NA], BF16))
        hl0 = _e(nc.sbuf_tensor("hl0", [128, CT * NA], BF16))
        hl1 = _e(nc.sbuf_tensor("hl1", [128, CT * NA], BF16))
        Kh_b = _e(nc.sbuf_tensor("Kh_b", [128, CT * NA], BF16))
        Kl_b = _e(nc.sbuf_tensor("Kl_b", [128, CT * NA], BF16))
        y0_b0 = _e(nc.sbuf_tensor("y0_b0", [128, CT * NA], F32))
        K_b = _e(nc.sbuf_tensor("K_b", [128, CT * NA], F32))
        G_b = K_b
        gse_b = _e(nc.sbuf_tensor("gse_b", [128, CT * (NA + 1)], F32))
        gse_c = _e(nc.sbuf_tensor("gse_c", [128, CT * (NA + 1)], F32))
        xsq_b = h32_b
        x_b = [x_b0, x_b1]
        hh_p = [hh0, hh1]
        hl_p = [hl0, hl1]
        y0_p = [y0_b0, y0_b0]
        gse_p = [gse_b, gse_c]
        scB = _e(nc.sbuf_tensor("scB", [128, NA], F32))
        wstr0 = _e(nc.sbuf_tensor("wstr0", [128, I_FFN], F32))
        wfst = wstr0
        wit = _e(nc.sbuf_tensor("wit", [128, I_FFN // 8], I32))
        wqb0 = _e(nc.sbuf_tensor("wqb0", [128, I_FFN // 8], BF16))
        wqb1 = _e(nc.sbuf_tensor("wqb1", [128, I_FFN // 8], BF16))
        winv = _e(nc.sbuf_tensor("winv", [128, 1], F32))

        class _V:
            def __init__(v, ap):
                v._ap = ap
            def ap(v):
                return v._ap
        gath = _V(gse_b.ap()[0:8, 0:1024])
        AT_s = _V(gse_b.ap()[0:8, 0:C])
        Bm_s = _V(h32_b.ap()[0:8, 0:C])
        wstr = [wstr0, wstr0]

        def xs(buf, m, N=NA, dt=4):
            return buf.ap()[:, m * N:(m + 1) * N]

        def wsl(w, kk, m):
            return w.ap()[:, kk * C + m * 128: kk * C + m * 128 + 128]

        g = sch.group("sync", writes=["const", "gath", "h32"], dma_sem="qs")
        for kk in range(CT):
            g.op(lambda e, kk=kk: e.dma_start(out=lr_t.ap()[:, kk:kk + 1], in_=log_lr[128 * kk:128 * kk + 128][:, None]))
            g.op(lambda e, kk=kk: e.dma_start(out=g_t.ap()[:, kk:kk + 1], in_=ln_g[128 * kk:128 * kk + 128][:, None]))
            g.op(lambda e, kk=kk: e.dma_start(out=b_t.ap()[:, kk:kk + 1], in_=ln_b[128 * kk:128 * kk + 128][:, None]))
            g.op(lambda e, kk=kk: e.dma_start(out=ls_t.ap()[:, kk:kk + 1], in_=ls_g[128 * kk:128 * kk + 128][:, None]))
        g.op(lambda e: e.dma_start(out=mask_t.ap(), in_=mask8[:, None]))
        g.op(lambda e: e.dma_start(out=clipt.ap()[0:1, 0:1], in_=clip_x[:, None]))
        g.op(lambda e: e.dma_start(out=clipt.ap()[0:1, 1:2], in_=clip_down[:, None]))
        g.op(lambda e: e.dma_start(out=AT_s.ap(), in_=ATd))
        g.op(lambda e: e.dma_start(out=Bm_s.ap(), in_=Bmd))
        g.done()
        g = sch.group("gpsimd", writes=["const2", "off", "rm", "rm2"])
        g.op(lambda e: e.memset(ones1.ap(), 1.0))
        g.op(lambda e: e.memset(onesk1.ap(), 1.0))
        g.op(lambda e: e.memset(off_t.ap(), 0.0))
        g.done()
        g = sch.group("scalar", reads=["const"], writes=["lr1"])
        g.op(lambda e: e.activation(lr_t.ap(), lr_t.ap(), AF.Exp))
        g.done()
        g = sch.group("vector", reads=["lr1", "const"], writes=["lr", "gder"])
        g.op(lambda e: e.tensor_scalar_min(lr_t.ap(), lr_t.ap(), 1.0))
        g.op(lambda e: e.reciprocal(ginv_t.ap(), g_t.ap()))
        g.op(lambda e: e.tensor_scalar_mul(nb_t.ap(), b_t.ap(), -1.0))
        g.op(lambda e: e.tensor_mul(bg_t.ap(), b_t.ap(), ginv_t.ap()))
        g.done()

        def load_split(src_hbm, dsth, dstl, hres, lres, negate=False, weff=False,
                       stage=None, stage_res=None, colscale=None, gacc=False):
            stg = stage if stage is not None else wfst
            sres = stage_res if stage_res is not None else ("wf0", "wf1")
            for kk in range(CT):
                rg = kk % 2
                r0, r1 = rg * C, rg * C + C
                g = sch.group("sync", writes=[sres[rg]], dma_sem="qs")
                g.op(lambda e, kk=kk, r0=r0, r1=r1, src_hbm=src_hbm: e.dma_start(
                    out=stg.ap()[:, r0:r1], in_=src_hbm[128 * kk:128 * kk + 128, :]))
                g.done()
                if colscale is not None:
                    g = sch.group("vector", reads=[sres[rg], "gder"], writes=[sres[rg]])
                    g.op(lambda e, kk=kk, r0=r0, r1=r1: e.tensor_scalar(
                        stg.ap()[:, r0:r1], stg.ap()[:, r0:r1],
                        colscale.ap()[:, kk:kk + 1], None, ALU.mult))
                    g.done()
                if gacc:
                    g = sch.group("tensor", reads=[sres[rg], "gder"], writes=["ps_cg"])
                    for half in range(2):
                        g.op(lambda e, kk=kk, r0=r0, half=half: e.matmul(
                            psb[6].ap()[32 * half:32 * half + 1, 0:512],
                            bg_t.ap()[:, kk:kk + 1],
                            stg.ap()[:, r0 + half * 512:r0 + half * 512 + 512],
                            start=(kk == 0), stop=(kk == CT - 1)))
                    g.done()
                if weff:
                    g = sch.group("tensor", reads=["gath", "h32"], writes=["psm0", "psm1"])
                    for half in range(2):
                        g.op(lambda e, kk=kk, half=half: e.matmul(
                            psb[half].ap()[:, 0:512], AT_s.ap()[:, kk * 128:kk * 128 + 128],
                            Bm_s.ap()[:, half * 512:half * 512 + 512], start=True, stop=True))
                    g.done()
                    g = sch.group("vector", reads=["psm0", "psm1", sres[rg]], writes=[sres[rg]])
                    for half in range(2):
                        g.op(lambda e, r0=r0, half=half: e.tensor_add(
                            stg.ap()[:, r0 + half * 512:r0 + half * 512 + 512],
                            stg.ap()[:, r0 + half * 512:r0 + half * 512 + 512],
                            psb[half].ap()[:, 0:512]))
                    g.done()
                if negate:
                    g = sch.group("scalar", reads=[sres[rg]], writes=[sres[rg]])
                    g.op(lambda e, r0=r0, r1=r1: e.mul(stg.ap()[:, r0:r1], stg.ap()[:, r0:r1], -1.0))
                    g.done()
                g = sch.group("scalar", reads=[sres[rg]], writes=[hres])
                g.op(lambda e, kk=kk, r0=r0, r1=r1, dsth=dsth: e.copy(
                    xs(dsth, kk, C), stg.ap()[:, r0:r1]))
                g.done()
                g = sch.group("gpsimd", reads=[sres[rg], hres], writes=[lres])
                g.op(lambda e, kk=kk, r0=r0, r1=r1, dsth=dsth, dstl=dstl: e.tensor_sub(
                    xs(dstl, kk, C), stg.ap()[:, r0:r1], xs(dsth, kk, C)))
                g.done()

        load_split(WkT, wAh, wAl, "wAh", "wAl")
        load_split(W0, wBh, wBl, "wBh", "wBl", weff=True)
        load_split(WvT, wCh, wCl, "wCh", "wCl", negate=True)
        g = sch.group("gpsimd", reads=["gath", "h32"], writes=["gse0", "gse1"])
        g.op(lambda e: e.memset(gse_b.ap(), 0.0))
        g.op(lambda e: e.memset(gse_c.ap(), 0.0))
        g.done()


        def quant_strip(w_in, wq_out, sc_tile, sc_res, s_i, cols, half=0):
            SEG = I_FFN // 8
            wqbs = [wqb0, wqb1]
            base = 1024 * half if cols <= 1024 else 0
            sres_q = [f"wf{half}"] if cols <= 1024 else ["wf0", "wf1"]
            g = sch.group("scalar", writes=sres_q, dma_sem="qs2")
            g.op(lambda e, s_i=s_i, base=base, cols=cols, w_in=w_in: e.dma_start(
                out=wstr0.ap()[:, base:base + cols], in_=w_in[128 * s_i:128 * s_i + 128, :]))
            g.done()
            segs = [(c0, min(SEG, cols - c0)) for c0 in range(0, cols, SEG)]
            for pi in range(0, len(segs), 2):
                pair = segs[pi:pi + 2]
                if pi == 0:
                    g = sch.group("vector", reads=sres_q,
                                  writes=[sc_res, "winv", "wmax", "wit"], serialize=True)
                    g.op(lambda e, base=base, cols=cols: e.tensor_reduce(
                        wmax.ap(), wstr0.ap()[:, base:base + cols], AX.X, ALU.max,
                        apply_absolute_value=True))
                    g.op(lambda e, s_i=s_i, sc_tile=sc_tile: e.tensor_scalar_mul(
                        sc_tile.ap()[:, s_i:s_i + 1], wmax.ap(), 1.0 / QMAX))
                    g.op(lambda e, s_i=s_i, sc_tile=sc_tile: e.reciprocal(
                        winv.ap(), sc_tile.ap()[:, s_i:s_i + 1]))
                    g.op(lambda e, base=base, cw=pair[0][1]: e.tensor_scalar(
                        wit.ap()[:, 0:cw], wstr0.ap()[:, base:base + cw], winv.ap(),
                        None, ALU.mult))
                    g.done()
                g = sch.group("vector", reads=sres_q + ["winv"],
                              writes=["wit", "wqb0", "wqb1"])
                for bi, (c0, cw) in enumerate(pair):
                    if not (pi == 0 and bi == 0):
                        g.op(lambda e, base=base, c0=c0, cw=cw: e.tensor_scalar(
                            wit.ap()[:, 0:cw], wstr0.ap()[:, base + c0:base + c0 + cw],
                            winv.ap(), None, ALU.mult))
                    g.op(lambda e, bi=bi, cw=cw, wqbs=wqbs: e.tensor_scalar(
                        wqbs[bi].ap()[:, 0:cw], wit.ap()[:, 0:cw], -128.0, 127.0,
                        ALU.max, ALU.min))
                g.done()
                for bi, (c0, cw) in enumerate(pair):
                    g = sch.group("gpsimd", reads=[f"wqb{bi}"], dma_sem="qg")
                    g.op(lambda e, s_i=s_i, c0=c0, cw=cw, bi=bi, wq_out=wq_out,
                         wqbs=wqbs: e.dma_start(
                        out=wq_out[128 * s_i:128 * s_i + 128, c0:c0 + cw],
                        in_=wqbs[bi].ap()[:, 0:cw]))
                    g.done()

        def strip_gen():
            n = 0
            for s_i in range(IT):
                yield (wg_r, wqg_sp, scg, "wscg", s_i, C, n % 2); n += 1
            for s_i in range(IT):
                yield (wu_r, wqu_sp, scu, "wscu", s_i, C, n % 2); n += 1
            for s_i in range(CT):
                yield (wd_r, wqd_sp, scd, "wscd", s_i, I_FFN, 0)

        strips = strip_gen()

        def pump_strips(n):
            for _ in range(n):
                try:
                    quant_strip(*next(strips))
                except StopIteration:
                    return

        def mm3(g, bank, wh, wl, rhit, rlot, m, N):
            passes = [(wh, rhit), (wh, rlot), (wl, rhit)]
            for pi, (ww, rr) in enumerate(passes):
                for kk in range(CT):
                    g.op(lambda e, ww=ww, rr=rr, m=m, kk=kk, bank=bank, N=N,
                         st=(pi == 0 and kk == 0), sp=(pi == 2 and kk == CT - 1): e.matmul(
                        psb[bank].ap()[:, 0:N], wsl(ww, kk, m), xs(rr, kk, N, dt=2),
                        start=st, stop=sp))

        def stats_mm(buf, res, N, xq_buf=None, xq_res="h32", xq_w=None):
            xqb = xq_buf if xq_buf is not None else xsq_b
            xqw = xq_w if xq_w is not None else NA
            g = sch.group("scalar", reads=[res], writes=[xq_res])
            for kk in range(CT):
                g.op(lambda e, kk=kk, buf=buf, N=N: e.activation(
                    xs(xqb, kk, xqw)[:, 0:N], xs(buf, kk, N), AF.Square))
            g.done()
            g = sch.group("tensor", reads=[res, xq_res, "const2"], writes=["ps_st"])
            for kk in range(CT):
                g.op(lambda e, kk=kk, buf=buf, N=N: e.matmul(
                    psb[2].ap()[0:1, 0:N], ones1.ap(), xs(buf, kk, N),
                    start=(kk == 0), stop=(kk == CT - 1)))
            for kk in range(CT):
                g.op(lambda e, kk=kk, N=N: e.matmul(
                    psb[2].ap()[32:33, 0:N], ones1.ap(), xs(xqb, kk, xqw)[:, 0:N],
                    start=(kk == 0), stop=(kk == CT - 1)))
            g.done()

        def stats_math(N, r1_slot, mrn_slot, second=True):
            V = lambda n: SV(n)[:, 0:N]
            g = sch.group("vector", reads=["ps_st"], writes=["stv"])
            g.op(lambda e: e.tensor_scalar_mul(V("mu"), psb[2].ap()[0:1, 0:N], 1.0 / C))
            g.op(lambda e: e.tensor_scalar_mul(V("e2"), psb[2].ap()[32:33, 0:N], 1.0 / C))
            g.op(lambda e: e.tensor_mul(V("a"), V("mu"), V("mu")))
            g.op(lambda e: e.tensor_sub(V("var"), V("e2"), V("a")))
            g.op(lambda e: e.tensor_scalar_add(V("te"), V("var"), EPS))
            if second:
                g.op(lambda e: e.reciprocal(V("rte"), V("te")))
                g.op(lambda e: e.tensor_mul(V("v1r"), V("var"), V("rte")))
                g.op(lambda e: e.tensor_scalar_add(V("te2"), V("v1r"), EPS))
            g.done()
            g = sch.group("scalar", reads=["stv"], writes=["stvs"])
            g.op(lambda e: e.activation(V("s0"), V("te"), AF.Sqrt))
            if second:
                g.op(lambda e: e.activation(V("s2"), V("te2"), AF.Sqrt))
            g.done()
            g = sch.group("vector", reads=["stvs"], writes=["stv"])

            def newton(s_in, t_in, r_out):
                g.op(lambda e, s_in=s_in: e.reciprocal(V("r0"), V(s_in)))
                for srcn, dstn in (("r0", r_out), (r_out, r_out)):
                    g.op(lambda e, srcn=srcn: e.tensor_mul(V("a"), V(srcn), V(srcn)))
                    g.op(lambda e, t_in=t_in: e.tensor_mul(V("b2"), V("a"), V(t_in)))
                    g.op(lambda e: e.tensor_scalar(V("cc2"), V("b2"), -0.5, 1.5, ALU.mult, ALU.add))
                    g.op(lambda e, srcn=srcn, dstn=dstn: e.tensor_mul(V(dstn), V(srcn), V("cc2")))

            newton("s0", "te", r1_slot)
            if second:
                newton("s2", "te2", "r2")
                g.op(lambda e, r1_slot=r1_slot: e.tensor_mul(V("r12"), V(r1_slot), V("r2")))
            g.op(lambda e, r1_slot=r1_slot, mrn_slot=mrn_slot: e.scalar_tensor_tensor(
                V(mrn_slot), V("mu"), -1.0, V(r1_slot), ALU.mult, ALU.mult))
            if second:
                g.op(lambda e, mrn_slot=mrn_slot: e.tensor_mul(V("hc2"), V(mrn_slot), V("r2")))
            g.done()

        def a_front(j):
            px = j % 2
            g = sch.group("sync", writes=[f"x{px}"], dma_sem="qs")
            g.op(lambda e, j=j, px=px: e.dma_start(out=sbuf3(x_b[px], NA),
                                                   in_=hbm3(xT, j * NA, NA)))
            g.done()
            stats_mm(x_b[px], f"x{px}", NA)

        def a_mid(j):
            stats_math(NA, "r1", "mrn", second=True)

        def a_bcast(j):
            g = sch.group("tensor", reads=["stv", "const2"], writes=["ps_bc"])
            for qq, sl in enumerate(("r12", "hc2")):
                g.op(lambda e, qq=qq, sl=sl: e.matmul(
                    psb[3].ap()[:, qq * NA:(qq + 1) * NA], onesk1.ap(),
                    SV(sl)[:, 0:NA], start=True, stop=True))
            g.done()

        def a_xah(j):
            px = j % 2
            g = sch.group("vector", reads=["ps_bc", f"x{px}", "const"],
                          writes=["h32"])
            for kk in range(CT):
                g.op(lambda e, kk=kk, px=px: e.tensor_mul(
                    xs(h32_b, kk), xs(x_b[px], kk), psb[3].ap()[:, 0:NA]))
            for kk in range(CT):
                g.op(lambda e, kk=kk: e.tensor_add(
                    xs(h32_b, kk), xs(h32_b, kk), psb[3].ap()[:, NA:2 * NA]))
            for kk in range(CT):
                g.op(lambda e, kk=kk: e.tensor_scalar(
                    xs(h32_b, kk), xs(h32_b, kk), g_t.ap()[:, kk:kk + 1],
                    b_t.ap()[:, kk:kk + 1], ALU.mult, ALU.add))
            g.done()
            g = sch.group("scalar", reads=["h32"], writes=[f"hh{px}"])
            g.op(lambda e, px=px: e.copy(hh_p[px].ap(), h32_b.ap()))
            g.done()
            g = sch.group("gpsimd", reads=["h32", f"hh{px}"], writes=[f"hl{px}"])
            g.op(lambda e, px=px: e.tensor_sub(hl_p[px].ap(), h32_b.ap(), hh_p[px].ap()))
            g.done()
            g = sch.group("gpsimd", reads=[f"hh{px}", f"hl{px}"], dma_sem="qg")
            g.op(lambda e, j=j, px=px: e.dma_start(out=hbm3(hh_sp, j * NA, NA),
                                                   in_=sbuf3(hh_p[px], NA)))
            g.op(lambda e, j=j, px=px: e.dma_start(out=hbm3(hl_sp, j * NA, NA),
                                                   in_=sbuf3(hl_p[px], NA)))
            g.done()

        def a_K(i):
            px = i % 2
            for m in range(CT):
                pm = f"psm{m % 2}"
                g = sch.group("tensor", reads=[f"hh{px}", f"hl{px}", "wAh", "wAl"], writes=[pm])
                mm3(g, m % 2, wAh, wAl, hh_p[px], hl_p[px], m, NA)
                g.done()
                g = sch.group("scalar", reads=[pm], writes=["K"])
                g.op(lambda e, m=m: e.copy(xs(K_b, m), psb[m % 2].ap()[:, 0:NA]))
                g.done()
            g = sch.group("scalar", reads=["K"], writes=["Kh"])
            g.op(lambda e: e.copy(Kh_b.ap(), K_b.ap()))
            g.done()
            g = sch.group("gpsimd", reads=["K", "Kh"], writes=["Kl"])
            g.op(lambda e: e.tensor_sub(Kl_b.ap(), K_b.ap(), Kh_b.ap()))
            g.done()

        def a_tail(i):
            px = i % 2
            for m in range(CT):
                pm = f"psm{m % 2}"
                g = sch.group("tensor", reads=["Kh", "Kl", "wBh", "wBl"], writes=[pm])
                mm3(g, m % 2, wBh, wBl, Kh_b, Kl_b, m, NA)
                g.done()
                g = sch.group("scalar", reads=[pm], writes=["y0A"])
                g.op(lambda e, m=m, px=px: e.copy(xs(y0_p[px], m), psb[m % 2].ap()[:, 0:NA]))
                g.done()
                g = sch.group("tensor", reads=[f"hh{px}", f"hl{px}", "wCh", "wCl", "y0A"],
                              writes=[pm])
                mm3(g, m % 2, wCh, wCl, hh_p[px], hl_p[px], m, NA)
                g.done()
                g = sch.group("vector", reads=[pm, "y0A"], writes=["K"])
                g.op(lambda e, m=m, px=px: e.tensor_add(scB.ap()[:, 0:NA], xs(y0_p[px], m),
                                                        psb[m % 2].ap()[:, 0:NA]))
                g.op(lambda e, m=m: e.tensor_mul(xs(G_b, m), xs(K_b, m), scB.ap()[:, 0:NA]))
                g.done()
            g = sch.group("gpsimd", reads=["y0A"], dma_sem="qg")
            g.op(lambda e, i=i, px=px: e.dma_start(out=hbm3(y0_sp, i * NA, NA),
                                                   in_=sbuf3(y0_p[px], NA)))
            g.done()
            pg = i % 2
            gcur, gprv = gse_p[pg], gse_p[1 - pg]
            g = sch.group("vector", reads=["K", f"gse{1 - pg}"], writes=[f"gse{pg}"])
            for m in range(CT):
                base = m * (NA + 1)
                g.op(lambda e, base=base, gcur=gcur, gprv=gprv: e.tensor_copy(
                    gcur.ap()[:, base:base + 1], gprv.ap()[:, base + NA:base + NA + 1]))
                g.op(lambda e, base=base, m=m, gcur=gcur, gprv=gprv: e.tensor_tensor_scan(
                    gcur.ap()[:, base + 1:base + 1 + NA], xs(G_b, m), xs(G_b, m),
                    gprv.ap()[:, base + NA:base + NA + 1], ALU.add, ALU.bypass))
            g.done()
            g = sch.group("gpsimd", reads=[f"gse{pg}"], dma_sem="qg")
            g.op(lambda e, i=i, gcur=gcur: e.dma_start(
                out=hbm3(gs_sp, i * NA, NA),
                in_=gcur.ap().rearrange("p (k n) -> p k n", k=CT)[:, :, 0:NA]))
            g.done()

        for s in range(nCA + 1):
            if s >= 1:
                a_K(s - 1)
            if s < nCA:
                a_front(s)
                a_mid(s)
            if s >= 1:
                a_tail(s - 1)
            if s < nCA:
                a_bcast(s)
                a_xah(s)
            pump_strips(4 if s > 0 else 0)
        pump_strips(64)

        pg_last = (nCA - 1) % 2
        glast = gse_b if pg_last == 0 else gse_c
        g = sch.group("gpsimd", reads=[f"gse{pg_last}"], dma_sem="qg")
        for m in range(CT):
            g.op(lambda e, m=m, glast=glast: e.dma_start(
                out=cc_gin[m * 128:m * 128 + 128][:, None],
                in_=glast.ap()[:, m * (NA + 1) + NA:m * (NA + 1) + NA + 1]))
        g.done()
        g = sch.group("gpsimd", writes=["ccg"], sem="cc", waits=sch.queue_totals("qg"))
        g.op(lambda e: e.collective_compute("AllGather", ALU.bypass, replica_groups=RG,
                                            ins=[cc_gin.opt()], outs=[cc_gout.opt()]))
        g.done()
        g = sch.group("sync", reads=["ccg"], writes=["gath", "gse0"], dma_sem="qs")
        g.op(lambda e: e.dma_start(out=gath.ap(), in_=cc_gout))
        g.done()
        for half in range(2):
            g = sch.group("tensor", reads=["gath", "const"], writes=["ps_bc"])
            g.op(lambda e, half=half: e.matmul(
                psb[3].ap()[0:1, 0:512], mask_t.ap(),
                gath.ap()[:, half * 512:half * 512 + 512], start=True, stop=True))
            g.done()
            g = sch.group("scalar", reads=["ps_bc"], writes=["offrow", "stv"])
            g.op(lambda e, half=half: e.copy(m1row.ap()[0:1, half * 512:half * 512 + 512],
                                             psb[3].ap()[0:1, 0:512]))
            g.done()
        g = sch.group("gpsimd", reads=["offrow"], dma_sem="qg")
        g.op(lambda e: e.dma_start(out=off_hbm[None, :], in_=m1row.ap()[0:1, 0:1024]))
        g.done()
        g = sch.group("sync", writes=["off"], dma_sem="qs", waits=sch.queue_totals("qg"))
        for m in range(CT):
            g.op(lambda e, m=m: e.dma_start(out=off_t.ap()[:, m:m + 1],
                                            in_=off_hbm[m * 128:m * 128 + 128][:, None]))
        g.done()

        snapAB = dict(sch.sem_count)
        for e in ENGINES:
            sch.wait_only(e, [(s, v) for s, v in snapAB.items()])
        p1a.close()

        p1b = ExitStack()
        _b = p1b.enter_context
        hhB = [_b(nc.sbuf_tensor(f"hhB{p}", [128, CT * NB], BF16)) for p in range(2)]
        hlB = [_b(nc.sbuf_tensor(f"hlB{p}", [128, CT * NB], BF16)) for p in range(2)]
        x1B = [_b(nc.sbuf_tensor(f"x1B{p}", [128, CT * NB], F32)) for p in range(2)]
        y0B = [_b(nc.sbuf_tensor(f"y0B{p}", [128, CT * NB], F32)) for p in range(2)]
        gseB = [_b(nc.sbuf_tensor(f"gseB{p}", [128, CT * (NB + 1)], F32)) for p in range(2)]
        xahB = _b(nc.sbuf_tensor("xahB", [128, CT * NB], BF16))
        xalB = _b(nc.sbuf_tensor("xalB", [128, CT * NB], BF16))
        h32B = _b(nc.sbuf_tensor("h32B", [128, CT * NB], F32))
        scAB = _b(nc.sbuf_tensor("scAB", [128, NB], F32))
        scBB = _b(nc.sbuf_tensor("scBB", [128, NB], F32))
        sresB = ("h32B0", "h32B1")

        load_split(WqT, wAh, wAl, "wAh", "wAl", stage=h32B, stage_res=sresB)
        load_split(WgT, wBh, wBl, "wBh", "wBl", stage=h32B, stage_res=sresB,
                   colscale=ginv_t, gacc=True)
        load_split(WoT, wCh, wCl, "wCh", "wCl", stage=h32B, stage_res=sresB)
        g = sch.group("scalar", reads=["ps_cg"], writes=["offrow", "stv"])
        for half in range(2):
            g.op(lambda e, half=half: e.mul(
                m1row.ap()[0:1, half * 512:half * 512 + 512],
                psb[6].ap()[32 * half:32 * half + 1, 0:512], -1.0))
        g.done()
        g = sch.group("gpsimd", reads=["offrow"], dma_sem="qg")
        g.op(lambda e: e.dma_start(out=cg_hbm[None, :], in_=m1row.ap()[0:1, 0:1024]))
        g.done()
        g = sch.group("sync", writes=["cgt"], dma_sem="qs", waits=sch.queue_totals("qg"))
        for m in range(CT):
            g.op(lambda e, m=m: e.dma_start(out=ngate_t.ap()[:, m:m + 1],
                                            in_=cg_hbm[m * 128:m * 128 + 128][:, None]))
        g.done()

        def b_loads(j):
            px = j % 2
            for dst, res_, srcp in [(hhB[px], f"hh{px}", hh_sp), (hlB[px], f"hl{px}", hl_sp),
                                    (y0B[px], f"y0{px}", y0_sp)]:
                g = sch.group("sync", writes=[res_], dma_sem="qs")
                g.op(lambda e, j=j, dst=dst, srcp=srcp: e.dma_start(
                    out=sbuf3(dst, NB), in_=hbm3(srcp, j * NB, NB)))
                g.done()
            g = sch.group("sync", writes=[f"gse{px}"], dma_sem="qs")
            g.op(lambda e, j=j, px=px: e.dma_start(
                out=gseB[px].ap().rearrange("p (k n) -> p k n", k=CT)[:, :, 0:NB],
                in_=hbm3(gs_sp, j * NB, NB)))
            g.done()
            psl = SV("psc") if px == 0 else SV("hc2")
            g = sch.group("sync", writes=[f"psc{px}"], dma_sem="qs")
            g.op(lambda e, j=j, psl=psl: e.dma_start(out=psl[:, 0:NB],
                                                     in_=ps_row_in[0:1, j * NB:j * NB + NB]))
            g.done()

        def b_main(i):
            px = i % 2
            psl = SV("psc") if px == 0 else SV("hc2")
            g = sch.group("tensor", reads=[f"psc{px}", "const2"], writes=["ps_ps"])
            g.op(lambda e, psl=psl: e.matmul(psb[7].ap()[:, 0:NB], onesk1.ap(), psl[:, 0:NB],
                                             start=True, stop=True))
            g.done()
            for m in range(CT):
                pm = f"psm{m % 2}"
                pgm = f"psg{m % 2}"
                g = sch.group("tensor", reads=[f"hh{px}", f"hl{px}", "wAh", "wAl"], writes=[pm])
                mm3(g, m % 2, wAh, wAl, hhB[px], hlB[px], m, NB)
                g.done()
                g = sch.group("tensor", reads=[f"hh{px}", f"hl{px}", "wBh", "wBl"], writes=[pgm])
                mm3(g, 4 + m % 2, wBh, wBl, hhB[px], hlB[px], m, NB)
                g.done()
                g = sch.group("scalar", reads=[pgm, "cgt"], writes=["h32B"])
                g.op(lambda e, m=m: e.activation(xs(h32B, m, NB), psb[4 + m % 2].ap()[:, 0:NB],
                                                 AF.Sigmoid, bias=ngate_t.ap()[:, m:m + 1]))
                g.done()
                g = sch.group("vector",
                              reads=[f"gse{px}", "off", "ps_ps", pm, f"y0{px}", "h32B", "lr"],
                              writes=["h32B"])
                base = m * (NB + 1)
                g.op(lambda e, base=base, m=m, px=px: e.scalar_tensor_tensor(
                    scAB.ap(), gseB[px].ap()[:, base:base + NB], off_t.ap()[:, m:m + 1],
                    psb[7].ap()[:, 0:NB], ALU.add, ALU.mult))
                g.op(lambda e, m=m: e.scalar_tensor_tensor(
                    scBB.ap(), psb[m % 2].ap()[:, 0:NB], lr_t.ap()[:, m:m + 1], scAB.ap(),
                    ALU.mult, ALU.mult))
                g.op(lambda e, m=m, px=px: e.tensor_sub(scAB.ap(), xs(y0B[px], m, NB),
                                                        scBB.ap()))
                g.op(lambda e, m=m: e.tensor_mul(xs(h32B, m, NB), scAB.ap(), xs(h32B, m, NB)))
                g.done()
            g = sch.group("scalar", reads=["h32B"], writes=["xah"])
            g.op(lambda e: e.copy(xahB.ap(), h32B.ap()))
            g.done()
            g = sch.group("gpsimd", reads=["h32B", "xah"], writes=["xal"])
            g.op(lambda e: e.tensor_sub(xalB.ap(), h32B.ap(), xahB.ap()))
            g.done()
            g = sch.group("vector", reads=[f"hh{px}", f"hl{px}", "gder", "const"],
                          writes=[f"x1{px}"])
            for kk in range(CT):
                g.op(lambda e, kk=kk, px=px: e.tensor_add(
                    scAB.ap(), xs(hhB[px], kk, NB), xs(hlB[px], kk, NB)))
                g.op(lambda e, kk=kk: e.tensor_scalar(
                    scAB.ap(), scAB.ap(), nb_t.ap()[:, kk:kk + 1],
                    ginv_t.ap()[:, kk:kk + 1], ALU.add, ALU.mult))
                g.op(lambda e, kk=kk, px=px: e.tensor_add(
                    xs(x1B[px], kk, NB), xs(x1B[px], kk, NB), scAB.ap()))
            g.done()
            for m in range(CT):
                pm = f"psm{m % 2}"
                g = sch.group("tensor", reads=["xah", "xal", "wCh", "wCl"], writes=[pm])
                mm3(g, m % 2, wCh, wCl, xahB, xalB, m, NB)
                g.done()
                g = sch.group("vector", reads=[pm, "const"], writes=[f"x1{px}"])
                g.op(lambda e, m=m, px=px: e.scalar_tensor_tensor(
                    xs(x1B[px], m, NB), psb[m % 2].ap()[:, 0:NB], ls_t.ap()[:, m:m + 1],
                    xs(x1B[px], m, NB), ALU.mult, ALU.add))
                g.done()
            g = sch.group("gpsimd", reads=[f"x1{px}"], dma_sem="qg")
            g.op(lambda e, i=i, px=px: e.dma_start(out=hbm3(x1_sp, i * NB, NB),
                                                   in_=sbuf3(x1B[px], NB)))
            g.done()
            stats_mm(x1B[px], f"x1{px}", NB, xq_buf=h32B, xq_res="h32B", xq_w=NB)
            stats_math(NB, "r1b", "mrnb", second=False)

        def b_y(i):
            px = i % 2
            g = sch.group("tensor", reads=["stv", "const2"], writes=["ps_bc"])
            g.op(lambda e: e.matmul(psb[3].ap()[:, 0:2 * NB], onesk1.ap(),
                                    stv_buf.ap()[:, 0:2 * NB], start=True, stop=True))
            g.done()
            g = sch.group("vector", reads=["ps_bc", f"x1{px}"], writes=["h32B", "rm"])
            for kk in range(CT):
                g.op(lambda e, kk=kk, px=px: e.tensor_mul(
                    xs(h32B, kk, NB), xs(x1B[px], kk, NB), psb[3].ap()[:, 0:NB]))
            for kk in range(CT):
                g.op(lambda e, kk=kk: e.tensor_add(xs(h32B, kk, NB), xs(h32B, kk, NB),
                                                   psb[3].ap()[:, NB:2 * NB]))
            for kk in range(CT):
                g.op(lambda e, kk=kk, i=i: e.tensor_reduce(
                    rmax.ap()[:, i * CT + kk:i * CT + kk + 1], xs(h32B, kk, NB), AX.X,
                    ALU.max, apply_absolute_value=True))
            g.done()
            g = sch.group("gpsimd", reads=["h32B"], dma_sem="qg")
            g.op(lambda e, i=i: e.dma_start(out=hbm3(y_sp, i * NB, NB), in_=sbuf3(h32B, NB)))
            g.done()

        def b_loadx(j):
            px = j % 2
            g = sch.group("sync", writes=[f"x1{px}"], dma_sem="qs")
            g.op(lambda e, j=j, px=px: e.dma_start(out=sbuf3(x1B[px], NB),
                                                   in_=hbm3(xT, j * NB, NB)))
            g.done()

        nCBB = T // NB
        b_loadx(0)
        for s in range(nCBB + 2):
            if s >= 2:
                b_y(s - 2)
            if s < nCBB:
                b_loads(s)
                if s >= 1:
                    b_loadx(s)
            if s >= 1 and s - 1 < nCBB:
                b_main(s - 1)

        part1 = sch.mark()
        with nc.Block() as blk:
            for e in ENGINES:
                def closure(eng, e=e):
                    sch.replay(e, eng, sems, 0, part1)
                getattr(blk, e)(closure)
        p1b.close()

    with ExitStack() as p2stack:
        _e2 = p2stack.enter_context
        wqgT = _e2(nc.sbuf_tensor("wqgT", [128, CT * I_FFN], BF16))
        wquT = _e2(nc.sbuf_tensor("wquT", [128, CT * I_FFN], BF16))
        wqdT = _e2(nc.sbuf_tensor("wqdT", [128, IT * C], BF16))
        y_bf0 = _e2(nc.sbuf_tensor("y_bf0", [128, CT * NF], F32))
        y_bf1 = y_bf0
        xq_b = _e2(nc.sbuf_tensor("xq_b", [128, CT * NF], BF16))
        itq = _e2(nc.sbuf_tensor("itq", [128, NF], I32))
        silu0 = _e2(nc.sbuf_tensor("silu0", [128, NF], F32))
        silu1 = _e2(nc.sbuf_tensor("silu1", [128, NF], F32))
        z_t0 = _e2(nc.sbuf_tensor("z_t0", [128, NF], F32))
        z_t1 = _e2(nc.sbuf_tensor("z_t1", [128, NF], F32))
        z_t2 = _e2(nc.sbuf_tensor("z_t2", [128, NF], F32))
        z_t3 = _e2(nc.sbuf_tensor("z_t3", [128, NF], F32))
        zq_t0 = _e2(nc.sbuf_tensor("zq_t0", [128, NF], BF16))
        zq_t1 = _e2(nc.sbuf_tensor("zq_t1", [128, NF], BF16))
        zq_t2 = _e2(nc.sbuf_tensor("zq_t2", [128, NF], BF16))
        zq_t3 = _e2(nc.sbuf_tensor("zq_t3", [128, NF], BF16))
        out_b = _e2(nc.sbuf_tensor("out_b", [128, CT * NF], F32))

        class _V2:
            def __init__(v, ap):
                v._ap = ap
            def ap(v):
                return v._ap
        m2row = _V2(out_b.ap()[0:1, 0:1024])
        dbgbufS = _e2(nc.sbuf_tensor("dbgbufS", [128, 64], F32))

        y_bufs = [y_bf0, y_bf1]
        silu_t = [silu0, silu1]
        z_t = [z_t0, z_t1]
        zD_t = [z_t0, z_t1, z_t2, z_t3]
        zqD_t = [zq_t0, zq_t1, zq_t2, zq_t3]

        snap = dict(sch.sem_count)
        for e in ENGINES:
            sch.wait_only(e, [(s, v) for s, v in snap.items()])

        g = sch.group("sync", writes=["wqg"], dma_sem="qs")
        for kk in range(CT):
            g.op(lambda e, kk=kk: e.dma_start_transpose(
                wqgT.ap()[:, kk * I_FFN:(kk + 1) * I_FFN], wqg_sp[:, 128 * kk:128 * kk + 128]))
        g.done()
        g = sch.group("sync", writes=["wqu"], dma_sem="qs")
        for kk in range(CT):
            g.op(lambda e, kk=kk: e.dma_start_transpose(
                wquT.ap()[:, kk * I_FFN:(kk + 1) * I_FFN], wqu_sp[:, 128 * kk:128 * kk + 128]))
        g.done()
        g = sch.group("sync", writes=["wqd"], dma_sem="qs")
        for kk in range(IT):
            g.op(lambda e, kk=kk: e.dma_start_transpose(
                wqdT.ap()[:, kk * C:(kk + 1) * C], wqd_sp[:, 128 * kk:128 * kk + 128]))
        g.done()

        g = sch.group("vector", reads=["rm"], writes=["redm"])
        g.op(lambda e: e.tensor_reduce(redt.ap(), rmax.ap()[:, 0:(T // NB) * CT], AX.X, ALU.max))
        g.done()
        g = sch.group("gpsimd", reads=["redm"], dma_sem="qg")
        g.op(lambda e: e.dma_start(out=cc_m1in[:, None], in_=redt.ap()))
        g.done()
        g = sch.group("gpsimd", writes=["ccm1"], sem="cc", waits=sch.queue_totals("qg"))
        g.op(lambda e: e.collective_compute("AllGather", ALU.bypass, replica_groups=RG,
                                            ins=[cc_m1in.opt()], outs=[cc_m1out.opt()]))
        g.done()
        g = sch.group("sync", reads=["ccm1"], writes=["m1row"], dma_sem="qs")
        g.op(lambda e: e.dma_start(out=m2row.ap(), in_=cc_m1out[None, :]))
        g.done()
        g = sch.group("scalar", reads=["const"], writes=["sig1"])
        g.op(lambda e: e.activation(sc1x1.ap()[0:1, 0:1], clipt.ap()[0:1, 0:1], AF.Sigmoid))
        g.done()
        g = sch.group("vector", reads=["m1row", "sig1"], writes=["s1s"], serialize=True)
        g.op(lambda e: e.tensor_reduce(sc1x1.ap()[0:1, 1:2], m2row.ap(), AX.X, ALU.max))
        g.op(lambda e: e.tensor_mul(sc1x1.ap()[0:1, 2:3], sc1x1.ap()[0:1, 1:2], sc1x1.ap()[0:1, 0:1]))
        g.op(lambda e: e.tensor_scalar_mul(sc1x1.ap()[0:1, 2:3], sc1x1.ap()[0:1, 2:3], 1.0 / QMAX))
        g.op(lambda e: e.reciprocal(sc1x1.ap()[0:1, 3:4], sc1x1.ap()[0:1, 2:3]))
        g.done()
        g = sch.group("tensor", reads=["s1s", "const2"], writes=["ps_bc"])
        g.op(lambda e: e.matmul(psb[3].ap()[:, 0:2], onesk1.ap(), sc1x1.ap()[0:1, 2:4],
                                start=True, stop=True))
        g.done()
        g = sch.group("scalar", reads=["ps_bc"], writes=["s1v"])
        g.op(lambda e: e.copy(s1p.ap(), psb[3].ap()[:, 0:1]))
        g.op(lambda e: e.copy(s1v.ap(), psb[3].ap()[:, 1:2]))
        g.done()
        g = sch.group("vector", reads=["s1v", "wscg", "wscu"], writes=["sq1"])
        g.op(lambda e: e.tensor_scalar(sgq.ap(), scg.ap(), s1p.ap(), None, ALU.mult))
        g.op(lambda e: e.tensor_scalar(suq.ap(), scu.ap(), s1p.ap(), None, ALU.mult))
        g.done()

        for i in range(nCF):
            t0 = i * NF
            p = i % 2
            ybr = "yb"
            g = sch.group("sync", writes=[ybr], dma_sem="qs")
            g.op(lambda e, t0=t0, p=p: e.dma_start(out=sbuf3(y_bufs[p], NF), in_=hbm3(y_sp, t0, NF)))
            g.done()
            g = sch.group("vector", reads=[ybr, "s1v"], writes=["xq"])
            for kk in range(CT):
                g.op(lambda e, kk=kk, p=p: e.tensor_scalar(
                    itq.ap(), y_bufs[p].ap()[:, kk * NF:(kk + 1) * NF], s1v.ap(), None, ALU.mult))
                g.op(lambda e, kk=kk: e.tensor_scalar(
                    xq_b.ap()[:, kk * NF:(kk + 1) * NF], itq.ap(), -128.0, 127.0, ALU.max, ALU.min))
            g.done()
            for kb in range(IT):
                pp = kb % 2
                g = sch.group("tensor", reads=["xq", "wqg"], writes=[f"psf{pp}"])
                for kk in range(CT):
                    g.op(lambda e, kb=kb, kk=kk, pp=pp: e.matmul(
                        psb[pp].ap()[:, 0:NF],
                        wqgT.ap()[:, kk * I_FFN + kb * 128: kk * I_FFN + kb * 128 + 128],
                        xq_b.ap()[:, kk * NF:(kk + 1) * NF], start=(kk == 0), stop=(kk == CT - 1)))
                g.done()
                g = sch.group("scalar", reads=[f"psf{pp}", "sq1"], writes=[f"sil{pp}"])
                g.op(lambda e, kb=kb, pp=pp: e.activation(
                    silu_t[pp].ap(), psb[pp].ap()[:, 0:NF], AF.Silu, scale=sgq.ap()[:, kb:kb + 1]))
                g.done()
                g = sch.group("tensor", reads=["xq", "wqu"], writes=[f"psf{2 + pp}"])
                for kk in range(CT):
                    g.op(lambda e, kb=kb, kk=kk, pp=pp: e.matmul(
                        psb[2 + pp].ap()[:, 0:NF],
                        wquT.ap()[:, kk * I_FFN + kb * 128: kk * I_FFN + kb * 128 + 128],
                        xq_b.ap()[:, kk * NF:(kk + 1) * NF], start=(kk == 0), stop=(kk == CT - 1)))
                g.done()
                g = sch.group("vector", reads=[f"psf{2 + pp}", f"sil{pp}", "sq1"],
                              writes=[f"zt{pp}", "rm2"])
                g.op(lambda e, kb=kb, pp=pp: e.scalar_tensor_tensor(
                    z_t[pp].ap(), silu_t[pp].ap(), suq.ap()[:, kb:kb + 1], psb[2 + pp].ap()[:, 0:NF],
                    ALU.mult, ALU.mult))
                g.op(lambda e, kb=kb, pp=pp, i=i: e.tensor_reduce(
                    rmax2.ap()[:, i * IT + kb:i * IT + kb + 1], z_t[pp].ap(), AX.X, ALU.max,
                    apply_absolute_value=True))
                g.done()
                g = sch.group("gpsimd", reads=[f"zt{pp}"], dma_sem="qg")
                g.op(lambda e, t0=t0, kb=kb, pp=pp: e.dma_start(
                    out=z_sp[kb * 128:kb * 128 + 128, t0:t0 + NF], in_=z_t[pp].ap()))
                g.done()

        g = sch.group("vector", reads=["rm2"], writes=["redm2"])
        g.op(lambda e, nCF=nCF: e.tensor_reduce(redt.ap(), rmax2.ap()[:, 0:nCF * IT], AX.X, ALU.max))
        g.done()
        g = sch.group("gpsimd", reads=["redm2"], dma_sem="qg")
        g.op(lambda e: e.dma_start(out=cc_m2in[:, None], in_=redt.ap()))
        g.done()
        g = sch.group("gpsimd", writes=["ccm2"], sem="cc", waits=sch.queue_totals("qg"))
        g.op(lambda e: e.collective_compute("AllGather", ALU.bypass, replica_groups=RG,
                                            ins=[cc_m2in.opt()], outs=[cc_m2out.opt()]))
        g.done()
        g = sch.group("sync", reads=["ccm2"], writes=["m2row"], dma_sem="qs")
        g.op(lambda e: e.dma_start(out=m2row.ap(), in_=cc_m2out[None, :]))
        g.done()
        g = sch.group("scalar", reads=["const"], writes=["sig2"])
        g.op(lambda e: e.activation(sc1x1.ap()[0:1, 4:5], clipt.ap()[0:1, 1:2], AF.Sigmoid))
        g.done()
        g = sch.group("vector", reads=["m2row", "sig2"], writes=["s2s"], serialize=True)
        g.op(lambda e: e.tensor_reduce(sc1x1.ap()[0:1, 5:6], m2row.ap(), AX.X, ALU.max))
        g.op(lambda e: e.tensor_mul(sc1x1.ap()[0:1, 6:7], sc1x1.ap()[0:1, 5:6], sc1x1.ap()[0:1, 4:5]))
        g.op(lambda e: e.tensor_scalar_mul(sc1x1.ap()[0:1, 6:7], sc1x1.ap()[0:1, 6:7], 1.0 / QMAX))
        g.op(lambda e: e.reciprocal(sc1x1.ap()[0:1, 7:8], sc1x1.ap()[0:1, 6:7]))
        g.done()
        g = sch.group("tensor", reads=["s2s", "const2"], writes=["ps_bc2"])
        g.op(lambda e: e.matmul(psb[6].ap()[:, 0:2], onesk1.ap(), sc1x1.ap()[0:1, 6:8],
                                start=True, stop=True))
        g.done()
        g = sch.group("scalar", reads=["ps_bc2"], writes=["s2v"])
        g.op(lambda e: e.copy(s2p.ap(), psb[6].ap()[:, 0:1]))
        g.op(lambda e: e.copy(s2v.ap(), psb[6].ap()[:, 1:2]))
        g.done()
        g = sch.group("vector", reads=["s2v", "wscd"], writes=["sqd"])
        g.op(lambda e: e.tensor_scalar(sdq.ap(), scd.ap(), s2p.ap(), None, ALU.mult))
        g.done()

        zg = 0
        for i in range(nCF):
            t0 = i * NF
            p = i % 2
            ybr = "yb"
            g = sch.group("sync", writes=[ybr], dma_sem="qs")
            g.op(lambda e, t0=t0, p=p: e.dma_start(out=sbuf3(y_bufs[p], NF), in_=hbm3(x1_sp, t0, NF)))
            g.done()
            for kb in range(IT):
                pp = zg % 4
                zg += 1
                g = sch.group("sync", writes=[f"zt{pp}"], dma_sem="qs")
                g.op(lambda e, t0=t0, kb=kb, pp=pp: e.dma_start(
                    out=zD_t[pp].ap(), in_=z_sp[kb * 128:kb * 128 + 128, t0:t0 + NF]))
                g.done()
                g = sch.group("vector", reads=[f"zt{pp}", "s2v"], writes=[f"zq{pp}"])
                g.op(lambda e, pp=pp: e.tensor_scalar(itq.ap(), zD_t[pp].ap(), s2v.ap(), None, ALU.mult))
                g.op(lambda e, pp=pp: e.tensor_scalar(zqD_t[pp].ap(), itq.ap(), -128.0, 127.0,
                                                      ALU.max, ALU.min))
                g.done()
                psw = ["psd"] if not (i == 0 and kb == 0) else [
                    "psd", "psf0", "psf1", "psf2", "psf3", "ps_bc2"]
                g = sch.group("tensor", reads=[f"zq{pp}", "wqd"], writes=psw)
                for mm in range(CT):
                    g.op(lambda e, kb=kb, mm=mm, pp=pp: e.matmul(
                        psb[mm].ap()[:, 0:NF],
                        wqdT.ap()[:, kb * C + mm * 128: kb * C + mm * 128 + 128],
                        zqD_t[pp].ap(), start=(kb == 0), stop=(kb == IT - 1)))
                g.done()
            g = sch.group("scalar", reads=["psd", "sqd"], writes=["outb"])
            for mm in range(CT):
                g.op(lambda e, mm=mm: e.activation(
                    out_b.ap()[:, mm * NF:(mm + 1) * NF], psb[mm].ap()[:, 0:NF],
                    AF.Copy, scale=sdq.ap()[:, mm:mm + 1]))
            g.done()
            g = sch.group("vector", reads=["outb", ybr], writes=["outb"])
            for kk in range(CT):
                g.op(lambda e, kk=kk, p=p: e.tensor_add(
                    out_b.ap()[:, kk * NF:(kk + 1) * NF], out_b.ap()[:, kk * NF:(kk + 1) * NF],
                    y_bufs[p].ap()[:, kk * NF:(kk + 1) * NF]))
            g.done()
            g = sch.group("sync", reads=["outb"], dma_sem="qs")
            g.op(lambda e, t0=t0: e.dma_start(out=hbm3(outT, t0, NF), in_=sbuf3(out_b, NF)))
            g.done()

        if DEBUG_SCALES:
            dbg_sc = dint("dbg_sc", [128, 64])
            g = sch.group("vector", reads=["wscg", "wscu", "wscd", "sq1", "sqd", "s1v", "s2v"],
                          writes=["dbgsc"])
            g.op(lambda e: e.tensor_copy(dbgbufS.ap()[:, 0:IT], scg.ap()))
            g.op(lambda e: e.tensor_copy(dbgbufS.ap()[:, 22:22 + IT], scu.ap()))
            g.op(lambda e: e.tensor_copy(dbgbufS.ap()[:, 44:44 + CT], scd.ap()))
            g.op(lambda e: e.tensor_copy(dbgbufS.ap()[:, 52:53], s1v.ap()))
            g.op(lambda e: e.tensor_copy(dbgbufS.ap()[:, 53:54], s2v.ap()))
            g.op(lambda e: e.tensor_copy(dbgbufS.ap()[:, 54:55], s1p.ap()))
            g.op(lambda e: e.tensor_copy(dbgbufS.ap()[:, 55:56], s2p.ap()))
            g.done()
            g = sch.group("gpsimd", reads=["dbgsc"], dma_sem="qg")
            g.op(lambda e: e.dma_start(out=dbg_sc, in_=dbgbufS.ap()))
            g.done()
        sch.wait_only("sync", sch.queue_totals("qs"))
        sch.wait_only("gpsimd", sch.queue_totals("qg"))

        part2 = sch.mark()
        with nc.Block() as blk2:
            for e in ENGINES:
                def closure2(eng, e=e):
                    sch.replay(e, eng, sems, part1, part2)
                getattr(blk2, e)(closure2)

    estack.close()
    return nc


_T = 4096


def _prep_core_inputs(core, inputs, T):
    b, q = core // 4, core % 4
    t0 = q * T
    d = {}
    d["xT"] = np.ascontiguousarray(inputs["L0_attn_x"][b, t0:t0 + T, :].T)
    d["WqT"] = np.ascontiguousarray(inputs["L0_attn_Wq"].T)
    d["WkT"] = np.ascontiguousarray(inputs["L0_attn_Wk"].T)
    d["WvT"] = np.ascontiguousarray(inputs["L0_attn_Wv"].T)
    d["WgT"] = np.ascontiguousarray(inputs["L0_attn_Wg"].T)
    d["WoT"] = np.ascontiguousarray(inputs["L0_attn_Wo"].T)
    d["W0"] = np.ascontiguousarray(inputs["L0_attn_W0"])
    d["AT"] = np.ascontiguousarray(inputs["L0_attn_A_init"].T)
    d["Bm"] = np.ascontiguousarray(inputs["L0_attn_B_init"])
    d["log_lr"] = inputs["L0_attn_log_lr"]
    d["ln_g"] = inputs["L0_attn_ln_g"]
    d["ln_b"] = inputs["L0_attn_ln_b"]
    d["ls_g"] = inputs["L0_attn_ls_g"]
    d["clip_x"] = inputs["L0_ffn_clip_x"]
    d["clip_down"] = inputs["L0_ffn_clip_down"]
    d["wg_r"] = inputs["L0_ffn_w_gate"]
    d["wu_r"] = inputs["L0_ffn_w_up"]
    d["wd_r"] = inputs["L0_ffn_w_down"]
    pos = np.arange(t0 + 1, t0 + T + 1, dtype=np.float64)
    d["ps_row"] = (1.0 / (1.0 + 0.1 * np.log(pos))).astype(np.float32)[None, :]
    m = np.zeros(8, dtype=np.float32)
    m[b * 4:b * 4 + q] = 1.0
    d["mask8"] = m
    return d


def kernel(**inputs):
    T = _T
    nc = build_program(T=T)
    in_maps = [_prep_core_inputs(c, inputs, T) for c in range(8)]
    res = run_bass_kernel_spmd(nc, in_maps, core_ids=list(range(8)))
    B, Tfull, _ = inputs["L0_attn_x"].shape
    out = np.empty((B, Tfull, C), dtype=np.float32)
    for c in range(8):
        b, q = c // 4, c % 4
        out[b, q * T:(q + 1) * T, :] = np.asarray(res.results[c]["outT"]).T
    return out


# version 16 (final)
# speedup vs baseline: 1.0510x; 1.0130x over previous
import numpy as np
import ml_dtypes
from contextlib import ExitStack

import concourse.bass as bass
import concourse.mybir as mybir
from concourse.bass_utils import run_bass_kernel_spmd

F32 = mybir.dt.float32
BF16 = mybir.dt.bfloat16
I32 = mybir.dt.int32
AF = mybir.ActivationFunctionType
ALU = mybir.AluOpType
AX = mybir.AxisListType

C = 1024
CT = 8
I_FFN = 2816
IT = 22
EPS = 1e-5
QMAX = 127.0

ENGINES = ("sync", "tensor", "vector", "scalar", "gpsimd")
DMA_RR = 4
DEBUG_OUTPUTS = ()
DEBUG_SCALES = False


class Sched:

    def __init__(self):
        self.groups = []
        self.sem_count = {}
        self.res = {}
        self.floor = {e: {} for e in ENGINES}
        self.max_waited = {}
        self.dma_rr = {}

    def _rec(self, name):
        return self.res.setdefault(name, {"w": None, "r": []})

    def _bump(self, sem, delta):
        self.sem_count[sem] = self.sem_count.get(sem, 0) + delta
        return self.sem_count[sem]

    def group(self, engine, reads=(), writes=(), sem=None, dma_sem=None, waits=(),
              serialize=False):
        sch = self
        want = {}
        if dma_sem is not None:
            idx = self.dma_rr.get(dma_sem, 0)
            self.dma_rr[dma_sem] = (idx + 1) % DMA_RR
            dma_sem = f"{dma_sem}{idx}"

        def add(sw):
            if sw is None:
                return
            s, v = sw
            if s == engine:
                return
            if want.get(s, 0) < v:
                want[s] = v

        for r in reads:
            add(self._rec(r)["w"])
        for w in writes:
            rec = self._rec(w)
            add(rec["w"])
            for sw in rec["r"]:
                add(sw)
        for sw in waits:
            add(sw)
        if dma_sem is not None:
            add((dma_sem, self.sem_count.get(dma_sem, 0)))
        elif sem and sem != engine:
            add((sem, self.sem_count.get(sem, 0)))
        for s, v in want.items():
            if self.max_waited.get(s, 0) < v:
                self.max_waited[s] = v
        fl = self.floor[engine]
        pruned = {s: v for s, v in want.items() if fl.get(s, 0) < v}
        for s, v in pruned.items():
            fl[s] = v

        class G:
            __slots__ = ("ops",)

            def __init__(g):
                g.ops = []

            def op(g, fn):
                g.ops.append(fn)
                return g

            def done(g):
                if dma_sem is not None:
                    incs = [(dma_sem, 16)] * len(g.ops)
                    val = sch._bump(dma_sem, 16 * len(g.ops))
                    mysem = dma_sem
                    mode = "dma"
                else:
                    s = sem or engine
                    n = len(g.ops) if serialize else 1
                    val = sch._bump(s, n)
                    incs = [(s, 1)] * n
                    mysem = s
                    mode = ("ser", val - n) if serialize else "eng"
                    if s == engine:
                        fl2 = sch.floor[engine]
                        fl2[s] = max(fl2.get(s, 0), val)
                sch.groups.append((engine, pruned, list(g.ops), incs, mode))
                for w in writes:
                    rec = sch._rec(w)
                    rec["w"] = (mysem, val)
                    rec["r"] = []
                for r in reads:
                    sch._rec(r)["r"].append((mysem, val))
                return (mysem, val)

        return G()

    def wait_only(self, engine, waits):
        fl = self.floor[engine]
        pruned = {}
        for s, v in waits:
            if s == engine or fl.get(s, 0) >= v:
                continue
            pruned[s] = max(pruned.get(s, 0), v)
            fl[s] = pruned[s]
            if self.max_waited.get(s, 0) < v:
                self.max_waited[s] = v
        if pruned:
            self.groups.append((engine, pruned, [], [], False))

    def queue_totals(self, base):
        return [(s, v) for s, v in self.sem_count.items()
                if s.startswith(base) and s[len(base):].isdigit()]

    def mark(self):
        return len(self.groups)

    def replay(self, engine, eng, sems, lo, hi):
        for (e, waits, ops, incs, mode) in self.groups[lo:hi]:
            if e != engine:
                continue
            for s, v in sorted(waits.items()):
                eng.wait_ge(sems[s], v)
            if not ops:
                continue
            if mode == "dma":
                for fn, (s, d) in zip(ops, incs):
                    fn(eng).then_inc(sems[s], d)
            elif isinstance(mode, tuple):
                s = incs[0][0]
                start = mode[1]
                for k, fn in enumerate(ops):
                    if k > 0:
                        eng.wait_ge(sems[s], start + k)
                    fn(eng).then_inc(sems[s], 1)
            else:
                last = None
                for fn in ops:
                    last = fn(eng)
                s, d = incs[0]
                last.then_inc(sems[s], d)


def build_program(T=4096, NA=256, NF=512):
    nCA = T // NA
    nCB = nCA
    nCF = T // NF
    nc = bass.Bass("TRN2", target_bir_lowering=False, detect_race_conditions=False)

    dram = {}

    def din(name, shape, dt=F32):
        dram[name] = nc.dram_tensor(name, shape, dt, kind="ExternalInput").ap()
        return dram[name]

    def dint(name, shape, dt=F32):
        kind = "ExternalOutput" if name in DEBUG_OUTPUTS else "Internal"
        dram[name] = nc.dram_tensor(name, shape, dt, kind=kind).ap()
        return dram[name]

    xT = din("xT", [C, T])
    WqT = din("WqT", [C, C]); WkT = din("WkT", [C, C]); WvT = din("WvT", [C, C])
    WgT = din("WgT", [C, C]); WoT = din("WoT", [C, C])
    W0 = din("W0", [C, C]); ATd = din("AT", [8, C]); Bmd = din("Bm", [8, C])
    log_lr = din("log_lr", [C]); ln_g = din("ln_g", [C]); ln_b = din("ln_b", [C]); ls_g = din("ls_g", [C])
    clip_x = din("clip_x", [1]); clip_down = din("clip_down", [1])
    wg_r = din("wg_r", [I_FFN, C]); wu_r = din("wu_r", [I_FFN, C]); wd_r = din("wd_r", [C, I_FFN])
    ps_row_in = din("ps_row", [1, T])
    mask8 = din("mask8", [8])
    outT = nc.dram_tensor("outT", [C, T], F32, kind="ExternalOutput").ap()

    xa_sp = dint("xa_sp", [C, T])
    hh_sp = dint("hh_sp", [C, T], BF16); hl_sp = dint("hl_sp", [C, T], BF16)
    y0_sp = dint("y0_sp", [C, T]); gs_sp = dint("gs_sp", [C, T])
    x1_sp = dint("x1_sp", [C, T]); y_sp = dint("y_sp", [C, T])
    z_sp = dint("z_sp", [I_FFN, T])
    wqg_sp = dint("wqg_sp", [I_FFN, C], BF16)
    wqu_sp = dint("wqu_sp", [I_FFN, C], BF16)
    wqd_sp = dint("wqd_sp", [C, I_FFN], BF16)
    cc_gin = dint("cc_gin", [1024]); cc_gout = dint("cc_gout", [8, 1024])
    cc_m1in = dint("cc_m1in", [128]); cc_m1out = dint("cc_m1out", [1024])
    cc_m2in = dint("cc_m2in", [128]); cc_m2out = dint("cc_m2out", [1024])
    off_hbm = dint("off_hbm", [1024]); cg_hbm = dint("cg_hbm", [1024])

    sch = Sched()
    estack = ExitStack()
    sem_names = list(ENGINES) + ["cc"] + [f"{b}{i}" for b in ("qs", "qg", "qs2")
                                           for i in range(DMA_RR)]
    sems = {n: estack.enter_context(nc.semaphore(n)) for n in sem_names}

    def sb(name, shape, dt=F32):
        return nc.alloc_sbuf_tensor(name, shape, dt)

    lr_t = sb("lr_t", [128, CT])
    g_t = sb("g_t", [128, CT]); b_t = sb("b_t", [128, CT]); ls_t = sb("ls_t", [128, CT])
    off_t = sb("off_t", [128, CT])
    scg = sb("scg", [128, IT]); scu = sb("scu", [128, IT]); scd = sb("scd", [128, CT])
    sgq = sb("sgq", [128, IT]); suq = sb("suq", [128, IT]); sdq = sb("sdq", [128, CT])
    s1v = sb("s1v", [128, 1]); s2v = sb("s2v", [128, 1])
    s1p = sb("s1p", [128, 1]); s2p = sb("s2p", [128, 1])
    rmax = sb("rmax", [128, 16 * CT])
    rmax2 = sb("rmax2", [128, 8 * IT])

    ones1 = sb("ones1", [128, 1]); onesk1 = sb("onesk1", [1, 128])
    ginv_t = sb("ginv_t", [128, CT]); nb_t = sb("nb_t", [128, CT])
    bg_t = sb("bg_t", [128, CT]); ngate_t = sb("ngate_t", [128, CT])
    mask_t = sb("mask_t", [8, 1])
    sc1x1 = sb("sc1x1", [1, 8])
    clipt = sb("clipt", [1, 2])
    redt = sb("redt", [128, 1])
    wmax = sb("wmax", [128, 1])
    NB = 256
    NS = NB
    stv_buf = sb("stv", [1, 13 * NS])
    _SLOT = {"r1": 0, "mrn": 1, "r12": 2, "hc2": 3,
             "r1b": 0, "mrnb": 1, "psc": 2,
             "mu": 4, "var": 5, "te": 6, "te2": 7, "s0": 8, "s2": 9, "r0": 12,
             "r2": 10, "tmp1": 11, "tmp2": 12,
             "e2": 12, "a": 11, "b2": 11, "cc2": 11, "rte": 12, "v1r": 12}

    def SV(n):
        i = _SLOT[n]
        return stv_buf.ap()[:, i * NS:(i + 1) * NS]
    psb = [nc.alloc_psum_tensor(f"ps{i}", [128, 512], F32) for i in range(8)]
    RG = [list(range(8))]

    def hbm3(t, t0, N):
        return t.rearrange("(k p) t -> p k t", p=128)[:, :, t0:t0 + N]

    def sbuf3(buf, N, nt=CT):
        return buf.ap().rearrange("p (k n) -> p k n", k=nt)

    with ExitStack() as p1stack:
        _w = p1stack.enter_context
        wAh = _w(nc.sbuf_tensor("wAh", [128, 8 * C], BF16))
        wAl = _w(nc.sbuf_tensor("wAl", [128, 8 * C], BF16))
        wBh = _w(nc.sbuf_tensor("wBh", [128, 8 * C], BF16))
        wBl = _w(nc.sbuf_tensor("wBl", [128, 8 * C], BF16))
        wCh = _w(nc.sbuf_tensor("wCh", [128, 8 * C], BF16))
        wCl = _w(nc.sbuf_tensor("wCl", [128, 8 * C], BF16))
        p1a = ExitStack()
        _e = p1a.enter_context
        m1row = stv_buf
        x_b0 = _e(nc.sbuf_tensor("x_b0", [128, CT * NA], F32))
        x_b1 = _e(nc.sbuf_tensor("x_b1", [128, CT * NA], F32))
        h32_b = _e(nc.sbuf_tensor("h32_b", [128, CT * NA], F32))
        hh0 = _e(nc.sbuf_tensor("hh0", [128, CT * NA], BF16))
        hh1 = _e(nc.sbuf_tensor("hh1", [128, CT * NA], BF16))
        hl0 = _e(nc.sbuf_tensor("hl0", [128, CT * NA], BF16))
        hl1 = _e(nc.sbuf_tensor("hl1", [128, CT * NA], BF16))
        Kh_b = _e(nc.sbuf_tensor("Kh_b", [128, CT * NA], BF16))
        Kl_b = _e(nc.sbuf_tensor("Kl_b", [128, CT * NA], BF16))
        y0_b0 = _e(nc.sbuf_tensor("y0_b0", [128, CT * NA], F32))
        K_b = _e(nc.sbuf_tensor("K_b", [128, CT * NA], F32))
        G_b = K_b
        gse_b = _e(nc.sbuf_tensor("gse_b", [128, CT * (NA + 1)], F32))
        gse_c = _e(nc.sbuf_tensor("gse_c", [128, CT * (NA + 1)], F32))
        xsq_b = h32_b
        x_b = [x_b0, x_b1]
        hh_p = [hh0, hh1]
        hl_p = [hl0, hl1]
        y0_p = [y0_b0, y0_b0]
        gse_p = [gse_b, gse_c]
        scB = _e(nc.sbuf_tensor("scB", [128, NA], F32))
        wstr0 = _e(nc.sbuf_tensor("wstr0", [128, I_FFN], F32))
        wfst = wstr0
        wit = _e(nc.sbuf_tensor("wit", [128, I_FFN // 8], I32))
        wqb0 = _e(nc.sbuf_tensor("wqb0", [128, I_FFN // 8], BF16))
        wqb1 = _e(nc.sbuf_tensor("wqb1", [128, I_FFN // 8], BF16))
        winv = _e(nc.sbuf_tensor("winv", [128, 1], F32))

        class _V:
            def __init__(v, ap):
                v._ap = ap
            def ap(v):
                return v._ap
        gath = _V(gse_b.ap()[0:8, 0:1024])
        AT_s = _V(gse_b.ap()[0:8, 0:C])
        Bm_s = _V(h32_b.ap()[0:8, 0:C])
        wstr = [wstr0, wstr0]

        def xs(buf, m, N=NA, dt=4):
            return buf.ap()[:, m * N:(m + 1) * N]

        def wsl(w, kk, m):
            return w.ap()[:, kk * C + m * 128: kk * C + m * 128 + 128]

        g = sch.group("sync", writes=["const", "gath", "h32"], dma_sem="qs")
        for kk in range(CT):
            g.op(lambda e, kk=kk: e.dma_start(out=lr_t.ap()[:, kk:kk + 1], in_=log_lr[128 * kk:128 * kk + 128][:, None]))
            g.op(lambda e, kk=kk: e.dma_start(out=g_t.ap()[:, kk:kk + 1], in_=ln_g[128 * kk:128 * kk + 128][:, None]))
            g.op(lambda e, kk=kk: e.dma_start(out=b_t.ap()[:, kk:kk + 1], in_=ln_b[128 * kk:128 * kk + 128][:, None]))
            g.op(lambda e, kk=kk: e.dma_start(out=ls_t.ap()[:, kk:kk + 1], in_=ls_g[128 * kk:128 * kk + 128][:, None]))
        g.op(lambda e: e.dma_start(out=mask_t.ap(), in_=mask8[:, None]))
        g.op(lambda e: e.dma_start(out=clipt.ap()[0:1, 0:1], in_=clip_x[:, None]))
        g.op(lambda e: e.dma_start(out=clipt.ap()[0:1, 1:2], in_=clip_down[:, None]))
        g.op(lambda e: e.dma_start(out=AT_s.ap(), in_=ATd))
        g.op(lambda e: e.dma_start(out=Bm_s.ap(), in_=Bmd))
        g.done()
        g = sch.group("gpsimd", writes=["const2", "off", "rm", "rm2"])
        g.op(lambda e: e.memset(ones1.ap(), 1.0))
        g.op(lambda e: e.memset(onesk1.ap(), 1.0))
        g.op(lambda e: e.memset(off_t.ap(), 0.0))
        g.done()
        g = sch.group("scalar", reads=["const"], writes=["lr1"])
        g.op(lambda e: e.activation(lr_t.ap(), lr_t.ap(), AF.Exp))
        g.done()
        g = sch.group("vector", reads=["lr1", "const"], writes=["lr", "gder"])
        g.op(lambda e: e.tensor_scalar_min(lr_t.ap(), lr_t.ap(), 1.0))
        g.op(lambda e: e.reciprocal(ginv_t.ap(), g_t.ap()))
        g.op(lambda e: e.tensor_scalar_mul(nb_t.ap(), b_t.ap(), -1.0))
        g.op(lambda e: e.tensor_mul(bg_t.ap(), b_t.ap(), ginv_t.ap()))
        g.done()

        def load_split(src_hbm, dsth, dstl, hres, lres, negate=False, weff=False,
                       stage=None, stage_res=None, colscale=None, gacc=False):
            stg = stage if stage is not None else wfst
            sres = stage_res if stage_res is not None else ("wf0", "wf1")
            for kk in range(CT):
                rg = kk % 2
                r0, r1 = rg * C, rg * C + C
                g = sch.group("sync", writes=[sres[rg]], dma_sem="qs")
                g.op(lambda e, kk=kk, r0=r0, r1=r1, src_hbm=src_hbm: e.dma_start(
                    out=stg.ap()[:, r0:r1], in_=src_hbm[128 * kk:128 * kk + 128, :]))
                g.done()
                if colscale is not None:
                    g = sch.group("vector", reads=[sres[rg], "gder"], writes=[sres[rg]])
                    g.op(lambda e, kk=kk, r0=r0, r1=r1: e.tensor_scalar(
                        stg.ap()[:, r0:r1], stg.ap()[:, r0:r1],
                        colscale.ap()[:, kk:kk + 1], None, ALU.mult))
                    g.done()
                if gacc:
                    g = sch.group("tensor", reads=[sres[rg], "gder"], writes=["ps_cg"])
                    for half in range(2):
                        g.op(lambda e, kk=kk, r0=r0, half=half: e.matmul(
                            psb[6].ap()[32 * half:32 * half + 1, 0:512],
                            bg_t.ap()[:, kk:kk + 1],
                            stg.ap()[:, r0 + half * 512:r0 + half * 512 + 512],
                            start=(kk == 0), stop=(kk == CT - 1)))
                    g.done()
                if weff:
                    g = sch.group("tensor", reads=["gath", "h32"], writes=["psm0", "psm1"])
                    for half in range(2):
                        g.op(lambda e, kk=kk, half=half: e.matmul(
                            psb[half].ap()[:, 0:512], AT_s.ap()[:, kk * 128:kk * 128 + 128],
                            Bm_s.ap()[:, half * 512:half * 512 + 512], start=True, stop=True))
                    g.done()
                    g = sch.group("vector", reads=["psm0", "psm1", sres[rg]], writes=[sres[rg]])
                    for half in range(2):
                        g.op(lambda e, r0=r0, half=half: e.tensor_add(
                            stg.ap()[:, r0 + half * 512:r0 + half * 512 + 512],
                            stg.ap()[:, r0 + half * 512:r0 + half * 512 + 512],
                            psb[half].ap()[:, 0:512]))
                    g.done()
                if negate:
                    g = sch.group("scalar", reads=[sres[rg]], writes=[sres[rg]])
                    g.op(lambda e, r0=r0, r1=r1: e.mul(stg.ap()[:, r0:r1], stg.ap()[:, r0:r1], -1.0))
                    g.done()
                g = sch.group("scalar", reads=[sres[rg]], writes=[hres])
                g.op(lambda e, kk=kk, r0=r0, r1=r1, dsth=dsth: e.copy(
                    xs(dsth, kk, C), stg.ap()[:, r0:r1]))
                g.done()
                g = sch.group("gpsimd", reads=[sres[rg], hres], writes=[lres])
                g.op(lambda e, kk=kk, r0=r0, r1=r1, dsth=dsth, dstl=dstl: e.tensor_sub(
                    xs(dstl, kk, C), stg.ap()[:, r0:r1], xs(dsth, kk, C)))
                g.done()

        load_split(WkT, wAh, wAl, "wAh", "wAl")
        load_split(W0, wBh, wBl, "wBh", "wBl", weff=True)
        load_split(WvT, wCh, wCl, "wCh", "wCl", negate=True)
        g = sch.group("gpsimd", reads=["gath", "h32"], writes=["gse0", "gse1"])
        g.op(lambda e: e.memset(gse_b.ap(), 0.0))
        g.op(lambda e: e.memset(gse_c.ap(), 0.0))
        g.done()


        def quant_strip(w_in, wq_out, sc_tile, sc_res, s_i, cols, half=0):
            SEG = I_FFN // 8
            wqbs = [wqb0, wqb1]
            base = 1024 * half if cols <= 1024 else 0
            sres_q = [f"wf{half}"] if cols <= 1024 else ["wf0", "wf1"]
            g = sch.group("sync", writes=sres_q, dma_sem="qs2")
            g.op(lambda e, s_i=s_i, base=base, cols=cols, w_in=w_in: e.dma_start(
                out=wstr0.ap()[:, base:base + cols], in_=w_in[128 * s_i:128 * s_i + 128, :]))
            g.done()
            segs = [(c0, min(SEG, cols - c0)) for c0 in range(0, cols, SEG)]
            for pi in range(0, len(segs), 2):
                pair = segs[pi:pi + 2]
                if pi == 0:
                    g = sch.group("vector", reads=sres_q,
                                  writes=[sc_res, "winv", "wmax", "wit"], serialize=True)
                    g.op(lambda e, base=base, cols=cols: e.tensor_reduce(
                        wmax.ap(), wstr0.ap()[:, base:base + cols], AX.X, ALU.max,
                        apply_absolute_value=True))
                    g.op(lambda e, s_i=s_i, sc_tile=sc_tile: e.tensor_scalar_mul(
                        sc_tile.ap()[:, s_i:s_i + 1], wmax.ap(), 1.0 / QMAX))
                    g.op(lambda e, s_i=s_i, sc_tile=sc_tile: e.reciprocal(
                        winv.ap(), sc_tile.ap()[:, s_i:s_i + 1]))
                    g.op(lambda e, base=base, cw=pair[0][1]: e.tensor_scalar(
                        wit.ap()[:, 0:cw], wstr0.ap()[:, base:base + cw], winv.ap(),
                        None, ALU.mult))
                    g.done()
                g = sch.group("vector", reads=sres_q + ["winv"],
                              writes=["wit", "wqb0", "wqb1"])
                for bi, (c0, cw) in enumerate(pair):
                    if not (pi == 0 and bi == 0):
                        g.op(lambda e, base=base, c0=c0, cw=cw: e.tensor_scalar(
                            wit.ap()[:, 0:cw], wstr0.ap()[:, base + c0:base + c0 + cw],
                            winv.ap(), None, ALU.mult))
                    g.op(lambda e, bi=bi, cw=cw, wqbs=wqbs: e.tensor_scalar(
                        wqbs[bi].ap()[:, 0:cw], wit.ap()[:, 0:cw], -128.0, 127.0,
                        ALU.max, ALU.min))
                g.done()
                for bi, (c0, cw) in enumerate(pair):
                    g = sch.group("gpsimd", reads=[f"wqb{bi}"], dma_sem="qg")
                    g.op(lambda e, s_i=s_i, c0=c0, cw=cw, bi=bi, wq_out=wq_out,
                         wqbs=wqbs: e.dma_start(
                        out=wq_out[128 * s_i:128 * s_i + 128, c0:c0 + cw],
                        in_=wqbs[bi].ap()[:, 0:cw]))
                    g.done()

        def strip_gen():
            n = 0
            for s_i in range(IT):
                yield (wg_r, wqg_sp, scg, "wscg", s_i, C, n % 2); n += 1
            for s_i in range(IT):
                yield (wu_r, wqu_sp, scu, "wscu", s_i, C, n % 2); n += 1
            for s_i in range(CT):
                yield (wd_r, wqd_sp, scd, "wscd", s_i, I_FFN, 0)

        strips = strip_gen()

        def pump_strips(n):
            for _ in range(n):
                try:
                    quant_strip(*next(strips))
                except StopIteration:
                    return

        def mm3(g, bank, wh, wl, rhit, rlot, m, N):
            passes = [(wh, rhit), (wh, rlot), (wl, rhit)]
            for pi, (ww, rr) in enumerate(passes):
                for kk in range(CT):
                    g.op(lambda e, ww=ww, rr=rr, m=m, kk=kk, bank=bank, N=N,
                         st=(pi == 0 and kk == 0), sp=(pi == 2 and kk == CT - 1): e.matmul(
                        psb[bank].ap()[:, 0:N], wsl(ww, kk, m), xs(rr, kk, N, dt=2),
                        start=st, stop=sp))

        def stats_mm(buf, res, N, xq_buf=None, xq_res="h32", xq_w=None):
            xqb = xq_buf if xq_buf is not None else xsq_b
            xqw = xq_w if xq_w is not None else NA
            g = sch.group("scalar", reads=[res], writes=[xq_res])
            for kk in range(CT):
                g.op(lambda e, kk=kk, buf=buf, N=N: e.activation(
                    xs(xqb, kk, xqw)[:, 0:N], xs(buf, kk, N), AF.Square))
            g.done()
            g = sch.group("tensor", reads=[res, xq_res, "const2"], writes=["ps_st"])
            for kk in range(CT):
                g.op(lambda e, kk=kk, buf=buf, N=N: e.matmul(
                    psb[2].ap()[0:1, 0:N], ones1.ap(), xs(buf, kk, N),
                    start=(kk == 0), stop=(kk == CT - 1)))
            for kk in range(CT):
                g.op(lambda e, kk=kk, N=N: e.matmul(
                    psb[2].ap()[32:33, 0:N], ones1.ap(), xs(xqb, kk, xqw)[:, 0:N],
                    start=(kk == 0), stop=(kk == CT - 1)))
            g.done()

        def stats_math(N, r1_slot, mrn_slot, second=True):
            V = lambda n: SV(n)[:, 0:N]
            g = sch.group("vector", reads=["ps_st"], writes=["stv"])
            g.op(lambda e: e.tensor_scalar_mul(V("mu"), psb[2].ap()[0:1, 0:N], 1.0 / C))
            g.op(lambda e: e.tensor_scalar_mul(V("e2"), psb[2].ap()[32:33, 0:N], 1.0 / C))
            g.op(lambda e: e.tensor_mul(V("a"), V("mu"), V("mu")))
            g.op(lambda e: e.tensor_sub(V("var"), V("e2"), V("a")))
            g.op(lambda e: e.tensor_scalar_add(V("te"), V("var"), EPS))
            if second:
                g.op(lambda e: e.reciprocal(V("rte"), V("te")))
                g.op(lambda e: e.tensor_mul(V("v1r"), V("var"), V("rte")))
                g.op(lambda e: e.tensor_scalar_add(V("te2"), V("v1r"), EPS))
            g.done()
            g = sch.group("scalar", reads=["stv"], writes=["stvs"])
            g.op(lambda e: e.activation(V("s0"), V("te"), AF.Sqrt))
            if second:
                g.op(lambda e: e.activation(V("s2"), V("te2"), AF.Sqrt))
            g.done()
            g = sch.group("vector", reads=["stvs"], writes=["stv"])

            def newton(s_in, t_in, r_out):
                g.op(lambda e, s_in=s_in: e.reciprocal(V("r0"), V(s_in)))
                for srcn, dstn in (("r0", r_out), (r_out, r_out)):
                    g.op(lambda e, srcn=srcn: e.tensor_mul(V("a"), V(srcn), V(srcn)))
                    g.op(lambda e, t_in=t_in: e.tensor_mul(V("b2"), V("a"), V(t_in)))
                    g.op(lambda e: e.tensor_scalar(V("cc2"), V("b2"), -0.5, 1.5, ALU.mult, ALU.add))
                    g.op(lambda e, srcn=srcn, dstn=dstn: e.tensor_mul(V(dstn), V(srcn), V("cc2")))

            newton("s0", "te", r1_slot)
            if second:
                newton("s2", "te2", "r2")
                g.op(lambda e, r1_slot=r1_slot: e.tensor_mul(V("r12"), V(r1_slot), V("r2")))
            g.op(lambda e, r1_slot=r1_slot, mrn_slot=mrn_slot: e.scalar_tensor_tensor(
                V(mrn_slot), V("mu"), -1.0, V(r1_slot), ALU.mult, ALU.mult))
            if second:
                g.op(lambda e, mrn_slot=mrn_slot: e.tensor_mul(V("hc2"), V(mrn_slot), V("r2")))
            g.done()

        def a_front(j):
            px = j % 2
            g = sch.group("sync", writes=[f"x{px}"], dma_sem="qs")
            g.op(lambda e, j=j, px=px: e.dma_start(out=sbuf3(x_b[px], NA),
                                                   in_=hbm3(xT, j * NA, NA)))
            g.done()
            stats_mm(x_b[px], f"x{px}", NA)

        def a_mid(j):
            stats_math(NA, "r1", "mrn", second=True)

        def a_bcast(j):
            g = sch.group("tensor", reads=["stv", "const2"], writes=["ps_bc"])
            for qq, sl in enumerate(("r12", "hc2")):
                g.op(lambda e, qq=qq, sl=sl: e.matmul(
                    psb[3].ap()[:, qq * NA:(qq + 1) * NA], onesk1.ap(),
                    SV(sl)[:, 0:NA], start=True, stop=True))
            g.done()

        def a_xah(j):
            px = j % 2
            g = sch.group("vector", reads=["ps_bc", f"x{px}", "const"],
                          writes=["h32"])
            for kk in range(CT):
                g.op(lambda e, kk=kk, px=px: e.tensor_mul(
                    xs(h32_b, kk), xs(x_b[px], kk), psb[3].ap()[:, 0:NA]))
            for kk in range(CT):
                g.op(lambda e, kk=kk: e.tensor_add(
                    xs(h32_b, kk), xs(h32_b, kk), psb[3].ap()[:, NA:2 * NA]))
            for kk in range(CT):
                g.op(lambda e, kk=kk: e.tensor_scalar(
                    xs(h32_b, kk), xs(h32_b, kk), g_t.ap()[:, kk:kk + 1],
                    b_t.ap()[:, kk:kk + 1], ALU.mult, ALU.add))
            g.done()
            g = sch.group("scalar", reads=["h32"], writes=[f"hh{px}"])
            g.op(lambda e, px=px: e.copy(hh_p[px].ap(), h32_b.ap()))
            g.done()
            g = sch.group("gpsimd", reads=["h32", f"hh{px}"], writes=[f"hl{px}"])
            g.op(lambda e, px=px: e.tensor_sub(hl_p[px].ap(), h32_b.ap(), hh_p[px].ap()))
            g.done()
            g = sch.group("gpsimd", reads=[f"hh{px}", f"hl{px}"], dma_sem="qg")
            g.op(lambda e, j=j, px=px: e.dma_start(out=hbm3(hh_sp, j * NA, NA),
                                                   in_=sbuf3(hh_p[px], NA)))
            g.op(lambda e, j=j, px=px: e.dma_start(out=hbm3(hl_sp, j * NA, NA),
                                                   in_=sbuf3(hl_p[px], NA)))
            g.done()

        def a_K(i):
            px = i % 2
            for m in range(CT):
                pm = f"psm{m % 2}"
                g = sch.group("tensor", reads=[f"hh{px}", f"hl{px}", "wAh", "wAl"], writes=[pm])
                mm3(g, m % 2, wAh, wAl, hh_p[px], hl_p[px], m, NA)
                g.done()
                g = sch.group("scalar", reads=[pm], writes=["K"])
                g.op(lambda e, m=m: e.copy(xs(K_b, m), psb[m % 2].ap()[:, 0:NA]))
                g.done()
            g = sch.group("scalar", reads=["K"], writes=["Kh"])
            g.op(lambda e: e.copy(Kh_b.ap(), K_b.ap()))
            g.done()
            g = sch.group("gpsimd", reads=["K", "Kh"], writes=["Kl"])
            g.op(lambda e: e.tensor_sub(Kl_b.ap(), K_b.ap(), Kh_b.ap()))
            g.done()

        def a_tail(i):
            px = i % 2
            for m in range(CT):
                pm = f"psm{m % 2}"
                g = sch.group("tensor", reads=["Kh", "Kl", "wBh", "wBl"], writes=[pm])
                mm3(g, m % 2, wBh, wBl, Kh_b, Kl_b, m, NA)
                g.done()
                g = sch.group("scalar", reads=[pm], writes=["y0A"])
                g.op(lambda e, m=m, px=px: e.copy(xs(y0_p[px], m), psb[m % 2].ap()[:, 0:NA]))
                g.done()
                g = sch.group("tensor", reads=[f"hh{px}", f"hl{px}", "wCh", "wCl", "y0A"],
                              writes=[pm])
                mm3(g, m % 2, wCh, wCl, hh_p[px], hl_p[px], m, NA)
                g.done()
                g = sch.group("vector", reads=[pm, "y0A"], writes=["K"])
                g.op(lambda e, m=m, px=px: e.tensor_add(scB.ap()[:, 0:NA], xs(y0_p[px], m),
                                                        psb[m % 2].ap()[:, 0:NA]))
                g.op(lambda e, m=m: e.tensor_mul(xs(G_b, m), xs(K_b, m), scB.ap()[:, 0:NA]))
                g.done()
            g = sch.group("gpsimd", reads=["y0A"], dma_sem="qg")
            g.op(lambda e, i=i, px=px: e.dma_start(out=hbm3(y0_sp, i * NA, NA),
                                                   in_=sbuf3(y0_p[px], NA)))
            g.done()
            pg = i % 2
            gcur, gprv = gse_p[pg], gse_p[1 - pg]
            g = sch.group("vector", reads=["K", f"gse{1 - pg}"], writes=[f"gse{pg}"])
            for m in range(CT):
                base = m * (NA + 1)
                g.op(lambda e, base=base, gcur=gcur, gprv=gprv: e.tensor_copy(
                    gcur.ap()[:, base:base + 1], gprv.ap()[:, base + NA:base + NA + 1]))
                g.op(lambda e, base=base, m=m, gcur=gcur, gprv=gprv: e.tensor_tensor_scan(
                    gcur.ap()[:, base + 1:base + 1 + NA], xs(G_b, m), xs(G_b, m),
                    gprv.ap()[:, base + NA:base + NA + 1], ALU.add, ALU.bypass))
            g.done()
            g = sch.group("gpsimd", reads=[f"gse{pg}"], dma_sem="qg")
            g.op(lambda e, i=i, gcur=gcur: e.dma_start(
                out=hbm3(gs_sp, i * NA, NA),
                in_=gcur.ap().rearrange("p (k n) -> p k n", k=CT)[:, :, 0:NA]))
            g.done()

        for s in range(nCA + 1):
            if s >= 1:
                a_K(s - 1)
            if s < nCA:
                a_front(s)
                a_mid(s)
            if s >= 1:
                a_tail(s - 1)
            if s < nCA:
                a_bcast(s)
                a_xah(s)
            pump_strips(4 if s > 0 else 0)
        pump_strips(64)

        pg_last = (nCA - 1) % 2
        glast = gse_b if pg_last == 0 else gse_c
        g = sch.group("gpsimd", reads=[f"gse{pg_last}"], dma_sem="qg")
        for m in range(CT):
            g.op(lambda e, m=m, glast=glast: e.dma_start(
                out=cc_gin[m * 128:m * 128 + 128][:, None],
                in_=glast.ap()[:, m * (NA + 1) + NA:m * (NA + 1) + NA + 1]))
        g.done()
        g = sch.group("gpsimd", writes=["ccg"], sem="cc", waits=sch.queue_totals("qg"))
        g.op(lambda e: e.collective_compute("AllGather", ALU.bypass, replica_groups=RG,
                                            ins=[cc_gin.opt()], outs=[cc_gout.opt()]))
        g.done()
        g = sch.group("sync", reads=["ccg"], writes=["gath", "gse0"], dma_sem="qs")
        g.op(lambda e: e.dma_start(out=gath.ap(), in_=cc_gout))
        g.done()
        for half in range(2):
            g = sch.group("tensor", reads=["gath", "const"], writes=["ps_bc"])
            g.op(lambda e, half=half: e.matmul(
                psb[3].ap()[0:1, 0:512], mask_t.ap(),
                gath.ap()[:, half * 512:half * 512 + 512], start=True, stop=True))
            g.done()
            g = sch.group("scalar", reads=["ps_bc"], writes=["offrow", "stv"])
            g.op(lambda e, half=half: e.copy(m1row.ap()[0:1, half * 512:half * 512 + 512],
                                             psb[3].ap()[0:1, 0:512]))
            g.done()
        g = sch.group("gpsimd", reads=["offrow"], dma_sem="qg")
        g.op(lambda e: e.dma_start(out=off_hbm[None, :], in_=m1row.ap()[0:1, 0:1024]))
        g.done()
        g = sch.group("sync", writes=["off"], dma_sem="qs", waits=sch.queue_totals("qg"))
        for m in range(CT):
            g.op(lambda e, m=m: e.dma_start(out=off_t.ap()[:, m:m + 1],
                                            in_=off_hbm[m * 128:m * 128 + 128][:, None]))
        g.done()

        snapAB = dict(sch.sem_count)
        for e in ENGINES:
            sch.wait_only(e, [(s, v) for s, v in snapAB.items()])
        p1a.close()

        p1b = ExitStack()
        _b = p1b.enter_context
        hhB = [_b(nc.sbuf_tensor(f"hhB{p}", [128, CT * NB], BF16)) for p in range(2)]
        hlB = [_b(nc.sbuf_tensor(f"hlB{p}", [128, CT * NB], BF16)) for p in range(2)]
        x1B = [_b(nc.sbuf_tensor(f"x1B{p}", [128, CT * NB], F32)) for p in range(2)]
        y0B = [_b(nc.sbuf_tensor(f"y0B{p}", [128, CT * NB], F32)) for p in range(2)]
        gseB = [_b(nc.sbuf_tensor(f"gseB{p}", [128, CT * (NB + 1)], F32)) for p in range(2)]
        xahB = _b(nc.sbuf_tensor("xahB", [128, CT * NB], BF16))
        xalB = _b(nc.sbuf_tensor("xalB", [128, CT * NB], BF16))
        h32B = _b(nc.sbuf_tensor("h32B", [128, CT * NB], F32))
        scAB = _b(nc.sbuf_tensor("scAB", [128, NB], F32))
        scBB = _b(nc.sbuf_tensor("scBB", [128, NB], F32))
        sresB = ("h32B0", "h32B1")

        load_split(WqT, wAh, wAl, "wAh", "wAl", stage=h32B, stage_res=sresB)
        load_split(WgT, wBh, wBl, "wBh", "wBl", stage=h32B, stage_res=sresB,
                   colscale=ginv_t, gacc=True)
        load_split(WoT, wCh, wCl, "wCh", "wCl", stage=h32B, stage_res=sresB)
        g = sch.group("scalar", reads=["ps_cg"], writes=["offrow", "stv"])
        for half in range(2):
            g.op(lambda e, half=half: e.mul(
                m1row.ap()[0:1, half * 512:half * 512 + 512],
                psb[6].ap()[32 * half:32 * half + 1, 0:512], -1.0))
        g.done()
        g = sch.group("gpsimd", reads=["offrow"], dma_sem="qg")
        g.op(lambda e: e.dma_start(out=cg_hbm[None, :], in_=m1row.ap()[0:1, 0:1024]))
        g.done()
        g = sch.group("sync", writes=["cgt"], dma_sem="qs", waits=sch.queue_totals("qg"))
        for m in range(CT):
            g.op(lambda e, m=m: e.dma_start(out=ngate_t.ap()[:, m:m + 1],
                                            in_=cg_hbm[m * 128:m * 128 + 128][:, None]))
        g.done()

        def b_loads(j):
            px = j % 2
            for dst, res_, srcp in [(hhB[px], f"hh{px}", hh_sp), (hlB[px], f"hl{px}", hl_sp),
                                    (y0B[px], f"y0{px}", y0_sp)]:
                g = sch.group("sync", writes=[res_], dma_sem="qs")
                g.op(lambda e, j=j, dst=dst, srcp=srcp: e.dma_start(
                    out=sbuf3(dst, NB), in_=hbm3(srcp, j * NB, NB)))
                g.done()
            g = sch.group("sync", writes=[f"gse{px}"], dma_sem="qs")
            g.op(lambda e, j=j, px=px: e.dma_start(
                out=gseB[px].ap().rearrange("p (k n) -> p k n", k=CT)[:, :, 0:NB],
                in_=hbm3(gs_sp, j * NB, NB)))
            g.done()
            psl = SV("psc") if px == 0 else SV("hc2")
            g = sch.group("sync", writes=[f"psc{px}"], dma_sem="qs")
            g.op(lambda e, j=j, psl=psl: e.dma_start(out=psl[:, 0:NB],
                                                     in_=ps_row_in[0:1, j * NB:j * NB + NB]))
            g.done()

        def b_main(i):
            px = i % 2
            psl = SV("psc") if px == 0 else SV("hc2")
            g = sch.group("tensor", reads=[f"psc{px}", "const2"], writes=["ps_ps"])
            g.op(lambda e, psl=psl: e.matmul(psb[7].ap()[:, 0:NB], onesk1.ap(), psl[:, 0:NB],
                                             start=True, stop=True))
            g.done()
            for m in range(CT):
                pm = f"psm{m % 2}"
                pgm = f"psg{m % 2}"
                g = sch.group("tensor", reads=[f"hh{px}", f"hl{px}", "wAh", "wAl"], writes=[pm])
                mm3(g, m % 2, wAh, wAl, hhB[px], hlB[px], m, NB)
                g.done()
                g = sch.group("tensor", reads=[f"hh{px}", f"hl{px}", "wBh", "wBl"], writes=[pgm])
                mm3(g, 4 + m % 2, wBh, wBl, hhB[px], hlB[px], m, NB)
                g.done()
                g = sch.group("scalar", reads=[pgm, "cgt"], writes=["h32B"])
                g.op(lambda e, m=m: e.activation(xs(h32B, m, NB), psb[4 + m % 2].ap()[:, 0:NB],
                                                 AF.Sigmoid, bias=ngate_t.ap()[:, m:m + 1]))
                g.done()
                g = sch.group("vector",
                              reads=[f"gse{px}", "off", "ps_ps", pm, f"y0{px}", "h32B", "lr"],
                              writes=["h32B"])
                base = m * (NB + 1)
                g.op(lambda e, base=base, m=m, px=px: e.scalar_tensor_tensor(
                    scAB.ap(), gseB[px].ap()[:, base:base + NB], off_t.ap()[:, m:m + 1],
                    psb[7].ap()[:, 0:NB], ALU.add, ALU.mult))
                g.op(lambda e, m=m: e.scalar_tensor_tensor(
                    scBB.ap(), psb[m % 2].ap()[:, 0:NB], lr_t.ap()[:, m:m + 1], scAB.ap(),
                    ALU.mult, ALU.mult))
                g.op(lambda e, m=m, px=px: e.tensor_sub(scAB.ap(), xs(y0B[px], m, NB),
                                                        scBB.ap()))
                g.op(lambda e, m=m: e.tensor_mul(xs(h32B, m, NB), scAB.ap(), xs(h32B, m, NB)))
                g.done()
            g = sch.group("scalar", reads=["h32B"], writes=["xah"])
            g.op(lambda e: e.copy(xahB.ap(), h32B.ap()))
            g.done()
            g = sch.group("gpsimd", reads=["h32B", "xah"], writes=["xal"])
            g.op(lambda e: e.tensor_sub(xalB.ap(), h32B.ap(), xahB.ap()))
            g.done()
            g = sch.group("vector", reads=[f"hh{px}", f"hl{px}", "gder", "const"],
                          writes=[f"x1{px}"])
            for kk in range(CT):
                g.op(lambda e, kk=kk, px=px: e.tensor_add(
                    scAB.ap(), xs(hhB[px], kk, NB), xs(hlB[px], kk, NB)))
                g.op(lambda e, kk=kk: e.tensor_scalar(
                    scAB.ap(), scAB.ap(), nb_t.ap()[:, kk:kk + 1],
                    ginv_t.ap()[:, kk:kk + 1], ALU.add, ALU.mult))
                g.op(lambda e, kk=kk, px=px: e.tensor_add(
                    xs(x1B[px], kk, NB), xs(x1B[px], kk, NB), scAB.ap()))
            g.done()
            for m in range(CT):
                pm = f"psm{m % 2}"
                g = sch.group("tensor", reads=["xah", "xal", "wCh", "wCl"], writes=[pm])
                mm3(g, m % 2, wCh, wCl, xahB, xalB, m, NB)
                g.done()
                g = sch.group("vector", reads=[pm, "const"], writes=[f"x1{px}"])
                g.op(lambda e, m=m, px=px: e.scalar_tensor_tensor(
                    xs(x1B[px], m, NB), psb[m % 2].ap()[:, 0:NB], ls_t.ap()[:, m:m + 1],
                    xs(x1B[px], m, NB), ALU.mult, ALU.add))
                g.done()
            g = sch.group("gpsimd", reads=[f"x1{px}"], dma_sem="qg")
            g.op(lambda e, i=i, px=px: e.dma_start(out=hbm3(x1_sp, i * NB, NB),
                                                   in_=sbuf3(x1B[px], NB)))
            g.done()
            stats_mm(x1B[px], f"x1{px}", NB, xq_buf=h32B, xq_res="h32B", xq_w=NB)
            stats_math(NB, "r1b", "mrnb", second=False)

        def b_y(i):
            px = i % 2
            g = sch.group("tensor", reads=["stv", "const2"], writes=["ps_bc"])
            g.op(lambda e: e.matmul(psb[3].ap()[:, 0:2 * NB], onesk1.ap(),
                                    stv_buf.ap()[:, 0:2 * NB], start=True, stop=True))
            g.done()
            g = sch.group("vector", reads=["ps_bc", f"x1{px}"], writes=["h32B", "rm"])
            for kk in range(CT):
                g.op(lambda e, kk=kk, px=px: e.tensor_mul(
                    xs(h32B, kk, NB), xs(x1B[px], kk, NB), psb[3].ap()[:, 0:NB]))
            for kk in range(CT):
                g.op(lambda e, kk=kk: e.tensor_add(xs(h32B, kk, NB), xs(h32B, kk, NB),
                                                   psb[3].ap()[:, NB:2 * NB]))
            for kk in range(CT):
                g.op(lambda e, kk=kk, i=i: e.tensor_reduce(
                    rmax.ap()[:, i * CT + kk:i * CT + kk + 1], xs(h32B, kk, NB), AX.X,
                    ALU.max, apply_absolute_value=True))
            g.done()
            g = sch.group("gpsimd", reads=["h32B"], dma_sem="qg")
            g.op(lambda e, i=i: e.dma_start(out=hbm3(y_sp, i * NB, NB), in_=sbuf3(h32B, NB)))
            g.done()

        def b_loadx(j):
            px = j % 2
            g = sch.group("sync", writes=[f"x1{px}"], dma_sem="qs")
            g.op(lambda e, j=j, px=px: e.dma_start(out=sbuf3(x1B[px], NB),
                                                   in_=hbm3(xT, j * NB, NB)))
            g.done()

        nCBB = T // NB
        b_loadx(0)
        for s in range(nCBB + 2):
            if s >= 2:
                b_y(s - 2)
            if s < nCBB:
                b_loads(s)
                if s >= 1:
                    b_loadx(s)
            if s >= 1 and s - 1 < nCBB:
                b_main(s - 1)

        part1 = sch.mark()
        with nc.Block() as blk:
            for e in ENGINES:
                def closure(eng, e=e):
                    sch.replay(e, eng, sems, 0, part1)
                getattr(blk, e)(closure)
        p1b.close()

    with ExitStack() as p2stack:
        _e2 = p2stack.enter_context
        wqgT = _e2(nc.sbuf_tensor("wqgT", [128, CT * I_FFN], BF16))
        wquT = _e2(nc.sbuf_tensor("wquT", [128, CT * I_FFN], BF16))
        wqdT = _e2(nc.sbuf_tensor("wqdT", [128, IT * C], BF16))
        y_bf0 = _e2(nc.sbuf_tensor("y_bf0", [128, CT * NF], F32))
        y_bf1 = y_bf0
        xq_b = _e2(nc.sbuf_tensor("xq_b", [128, CT * NF], BF16))
        itq = _e2(nc.sbuf_tensor("itq", [128, NF], I32))
        silu0 = _e2(nc.sbuf_tensor("silu0", [128, NF], F32))
        silu1 = _e2(nc.sbuf_tensor("silu1", [128, NF], F32))
        z_t0 = _e2(nc.sbuf_tensor("z_t0", [128, NF], F32))
        z_t1 = _e2(nc.sbuf_tensor("z_t1", [128, NF], F32))
        z_t2 = _e2(nc.sbuf_tensor("z_t2", [128, NF], F32))
        z_t3 = _e2(nc.sbuf_tensor("z_t3", [128, NF], F32))
        zq_t0 = _e2(nc.sbuf_tensor("zq_t0", [128, NF], BF16))
        zq_t1 = _e2(nc.sbuf_tensor("zq_t1", [128, NF], BF16))
        zq_t2 = _e2(nc.sbuf_tensor("zq_t2", [128, NF], BF16))
        zq_t3 = _e2(nc.sbuf_tensor("zq_t3", [128, NF], BF16))
        out_b = _e2(nc.sbuf_tensor("out_b", [128, CT * NF], F32))

        class _V2:
            def __init__(v, ap):
                v._ap = ap
            def ap(v):
                return v._ap
        m2row = _V2(out_b.ap()[0:1, 0:1024])
        dbgbufS = _e2(nc.sbuf_tensor("dbgbufS", [128, 64], F32))

        y_bufs = [y_bf0, y_bf1]
        silu_t = [silu0, silu1]
        z_t = [z_t0, z_t1]
        zD_t = [z_t0, z_t1, z_t2, z_t3]
        zqD_t = [zq_t0, zq_t1, zq_t2, zq_t3]

        snap = dict(sch.sem_count)
        for e in ENGINES:
            sch.wait_only(e, [(s, v) for s, v in snap.items()])

        g = sch.group("sync", writes=["wqg"], dma_sem="qs")
        for kk in range(CT):
            g.op(lambda e, kk=kk: e.dma_start_transpose(
                wqgT.ap()[:, kk * I_FFN:(kk + 1) * I_FFN], wqg_sp[:, 128 * kk:128 * kk + 128]))
        g.done()
        g = sch.group("sync", writes=["wqu"], dma_sem="qs")
        for kk in range(CT):
            g.op(lambda e, kk=kk: e.dma_start_transpose(
                wquT.ap()[:, kk * I_FFN:(kk + 1) * I_FFN], wqu_sp[:, 128 * kk:128 * kk + 128]))
        g.done()
        g = sch.group("sync", writes=["wqd"], dma_sem="qs")
        for kk in range(IT):
            g.op(lambda e, kk=kk: e.dma_start_transpose(
                wqdT.ap()[:, kk * C:(kk + 1) * C], wqd_sp[:, 128 * kk:128 * kk + 128]))
        g.done()

        g = sch.group("vector", reads=["rm"], writes=["redm"])
        g.op(lambda e: e.tensor_reduce(redt.ap(), rmax.ap()[:, 0:(T // NB) * CT], AX.X, ALU.max))
        g.done()
        g = sch.group("gpsimd", reads=["redm"], dma_sem="qg")
        g.op(lambda e: e.dma_start(out=cc_m1in[:, None], in_=redt.ap()))
        g.done()
        g = sch.group("gpsimd", writes=["ccm1"], sem="cc", waits=sch.queue_totals("qg"))
        g.op(lambda e: e.collective_compute("AllGather", ALU.bypass, replica_groups=RG,
                                            ins=[cc_m1in.opt()], outs=[cc_m1out.opt()]))
        g.done()
        g = sch.group("sync", reads=["ccm1"], writes=["m1row"], dma_sem="qs")
        g.op(lambda e: e.dma_start(out=m2row.ap(), in_=cc_m1out[None, :]))
        g.done()
        g = sch.group("scalar", reads=["const"], writes=["sig1"])
        g.op(lambda e: e.activation(sc1x1.ap()[0:1, 0:1], clipt.ap()[0:1, 0:1], AF.Sigmoid))
        g.done()
        g = sch.group("vector", reads=["m1row", "sig1"], writes=["s1s"], serialize=True)
        g.op(lambda e: e.tensor_reduce(sc1x1.ap()[0:1, 1:2], m2row.ap(), AX.X, ALU.max))
        g.op(lambda e: e.tensor_mul(sc1x1.ap()[0:1, 2:3], sc1x1.ap()[0:1, 1:2], sc1x1.ap()[0:1, 0:1]))
        g.op(lambda e: e.tensor_scalar_mul(sc1x1.ap()[0:1, 2:3], sc1x1.ap()[0:1, 2:3], 1.0 / QMAX))
        g.op(lambda e: e.reciprocal(sc1x1.ap()[0:1, 3:4], sc1x1.ap()[0:1, 2:3]))
        g.done()
        g = sch.group("tensor", reads=["s1s", "const2"], writes=["ps_bc"])
        g.op(lambda e: e.matmul(psb[3].ap()[:, 0:2], onesk1.ap(), sc1x1.ap()[0:1, 2:4],
                                start=True, stop=True))
        g.done()
        g = sch.group("scalar", reads=["ps_bc"], writes=["s1v"])
        g.op(lambda e: e.copy(s1p.ap(), psb[3].ap()[:, 0:1]))
        g.op(lambda e: e.copy(s1v.ap(), psb[3].ap()[:, 1:2]))
        g.done()
        g = sch.group("vector", reads=["s1v", "wscg", "wscu"], writes=["sq1"])
        g.op(lambda e: e.tensor_scalar(sgq.ap(), scg.ap(), s1p.ap(), None, ALU.mult))
        g.op(lambda e: e.tensor_scalar(suq.ap(), scu.ap(), s1p.ap(), None, ALU.mult))
        g.done()

        for i in range(nCF):
            t0 = i * NF
            p = i % 2
            ybr = "yb"
            g = sch.group("sync", writes=[ybr], dma_sem="qs")
            g.op(lambda e, t0=t0, p=p: e.dma_start(out=sbuf3(y_bufs[p], NF), in_=hbm3(y_sp, t0, NF)))
            g.done()
            g = sch.group("vector", reads=[ybr, "s1v"], writes=["xq"])
            for kk in range(CT):
                g.op(lambda e, kk=kk, p=p: e.tensor_scalar(
                    itq.ap(), y_bufs[p].ap()[:, kk * NF:(kk + 1) * NF], s1v.ap(), None, ALU.mult))
                g.op(lambda e, kk=kk: e.tensor_scalar(
                    xq_b.ap()[:, kk * NF:(kk + 1) * NF], itq.ap(), -128.0, 127.0, ALU.max, ALU.min))
            g.done()
            for kb in range(IT):
                pp = kb % 2
                g = sch.group("tensor", reads=["xq", "wqg"], writes=[f"psf{pp}"])
                for kk in range(CT):
                    g.op(lambda e, kb=kb, kk=kk, pp=pp: e.matmul(
                        psb[pp].ap()[:, 0:NF],
                        wqgT.ap()[:, kk * I_FFN + kb * 128: kk * I_FFN + kb * 128 + 128],
                        xq_b.ap()[:, kk * NF:(kk + 1) * NF], start=(kk == 0), stop=(kk == CT - 1)))
                g.done()
                g = sch.group("scalar", reads=[f"psf{pp}", "sq1"], writes=[f"sil{pp}"])
                g.op(lambda e, kb=kb, pp=pp: e.activation(
                    silu_t[pp].ap(), psb[pp].ap()[:, 0:NF], AF.Silu, scale=sgq.ap()[:, kb:kb + 1]))
                g.done()
                g = sch.group("tensor", reads=["xq", "wqu"], writes=[f"psf{2 + pp}"])
                for kk in range(CT):
                    g.op(lambda e, kb=kb, kk=kk, pp=pp: e.matmul(
                        psb[2 + pp].ap()[:, 0:NF],
                        wquT.ap()[:, kk * I_FFN + kb * 128: kk * I_FFN + kb * 128 + 128],
                        xq_b.ap()[:, kk * NF:(kk + 1) * NF], start=(kk == 0), stop=(kk == CT - 1)))
                g.done()
                g = sch.group("vector", reads=[f"psf{2 + pp}", f"sil{pp}", "sq1"],
                              writes=[f"zt{pp}", "rm2"])
                g.op(lambda e, kb=kb, pp=pp: e.scalar_tensor_tensor(
                    z_t[pp].ap(), silu_t[pp].ap(), suq.ap()[:, kb:kb + 1], psb[2 + pp].ap()[:, 0:NF],
                    ALU.mult, ALU.mult))
                g.op(lambda e, kb=kb, pp=pp, i=i: e.tensor_reduce(
                    rmax2.ap()[:, i * IT + kb:i * IT + kb + 1], z_t[pp].ap(), AX.X, ALU.max,
                    apply_absolute_value=True))
                g.done()
                g = sch.group("gpsimd", reads=[f"zt{pp}"], dma_sem="qg")
                g.op(lambda e, t0=t0, kb=kb, pp=pp: e.dma_start(
                    out=z_sp[kb * 128:kb * 128 + 128, t0:t0 + NF], in_=z_t[pp].ap()))
                g.done()

        g = sch.group("vector", reads=["rm2"], writes=["redm2"])
        g.op(lambda e, nCF=nCF: e.tensor_reduce(redt.ap(), rmax2.ap()[:, 0:nCF * IT], AX.X, ALU.max))
        g.done()
        g = sch.group("gpsimd", reads=["redm2"], dma_sem="qg")
        g.op(lambda e: e.dma_start(out=cc_m2in[:, None], in_=redt.ap()))
        g.done()
        g = sch.group("gpsimd", writes=["ccm2"], sem="cc", waits=sch.queue_totals("qg"))
        g.op(lambda e: e.collective_compute("AllGather", ALU.bypass, replica_groups=RG,
                                            ins=[cc_m2in.opt()], outs=[cc_m2out.opt()]))
        g.done()
        g = sch.group("sync", reads=["ccm2"], writes=["m2row"], dma_sem="qs")
        g.op(lambda e: e.dma_start(out=m2row.ap(), in_=cc_m2out[None, :]))
        g.done()
        g = sch.group("scalar", reads=["const"], writes=["sig2"])
        g.op(lambda e: e.activation(sc1x1.ap()[0:1, 4:5], clipt.ap()[0:1, 1:2], AF.Sigmoid))
        g.done()
        g = sch.group("vector", reads=["m2row", "sig2"], writes=["s2s"], serialize=True)
        g.op(lambda e: e.tensor_reduce(sc1x1.ap()[0:1, 5:6], m2row.ap(), AX.X, ALU.max))
        g.op(lambda e: e.tensor_mul(sc1x1.ap()[0:1, 6:7], sc1x1.ap()[0:1, 5:6], sc1x1.ap()[0:1, 4:5]))
        g.op(lambda e: e.tensor_scalar_mul(sc1x1.ap()[0:1, 6:7], sc1x1.ap()[0:1, 6:7], 1.0 / QMAX))
        g.op(lambda e: e.reciprocal(sc1x1.ap()[0:1, 7:8], sc1x1.ap()[0:1, 6:7]))
        g.done()
        g = sch.group("tensor", reads=["s2s", "const2"], writes=["ps_bc2"])
        g.op(lambda e: e.matmul(psb[6].ap()[:, 0:2], onesk1.ap(), sc1x1.ap()[0:1, 6:8],
                                start=True, stop=True))
        g.done()
        g = sch.group("scalar", reads=["ps_bc2"], writes=["s2v"])
        g.op(lambda e: e.copy(s2p.ap(), psb[6].ap()[:, 0:1]))
        g.op(lambda e: e.copy(s2v.ap(), psb[6].ap()[:, 1:2]))
        g.done()
        g = sch.group("vector", reads=["s2v", "wscd"], writes=["sqd"])
        g.op(lambda e: e.tensor_scalar(sdq.ap(), scd.ap(), s2p.ap(), None, ALU.mult))
        g.done()

        zg = 0
        for i in range(nCF):
            t0 = i * NF
            p = i % 2
            ybr = "yb"
            g = sch.group("sync", writes=[ybr], dma_sem="qs")
            g.op(lambda e, t0=t0, p=p: e.dma_start(out=sbuf3(y_bufs[p], NF), in_=hbm3(x1_sp, t0, NF)))
            g.done()
            for kb in range(IT):
                pp = zg % 4
                zg += 1
                g = sch.group("sync", writes=[f"zt{pp}"], dma_sem="qs")
                g.op(lambda e, t0=t0, kb=kb, pp=pp: e.dma_start(
                    out=zD_t[pp].ap(), in_=z_sp[kb * 128:kb * 128 + 128, t0:t0 + NF]))
                g.done()
                g = sch.group("vector", reads=[f"zt{pp}", "s2v"], writes=[f"zq{pp}"])
                g.op(lambda e, pp=pp: e.tensor_scalar(itq.ap(), zD_t[pp].ap(), s2v.ap(), None, ALU.mult))
                g.op(lambda e, pp=pp: e.tensor_scalar(zqD_t[pp].ap(), itq.ap(), -128.0, 127.0,
                                                      ALU.max, ALU.min))
                g.done()
                psw = ["psd"] if not (i == 0 and kb == 0) else [
                    "psd", "psf0", "psf1", "psf2", "psf3", "ps_bc2"]
                g = sch.group("tensor", reads=[f"zq{pp}", "wqd"], writes=psw)
                for mm in range(CT):
                    g.op(lambda e, kb=kb, mm=mm, pp=pp: e.matmul(
                        psb[mm].ap()[:, 0:NF],
                        wqdT.ap()[:, kb * C + mm * 128: kb * C + mm * 128 + 128],
                        zqD_t[pp].ap(), start=(kb == 0), stop=(kb == IT - 1)))
                g.done()
            g = sch.group("scalar", reads=["psd", "sqd"], writes=["outb"])
            for mm in range(CT):
                g.op(lambda e, mm=mm: e.activation(
                    out_b.ap()[:, mm * NF:(mm + 1) * NF], psb[mm].ap()[:, 0:NF],
                    AF.Copy, scale=sdq.ap()[:, mm:mm + 1]))
            g.done()
            g = sch.group("vector", reads=["outb", ybr], writes=["outb"])
            for kk in range(CT):
                g.op(lambda e, kk=kk, p=p: e.tensor_add(
                    out_b.ap()[:, kk * NF:(kk + 1) * NF], out_b.ap()[:, kk * NF:(kk + 1) * NF],
                    y_bufs[p].ap()[:, kk * NF:(kk + 1) * NF]))
            g.done()
            g = sch.group("sync", reads=["outb"], dma_sem="qs")
            g.op(lambda e, t0=t0: e.dma_start(out=hbm3(outT, t0, NF), in_=sbuf3(out_b, NF)))
            g.done()

        if DEBUG_SCALES:
            dbg_sc = dint("dbg_sc", [128, 64])
            g = sch.group("vector", reads=["wscg", "wscu", "wscd", "sq1", "sqd", "s1v", "s2v"],
                          writes=["dbgsc"])
            g.op(lambda e: e.tensor_copy(dbgbufS.ap()[:, 0:IT], scg.ap()))
            g.op(lambda e: e.tensor_copy(dbgbufS.ap()[:, 22:22 + IT], scu.ap()))
            g.op(lambda e: e.tensor_copy(dbgbufS.ap()[:, 44:44 + CT], scd.ap()))
            g.op(lambda e: e.tensor_copy(dbgbufS.ap()[:, 52:53], s1v.ap()))
            g.op(lambda e: e.tensor_copy(dbgbufS.ap()[:, 53:54], s2v.ap()))
            g.op(lambda e: e.tensor_copy(dbgbufS.ap()[:, 54:55], s1p.ap()))
            g.op(lambda e: e.tensor_copy(dbgbufS.ap()[:, 55:56], s2p.ap()))
            g.done()
            g = sch.group("gpsimd", reads=["dbgsc"], dma_sem="qg")
            g.op(lambda e: e.dma_start(out=dbg_sc, in_=dbgbufS.ap()))
            g.done()
        sch.wait_only("sync", sch.queue_totals("qs"))
        sch.wait_only("gpsimd", sch.queue_totals("qg"))

        part2 = sch.mark()
        with nc.Block() as blk2:
            for e in ENGINES:
                def closure2(eng, e=e):
                    sch.replay(e, eng, sems, part1, part2)
                getattr(blk2, e)(closure2)

    estack.close()
    return nc


_T = 4096


def _prep_core_inputs(core, inputs, T):
    b, q = core // 4, core % 4
    t0 = q * T
    d = {}
    d["xT"] = np.ascontiguousarray(inputs["L0_attn_x"][b, t0:t0 + T, :].T)
    d["WqT"] = np.ascontiguousarray(inputs["L0_attn_Wq"].T)
    d["WkT"] = np.ascontiguousarray(inputs["L0_attn_Wk"].T)
    d["WvT"] = np.ascontiguousarray(inputs["L0_attn_Wv"].T)
    d["WgT"] = np.ascontiguousarray(inputs["L0_attn_Wg"].T)
    d["WoT"] = np.ascontiguousarray(inputs["L0_attn_Wo"].T)
    d["W0"] = np.ascontiguousarray(inputs["L0_attn_W0"])
    d["AT"] = np.ascontiguousarray(inputs["L0_attn_A_init"].T)
    d["Bm"] = np.ascontiguousarray(inputs["L0_attn_B_init"])
    d["log_lr"] = inputs["L0_attn_log_lr"]
    d["ln_g"] = inputs["L0_attn_ln_g"]
    d["ln_b"] = inputs["L0_attn_ln_b"]
    d["ls_g"] = inputs["L0_attn_ls_g"]
    d["clip_x"] = inputs["L0_ffn_clip_x"]
    d["clip_down"] = inputs["L0_ffn_clip_down"]
    d["wg_r"] = inputs["L0_ffn_w_gate"]
    d["wu_r"] = inputs["L0_ffn_w_up"]
    d["wd_r"] = inputs["L0_ffn_w_down"]
    pos = np.arange(t0 + 1, t0 + T + 1, dtype=np.float64)
    d["ps_row"] = (1.0 / (1.0 + 0.1 * np.log(pos))).astype(np.float32)[None, :]
    m = np.zeros(8, dtype=np.float32)
    m[b * 4:b * 4 + q] = 1.0
    d["mask8"] = m
    return d


def kernel(**inputs):
    T = _T
    nc = build_program(T=T)
    in_maps = [_prep_core_inputs(c, inputs, T) for c in range(8)]
    res = run_bass_kernel_spmd(nc, in_maps, core_ids=list(range(8)))
    B, Tfull, _ = inputs["L0_attn_x"].shape
    out = np.empty((B, Tfull, C), dtype=np.float32)
    for c in range(8):
        b, q = c // 4, c % 4
        out[b, q * T:(q + 1) * T, :] = np.asarray(res.results[c]["outT"]).T
    return out


# version 18
# speedup vs baseline: 1.0642x; 1.0126x over previous
import numpy as np
import ml_dtypes
from contextlib import ExitStack

import concourse.bass as bass
import concourse.mybir as mybir
from concourse.bass_utils import run_bass_kernel_spmd

F32 = mybir.dt.float32
BF16 = mybir.dt.bfloat16
I32 = mybir.dt.int32
AF = mybir.ActivationFunctionType
ALU = mybir.AluOpType
AX = mybir.AxisListType

C = 1024
CT = 8
I_FFN = 2816
IT = 22
EPS = 1e-5
QMAX = 127.0

ENGINES = ("sync", "tensor", "vector", "scalar", "gpsimd")
DMA_RR = 4
DEBUG_OUTPUTS = ()
DEBUG_SCALES = False


class Sched:

    def __init__(self):
        self.groups = []
        self.sem_count = {}
        self.res = {}
        self.floor = {e: {} for e in ENGINES}
        self.max_waited = {}
        self.dma_rr = {}

    def _rec(self, name):
        return self.res.setdefault(name, {"w": None, "r": []})

    def _bump(self, sem, delta):
        self.sem_count[sem] = self.sem_count.get(sem, 0) + delta
        return self.sem_count[sem]

    def group(self, engine, reads=(), writes=(), sem=None, dma_sem=None, waits=(),
              serialize=False):
        sch = self
        want = {}
        if dma_sem is not None:
            idx = self.dma_rr.get(dma_sem, 0)
            self.dma_rr[dma_sem] = (idx + 1) % DMA_RR
            dma_sem = f"{dma_sem}{idx}"

        def add(sw):
            if sw is None:
                return
            s, v = sw
            if s == engine:
                return
            if want.get(s, 0) < v:
                want[s] = v

        for r in reads:
            add(self._rec(r)["w"])
        for w in writes:
            rec = self._rec(w)
            add(rec["w"])
            for sw in rec["r"]:
                add(sw)
        for sw in waits:
            add(sw)
        if dma_sem is not None:
            add((dma_sem, self.sem_count.get(dma_sem, 0)))
        elif sem and sem != engine:
            add((sem, self.sem_count.get(sem, 0)))
        for s, v in want.items():
            if self.max_waited.get(s, 0) < v:
                self.max_waited[s] = v
        fl = self.floor[engine]
        pruned = {s: v for s, v in want.items() if fl.get(s, 0) < v}
        for s, v in pruned.items():
            fl[s] = v

        class G:
            __slots__ = ("ops",)

            def __init__(g):
                g.ops = []

            def op(g, fn):
                g.ops.append(fn)
                return g

            def done(g):
                if dma_sem is not None:
                    incs = [(dma_sem, 16)] * len(g.ops)
                    val = sch._bump(dma_sem, 16 * len(g.ops))
                    mysem = dma_sem
                    mode = "dma"
                else:
                    s = sem or engine
                    n = len(g.ops) if serialize else 1
                    val = sch._bump(s, n)
                    incs = [(s, 1)] * n
                    mysem = s
                    mode = ("ser", val - n) if serialize else "eng"
                    if s == engine:
                        fl2 = sch.floor[engine]
                        fl2[s] = max(fl2.get(s, 0), val)
                sch.groups.append((engine, pruned, list(g.ops), incs, mode))
                for w in writes:
                    rec = sch._rec(w)
                    rec["w"] = (mysem, val)
                    rec["r"] = []
                for r in reads:
                    sch._rec(r)["r"].append((mysem, val))
                return (mysem, val)

        return G()

    def wait_only(self, engine, waits):
        fl = self.floor[engine]
        pruned = {}
        for s, v in waits:
            if s == engine or fl.get(s, 0) >= v:
                continue
            pruned[s] = max(pruned.get(s, 0), v)
            fl[s] = pruned[s]
            if self.max_waited.get(s, 0) < v:
                self.max_waited[s] = v
        if pruned:
            self.groups.append((engine, pruned, [], [], False))

    def queue_totals(self, base):
        return [(s, v) for s, v in self.sem_count.items()
                if s.startswith(base) and s[len(base):].isdigit()]

    def mark(self):
        return len(self.groups)

    def replay(self, engine, eng, sems, lo, hi):
        for (e, waits, ops, incs, mode) in self.groups[lo:hi]:
            if e != engine:
                continue
            for s, v in sorted(waits.items()):
                eng.wait_ge(sems[s], v)
            if not ops:
                continue
            if mode == "dma":
                for fn, (s, d) in zip(ops, incs):
                    fn(eng).then_inc(sems[s], d)
            elif isinstance(mode, tuple):
                s = incs[0][0]
                start = mode[1]
                for k, fn in enumerate(ops):
                    if k > 0:
                        eng.wait_ge(sems[s], start + k)
                    fn(eng).then_inc(sems[s], 1)
            else:
                last = None
                for fn in ops:
                    last = fn(eng)
                s, d = incs[0]
                last.then_inc(sems[s], d)


def build_program(T=4096, NA=256, NF=512):
    nCA = T // NA
    nCB = nCA
    nCF = T // NF
    nc = bass.Bass("TRN2", target_bir_lowering=False, detect_race_conditions=False)

    dram = {}

    def din(name, shape, dt=F32):
        dram[name] = nc.dram_tensor(name, shape, dt, kind="ExternalInput").ap()
        return dram[name]

    def dint(name, shape, dt=F32):
        kind = "ExternalOutput" if name in DEBUG_OUTPUTS else "Internal"
        dram[name] = nc.dram_tensor(name, shape, dt, kind=kind).ap()
        return dram[name]

    xT = din("xT", [C, T])
    WqT = din("WqT", [C, C]); WkT = din("WkT", [C, C]); WvT = din("WvT", [C, C])
    WgT = din("WgT", [C, C]); WoT = din("WoT", [C, C])
    W0 = din("W0", [C, C]); ATd = din("AT", [8, C]); Bmd = din("Bm", [8, C])
    log_lr = din("log_lr", [C]); ln_g = din("ln_g", [C]); ln_b = din("ln_b", [C]); ls_g = din("ls_g", [C])
    clip_x = din("clip_x", [1]); clip_down = din("clip_down", [1])
    wg_r = din("wg_r", [I_FFN, C]); wu_r = din("wu_r", [I_FFN, C]); wd_r = din("wd_r", [C, I_FFN])
    ps_row_in = din("ps_row", [1, T])
    mask8 = din("mask8", [8])
    outT = nc.dram_tensor("outT", [C, T], F32, kind="ExternalOutput").ap()

    xa_sp = dint("xa_sp", [C, T])
    hh_sp = dint("hh_sp", [C, T], BF16); hl_sp = dint("hl_sp", [C, T], BF16)
    y0_sp = dint("y0_sp", [C, T]); gs_sp = dint("gs_sp", [C, T])
    x1_sp = dint("x1_sp", [C, T]); y_sp = dint("y_sp", [C, T])
    z_sp = dint("z_sp", [I_FFN, T])
    wqg_sp = dint("wqg_sp", [I_FFN, C], BF16)
    wqu_sp = dint("wqu_sp", [I_FFN, C], BF16)
    wqd_sp = dint("wqd_sp", [C, I_FFN], BF16)
    cc_gin = dint("cc_gin", [1024]); cc_gout = dint("cc_gout", [8, 1024])
    cc_m1in = dint("cc_m1in", [128]); cc_m1out = dint("cc_m1out", [1024])
    cc_m2in = dint("cc_m2in", [128]); cc_m2out = dint("cc_m2out", [1024])
    off_hbm = dint("off_hbm", [1024]); cg_hbm = dint("cg_hbm", [1024])

    sch = Sched()
    estack = ExitStack()
    sem_names = list(ENGINES) + ["cc"] + [f"{b}{i}" for b in ("qs", "qg", "qs2")
                                           for i in range(DMA_RR)]
    sems = {n: estack.enter_context(nc.semaphore(n)) for n in sem_names}

    def sb(name, shape, dt=F32):
        return nc.alloc_sbuf_tensor(name, shape, dt)

    lr_t = sb("lr_t", [128, CT])
    g_t = sb("g_t", [128, CT]); b_t = sb("b_t", [128, CT]); ls_t = sb("ls_t", [128, CT])
    off_t = sb("off_t", [128, CT])
    scg = sb("scg", [128, IT]); scu = sb("scu", [128, IT]); scd = sb("scd", [128, CT])
    sgq = sb("sgq", [128, IT]); suq = sb("suq", [128, IT]); sdq = sb("sdq", [128, CT])
    s1v = sb("s1v", [128, 1]); s2v = sb("s2v", [128, 1])
    s1p = sb("s1p", [128, 1]); s2p = sb("s2p", [128, 1])
    rmax = sb("rmax", [128, 16 * CT])
    rmax2 = sb("rmax2", [128, 8 * IT])

    ones1 = sb("ones1", [128, 1]); onesk1 = sb("onesk1", [1, 128])
    ginv_t = sb("ginv_t", [128, CT]); nb_t = sb("nb_t", [128, CT])
    bg_t = sb("bg_t", [128, CT]); ngate_t = sb("ngate_t", [128, CT])
    mask_t = sb("mask_t", [8, 1])
    sc1x1 = sb("sc1x1", [1, 8])
    clipt = sb("clipt", [1, 2])
    redt = sb("redt", [128, 1])
    wmax = sb("wmax", [128, 1])
    NB = 256
    NS = NB
    stv_buf = sb("stv", [1, 13 * NS])
    _SLOT = {"r1": 0, "mrn": 1, "r12": 2, "hc2": 3,
             "r1b": 0, "mrnb": 1, "psc": 2,
             "mu": 4, "var": 5, "te": 6, "te2": 7, "s0": 8, "s2": 9, "r0": 12,
             "r2": 10, "tmp1": 11, "tmp2": 12,
             "e2": 12, "a": 11, "b2": 11, "cc2": 11, "rte": 12, "v1r": 12}

    def SV(n):
        i = _SLOT[n]
        return stv_buf.ap()[:, i * NS:(i + 1) * NS]
    psb = [nc.alloc_psum_tensor(f"ps{i}", [128, 512], F32) for i in range(8)]
    RG = [list(range(8))]

    def hbm3(t, t0, N):
        return t.rearrange("(k p) t -> p k t", p=128)[:, :, t0:t0 + N]

    def sbuf3(buf, N, nt=CT):
        return buf.ap().rearrange("p (k n) -> p k n", k=nt)

    with ExitStack() as p1stack:
        _w = p1stack.enter_context
        wAh = _w(nc.sbuf_tensor("wAh", [128, 8 * C], BF16))
        wAl = _w(nc.sbuf_tensor("wAl", [128, 8 * C], BF16))
        wBh = _w(nc.sbuf_tensor("wBh", [128, 8 * C], BF16))
        wBl = _w(nc.sbuf_tensor("wBl", [128, 8 * C], BF16))
        wCh = _w(nc.sbuf_tensor("wCh", [128, 8 * C], BF16))
        wCl = _w(nc.sbuf_tensor("wCl", [128, 8 * C], BF16))
        p1a = ExitStack()
        _e = p1a.enter_context
        m1row = stv_buf
        x_b0 = _e(nc.sbuf_tensor("x_b0", [128, CT * NA], F32))
        x_b1 = _e(nc.sbuf_tensor("x_b1", [128, CT * NA], F32))
        h32_b = _e(nc.sbuf_tensor("h32_b", [128, CT * NA], F32))
        hh0 = _e(nc.sbuf_tensor("hh0", [128, CT * NA], BF16))
        hh1 = _e(nc.sbuf_tensor("hh1", [128, CT * NA], BF16))
        hl0 = _e(nc.sbuf_tensor("hl0", [128, CT * NA], BF16))
        hl1 = _e(nc.sbuf_tensor("hl1", [128, CT * NA], BF16))
        Kh_b = _e(nc.sbuf_tensor("Kh_b", [128, CT * NA], BF16))
        Kl_b = _e(nc.sbuf_tensor("Kl_b", [128, CT * NA], BF16))
        y0_b0 = _e(nc.sbuf_tensor("y0_b0", [128, CT * NA], F32))
        K_b = _e(nc.sbuf_tensor("K_b", [128, CT * NA], F32))
        G_b = K_b
        gse_b = _e(nc.sbuf_tensor("gse_b", [128, CT * (NA + 1)], F32))
        gse_c = _e(nc.sbuf_tensor("gse_c", [128, CT * (NA + 1)], F32))
        xsq_b = h32_b
        x_b = [x_b0, x_b1]
        hh_p = [hh0, hh1]
        hl_p = [hl0, hl1]
        y0_p = [y0_b0, y0_b0]
        gse_p = [gse_b, gse_c]
        scB = _e(nc.sbuf_tensor("scB", [128, NA], F32))
        wstr0 = _e(nc.sbuf_tensor("wstr0", [128, I_FFN], F32))
        wfst = wstr0
        wit = _e(nc.sbuf_tensor("wit", [128, I_FFN // 8], I32))
        wqb0 = _e(nc.sbuf_tensor("wqb0", [128, I_FFN // 8], BF16))
        wqb1 = _e(nc.sbuf_tensor("wqb1", [128, I_FFN // 8], BF16))
        winv = _e(nc.sbuf_tensor("winv", [128, 1], F32))

        class _V:
            def __init__(v, ap):
                v._ap = ap
            def ap(v):
                return v._ap
        gath = _V(gse_b.ap()[0:8, 0:1024])
        AT_s = _V(gse_b.ap()[0:8, 0:C])
        Bm_s = _V(h32_b.ap()[0:8, 0:C])
        wstr = [wstr0, wstr0]

        def xs(buf, m, N=NA, dt=4):
            return buf.ap()[:, m * N:(m + 1) * N]

        def wsl(w, kk, m):
            return w.ap()[:, kk * C + m * 128: kk * C + m * 128 + 128]

        g = sch.group("sync", writes=["const", "gath", "h32"], dma_sem="qs")
        for kk in range(CT):
            g.op(lambda e, kk=kk: e.dma_start(out=lr_t.ap()[:, kk:kk + 1], in_=log_lr[128 * kk:128 * kk + 128][:, None]))
            g.op(lambda e, kk=kk: e.dma_start(out=g_t.ap()[:, kk:kk + 1], in_=ln_g[128 * kk:128 * kk + 128][:, None]))
            g.op(lambda e, kk=kk: e.dma_start(out=b_t.ap()[:, kk:kk + 1], in_=ln_b[128 * kk:128 * kk + 128][:, None]))
            g.op(lambda e, kk=kk: e.dma_start(out=ls_t.ap()[:, kk:kk + 1], in_=ls_g[128 * kk:128 * kk + 128][:, None]))
        g.op(lambda e: e.dma_start(out=mask_t.ap(), in_=mask8[:, None]))
        g.op(lambda e: e.dma_start(out=clipt.ap()[0:1, 0:1], in_=clip_x[:, None]))
        g.op(lambda e: e.dma_start(out=clipt.ap()[0:1, 1:2], in_=clip_down[:, None]))
        g.op(lambda e: e.dma_start(out=AT_s.ap(), in_=ATd))
        g.op(lambda e: e.dma_start(out=Bm_s.ap(), in_=Bmd))
        g.done()
        g = sch.group("gpsimd", writes=["const2", "off", "rm", "rm2"])
        g.op(lambda e: e.memset(ones1.ap(), 1.0))
        g.op(lambda e: e.memset(onesk1.ap(), 1.0))
        g.op(lambda e: e.memset(off_t.ap(), 0.0))
        g.done()
        g = sch.group("scalar", reads=["const"], writes=["lr1"])
        g.op(lambda e: e.activation(lr_t.ap(), lr_t.ap(), AF.Exp))
        g.done()
        g = sch.group("vector", reads=["lr1", "const"], writes=["lr", "gder"])
        g.op(lambda e: e.tensor_scalar_min(lr_t.ap(), lr_t.ap(), 1.0))
        g.op(lambda e: e.reciprocal(ginv_t.ap(), g_t.ap()))
        g.op(lambda e: e.tensor_scalar_mul(nb_t.ap(), b_t.ap(), -1.0))
        g.op(lambda e: e.tensor_mul(bg_t.ap(), b_t.ap(), ginv_t.ap()))
        g.done()

        def load_split(src_hbm, dsth, dstl, hres, lres, negate=False, weff=False,
                       stage=None, stage_res=None, colscale=None, gacc=False):
            stg = stage if stage is not None else wfst
            sres = stage_res if stage_res is not None else ("wf0", "wf1")
            for kk in range(CT):
                rg = kk % 2
                r0, r1 = rg * C, rg * C + C
                g = sch.group("sync", writes=[sres[rg]], dma_sem="qs")
                g.op(lambda e, kk=kk, r0=r0, r1=r1, src_hbm=src_hbm: e.dma_start(
                    out=stg.ap()[:, r0:r1], in_=src_hbm[128 * kk:128 * kk + 128, :]))
                g.done()
                if colscale is not None:
                    g = sch.group("vector", reads=[sres[rg], "gder"], writes=[sres[rg]])
                    g.op(lambda e, kk=kk, r0=r0, r1=r1: e.tensor_scalar(
                        stg.ap()[:, r0:r1], stg.ap()[:, r0:r1],
                        colscale.ap()[:, kk:kk + 1], None, ALU.mult))
                    g.done()
                if gacc:
                    g = sch.group("tensor", reads=[sres[rg], "gder"], writes=["ps_cg"])
                    for half in range(2):
                        g.op(lambda e, kk=kk, r0=r0, half=half: e.matmul(
                            psb[6].ap()[32 * half:32 * half + 1, 0:512],
                            bg_t.ap()[:, kk:kk + 1],
                            stg.ap()[:, r0 + half * 512:r0 + half * 512 + 512],
                            start=(kk == 0), stop=(kk == CT - 1)))
                    g.done()
                if weff:
                    g = sch.group("tensor", reads=["gath", "h32"], writes=["psm0", "psm1"])
                    for half in range(2):
                        g.op(lambda e, kk=kk, half=half: e.matmul(
                            psb[half].ap()[:, 0:512], AT_s.ap()[:, kk * 128:kk * 128 + 128],
                            Bm_s.ap()[:, half * 512:half * 512 + 512], start=True, stop=True))
                    g.done()
                    g = sch.group("vector", reads=["psm0", "psm1", sres[rg]], writes=[sres[rg]])
                    for half in range(2):
                        g.op(lambda e, r0=r0, half=half: e.tensor_add(
                            stg.ap()[:, r0 + half * 512:r0 + half * 512 + 512],
                            stg.ap()[:, r0 + half * 512:r0 + half * 512 + 512],
                            psb[half].ap()[:, 0:512]))
                    g.done()
                if negate:
                    g = sch.group("scalar", reads=[sres[rg]], writes=[sres[rg]])
                    g.op(lambda e, r0=r0, r1=r1: e.mul(stg.ap()[:, r0:r1], stg.ap()[:, r0:r1], -1.0))
                    g.done()
                g = sch.group("scalar", reads=[sres[rg]], writes=[hres])
                g.op(lambda e, kk=kk, r0=r0, r1=r1, dsth=dsth: e.copy(
                    xs(dsth, kk, C), stg.ap()[:, r0:r1]))
                g.done()
                g = sch.group("gpsimd", reads=[sres[rg], hres], writes=[lres])
                g.op(lambda e, kk=kk, r0=r0, r1=r1, dsth=dsth, dstl=dstl: e.tensor_sub(
                    xs(dstl, kk, C), stg.ap()[:, r0:r1], xs(dsth, kk, C)))
                g.done()

        load_split(WkT, wAh, wAl, "wAh", "wAl")
        load_split(W0, wBh, wBl, "wBh", "wBl", weff=True)
        load_split(WvT, wCh, wCl, "wCh", "wCl", negate=True)
        g = sch.group("gpsimd", reads=["gath", "h32"], writes=["gse0", "gse1"])
        g.op(lambda e: e.memset(gse_b.ap(), 0.0))
        g.op(lambda e: e.memset(gse_c.ap(), 0.0))
        g.done()


        def quant_strip(w_in, wq_out, sc_tile, sc_res, s_i, cols, half=0):
            SEG = I_FFN // 8
            wqbs = [wqb0, wqb1]
            base = 1024 * half if cols <= 1024 else 0
            sres_q = [f"wf{half}"] if cols <= 1024 else ["wf0", "wf1"]
            g = sch.group("sync", writes=sres_q, dma_sem="qs2")
            g.op(lambda e, s_i=s_i, base=base, cols=cols, w_in=w_in: e.dma_start(
                out=wstr0.ap()[:, base:base + cols], in_=w_in[128 * s_i:128 * s_i + 128, :]))
            g.done()
            segs = [(c0, min(SEG, cols - c0)) for c0 in range(0, cols, SEG)]
            for pi in range(0, len(segs), 2):
                pair = segs[pi:pi + 2]
                if pi == 0:
                    g = sch.group("vector", reads=sres_q,
                                  writes=[sc_res, "winv", "wmax", "wit"], serialize=True)
                    g.op(lambda e, base=base, cols=cols: e.tensor_reduce(
                        wmax.ap(), wstr0.ap()[:, base:base + cols], AX.X, ALU.max,
                        apply_absolute_value=True))
                    g.op(lambda e, s_i=s_i, sc_tile=sc_tile: e.tensor_scalar_mul(
                        sc_tile.ap()[:, s_i:s_i + 1], wmax.ap(), 1.0 / QMAX))
                    g.op(lambda e, s_i=s_i, sc_tile=sc_tile: e.reciprocal(
                        winv.ap(), sc_tile.ap()[:, s_i:s_i + 1]))
                    g.op(lambda e, base=base, cw=pair[0][1]: e.tensor_scalar(
                        wit.ap()[:, 0:cw], wstr0.ap()[:, base:base + cw], winv.ap(),
                        None, ALU.mult))
                    g.done()
                g = sch.group("vector", reads=sres_q + ["winv"],
                              writes=["wit", "wqb0", "wqb1"])
                for bi, (c0, cw) in enumerate(pair):
                    if not (pi == 0 and bi == 0):
                        g.op(lambda e, base=base, c0=c0, cw=cw: e.tensor_scalar(
                            wit.ap()[:, 0:cw], wstr0.ap()[:, base + c0:base + c0 + cw],
                            winv.ap(), None, ALU.mult))
                    g.op(lambda e, bi=bi, cw=cw, wqbs=wqbs: e.tensor_scalar(
                        wqbs[bi].ap()[:, 0:cw], wit.ap()[:, 0:cw], -128.0, 127.0,
                        ALU.max, ALU.min))
                g.done()
                for bi, (c0, cw) in enumerate(pair):
                    g = sch.group("gpsimd", reads=[f"wqb{bi}"], dma_sem="qg")
                    g.op(lambda e, s_i=s_i, c0=c0, cw=cw, bi=bi, wq_out=wq_out,
                         wqbs=wqbs: e.dma_start(
                        out=wq_out[128 * s_i:128 * s_i + 128, c0:c0 + cw],
                        in_=wqbs[bi].ap()[:, 0:cw]))
                    g.done()

        def strip_gen():
            n = 0
            for s_i in range(IT):
                yield (wg_r, wqg_sp, scg, "wscg", s_i, C, n % 2); n += 1
            for s_i in range(IT):
                yield (wu_r, wqu_sp, scu, "wscu", s_i, C, n % 2); n += 1
            for s_i in range(CT):
                yield (wd_r, wqd_sp, scd, "wscd", s_i, I_FFN, 0)

        strips = strip_gen()

        def pump_strips(n):
            for _ in range(n):
                try:
                    quant_strip(*next(strips))
                except StopIteration:
                    return

        def mm3(g, bank, wh, wl, rhit, rlot, m, N):
            passes = [(wh, rhit), (wh, rlot), (wl, rhit)]
            for pi, (ww, rr) in enumerate(passes):
                for kk in range(CT):
                    g.op(lambda e, ww=ww, rr=rr, m=m, kk=kk, bank=bank, N=N,
                         st=(pi == 0 and kk == 0), sp=(pi == 2 and kk == CT - 1): e.matmul(
                        psb[bank].ap()[:, 0:N], wsl(ww, kk, m), xs(rr, kk, N, dt=2),
                        start=st, stop=sp))

        def stats_mm(buf, res, N, xq_buf=None, xq_res="h32", xq_w=None):
            xqb = xq_buf if xq_buf is not None else xsq_b
            xqw = xq_w if xq_w is not None else NA
            g = sch.group("scalar", reads=[res], writes=[xq_res])
            for kk in range(CT):
                g.op(lambda e, kk=kk, buf=buf, N=N: e.activation(
                    xs(xqb, kk, xqw)[:, 0:N], xs(buf, kk, N), AF.Square))
            g.done()
            g = sch.group("tensor", reads=[res, xq_res, "const2"], writes=["ps_st"])
            for kk in range(CT):
                g.op(lambda e, kk=kk, buf=buf, N=N: e.matmul(
                    psb[2].ap()[0:1, 0:N], ones1.ap(), xs(buf, kk, N),
                    start=(kk == 0), stop=(kk == CT - 1)))
            for kk in range(CT):
                g.op(lambda e, kk=kk, N=N: e.matmul(
                    psb[2].ap()[32:33, 0:N], ones1.ap(), xs(xqb, kk, xqw)[:, 0:N],
                    start=(kk == 0), stop=(kk == CT - 1)))
            g.done()

        def stats_math(N, r1_slot, mrn_slot, second=True):
            V = lambda n: SV(n)[:, 0:N]
            g = sch.group("vector", reads=["ps_st"], writes=["stv"])
            g.op(lambda e: e.tensor_scalar_mul(V("mu"), psb[2].ap()[0:1, 0:N], 1.0 / C))
            g.op(lambda e: e.tensor_scalar_mul(V("e2"), psb[2].ap()[32:33, 0:N], 1.0 / C))
            g.op(lambda e: e.tensor_mul(V("a"), V("mu"), V("mu")))
            g.op(lambda e: e.tensor_sub(V("var"), V("e2"), V("a")))
            g.op(lambda e: e.tensor_scalar_add(V("te"), V("var"), EPS))
            if second:
                g.op(lambda e: e.reciprocal(V("rte"), V("te")))
                g.op(lambda e: e.tensor_mul(V("v1r"), V("var"), V("rte")))
                g.op(lambda e: e.tensor_scalar_add(V("te2"), V("v1r"), EPS))
            g.done()
            g = sch.group("scalar", reads=["stv"], writes=["stvs"])
            g.op(lambda e: e.activation(V("s0"), V("te"), AF.Sqrt))
            if second:
                g.op(lambda e: e.activation(V("s2"), V("te2"), AF.Sqrt))
            g.done()
            g = sch.group("vector", reads=["stvs"], writes=["stv"])

            def newton(s_in, t_in, r_out):
                g.op(lambda e, s_in=s_in: e.reciprocal(V("r0"), V(s_in)))
                for srcn, dstn in (("r0", r_out), (r_out, r_out)):
                    g.op(lambda e, srcn=srcn: e.tensor_mul(V("a"), V(srcn), V(srcn)))
                    g.op(lambda e, t_in=t_in: e.tensor_mul(V("b2"), V("a"), V(t_in)))
                    g.op(lambda e: e.tensor_scalar(V("cc2"), V("b2"), -0.5, 1.5, ALU.mult, ALU.add))
                    g.op(lambda e, srcn=srcn, dstn=dstn: e.tensor_mul(V(dstn), V(srcn), V("cc2")))

            newton("s0", "te", r1_slot)
            if second:
                newton("s2", "te2", "r2")
                g.op(lambda e, r1_slot=r1_slot: e.tensor_mul(V("r12"), V(r1_slot), V("r2")))
            g.op(lambda e, r1_slot=r1_slot, mrn_slot=mrn_slot: e.scalar_tensor_tensor(
                V(mrn_slot), V("mu"), -1.0, V(r1_slot), ALU.mult, ALU.mult))
            if second:
                g.op(lambda e, mrn_slot=mrn_slot: e.tensor_mul(V("hc2"), V(mrn_slot), V("r2")))
            g.done()

        def a_front(j):
            px = j % 2
            g = sch.group("sync", writes=[f"x{px}"], dma_sem="qs")
            g.op(lambda e, j=j, px=px: e.dma_start(out=sbuf3(x_b[px], NA),
                                                   in_=hbm3(xT, j * NA, NA)))
            g.done()
            stats_mm(x_b[px], f"x{px}", NA)

        def a_mid(j):
            stats_math(NA, "r1", "mrn", second=True)

        def a_bcast(j):
            g = sch.group("tensor", reads=["stv", "const2"], writes=["ps_bc"])
            for qq, sl in enumerate(("r12", "hc2")):
                g.op(lambda e, qq=qq, sl=sl: e.matmul(
                    psb[3].ap()[:, qq * NA:(qq + 1) * NA], onesk1.ap(),
                    SV(sl)[:, 0:NA], start=True, stop=True))
            g.done()

        def a_xah(j):
            px = j % 2
            g = sch.group("vector", reads=["ps_bc", f"x{px}", "const"],
                          writes=["h32"])
            for kk in range(CT):
                g.op(lambda e, kk=kk, px=px: e.tensor_mul(
                    xs(h32_b, kk), xs(x_b[px], kk), psb[3].ap()[:, 0:NA]))
            for kk in range(CT):
                g.op(lambda e, kk=kk: e.tensor_add(
                    xs(h32_b, kk), xs(h32_b, kk), psb[3].ap()[:, NA:2 * NA]))
            for kk in range(CT):
                g.op(lambda e, kk=kk: e.tensor_scalar(
                    xs(h32_b, kk), xs(h32_b, kk), g_t.ap()[:, kk:kk + 1],
                    b_t.ap()[:, kk:kk + 1], ALU.mult, ALU.add))
            g.done()
            g = sch.group("scalar", reads=["h32"], writes=[f"hh{px}"])
            g.op(lambda e, px=px: e.copy(hh_p[px].ap(), h32_b.ap()))
            g.done()
            g = sch.group("gpsimd", reads=["h32", f"hh{px}"], writes=[f"hl{px}"])
            g.op(lambda e, px=px: e.tensor_sub(hl_p[px].ap(), h32_b.ap(), hh_p[px].ap()))
            g.done()
            g = sch.group("gpsimd", reads=[f"hh{px}", f"hl{px}"], dma_sem="qg")
            g.op(lambda e, j=j, px=px: e.dma_start(out=hbm3(hh_sp, j * NA, NA),
                                                   in_=sbuf3(hh_p[px], NA)))
            g.op(lambda e, j=j, px=px: e.dma_start(out=hbm3(hl_sp, j * NA, NA),
                                                   in_=sbuf3(hl_p[px], NA)))
            g.done()

        def a_K(i):
            px = i % 2
            for m in range(CT):
                pm = f"psm{m % 2}"
                g = sch.group("tensor", reads=[f"hh{px}", f"hl{px}", "wAh", "wAl"], writes=[pm])
                mm3(g, m % 2, wAh, wAl, hh_p[px], hl_p[px], m, NA)
                g.done()
                g = sch.group("scalar", reads=[pm], writes=["K"])
                g.op(lambda e, m=m: e.copy(xs(K_b, m), psb[m % 2].ap()[:, 0:NA]))
                g.done()
            g = sch.group("scalar", reads=["K"], writes=["Kh"])
            g.op(lambda e: e.copy(Kh_b.ap(), K_b.ap()))
            g.done()
            g = sch.group("gpsimd", reads=["K", "Kh"], writes=["Kl"])
            g.op(lambda e: e.tensor_sub(Kl_b.ap(), K_b.ap(), Kh_b.ap()))
            g.done()

        def a_tail(i):
            px = i % 2
            for m in range(CT):
                pm = f"psm{m % 2}"
                g = sch.group("tensor", reads=["Kh", "Kl", "wBh", "wBl"], writes=[pm])
                mm3(g, m % 2, wBh, wBl, Kh_b, Kl_b, m, NA)
                g.done()
                g = sch.group("scalar", reads=[pm], writes=["y0A"])
                g.op(lambda e, m=m, px=px: e.copy(xs(y0_p[px], m), psb[m % 2].ap()[:, 0:NA]))
                g.done()
                g = sch.group("tensor", reads=[f"hh{px}", f"hl{px}", "wCh", "wCl", "y0A"],
                              writes=[pm])
                mm3(g, m % 2, wCh, wCl, hh_p[px], hl_p[px], m, NA)
                g.done()
                g = sch.group("vector", reads=[pm, "y0A"], writes=["K"])
                g.op(lambda e, m=m, px=px: e.tensor_add(scB.ap()[:, 0:NA], xs(y0_p[px], m),
                                                        psb[m % 2].ap()[:, 0:NA]))
                g.op(lambda e, m=m: e.tensor_mul(xs(G_b, m), xs(K_b, m), scB.ap()[:, 0:NA]))
                g.done()
            g = sch.group("scalar", reads=["y0A"], dma_sem="qg")
            g.op(lambda e, i=i, px=px: e.dma_start(out=hbm3(y0_sp, i * NA, NA),
                                                   in_=sbuf3(y0_p[px], NA)))
            g.done()
            pg = i % 2
            gcur, gprv = gse_p[pg], gse_p[1 - pg]
            g = sch.group("vector", reads=["K", f"gse{1 - pg}"], writes=[f"gse{pg}"])
            for m in range(CT):
                base = m * (NA + 1)
                g.op(lambda e, base=base, gcur=gcur, gprv=gprv: e.tensor_copy(
                    gcur.ap()[:, base:base + 1], gprv.ap()[:, base + NA:base + NA + 1]))
                g.op(lambda e, base=base, m=m, gcur=gcur, gprv=gprv: e.tensor_tensor_scan(
                    gcur.ap()[:, base + 1:base + 1 + NA], xs(G_b, m), xs(G_b, m),
                    gprv.ap()[:, base + NA:base + NA + 1], ALU.add, ALU.bypass))
            g.done()
            g = sch.group("scalar", reads=[f"gse{pg}"], dma_sem="qg")
            g.op(lambda e, i=i, gcur=gcur: e.dma_start(
                out=hbm3(gs_sp, i * NA, NA),
                in_=gcur.ap().rearrange("p (k n) -> p k n", k=CT)[:, :, 0:NA]))
            g.done()

        for s in range(nCA + 1):
            if s >= 1:
                a_K(s - 1)
            if s < nCA:
                a_front(s)
                a_mid(s)
            if s >= 1:
                a_tail(s - 1)
            if s < nCA:
                a_bcast(s)
                a_xah(s)
            pump_strips(4 if s > 0 else 0)
        pump_strips(64)

        pg_last = (nCA - 1) % 2
        glast = gse_b if pg_last == 0 else gse_c
        g = sch.group("gpsimd", reads=[f"gse{pg_last}"], dma_sem="qg")
        for m in range(CT):
            g.op(lambda e, m=m, glast=glast: e.dma_start(
                out=cc_gin[m * 128:m * 128 + 128][:, None],
                in_=glast.ap()[:, m * (NA + 1) + NA:m * (NA + 1) + NA + 1]))
        g.done()
        g = sch.group("gpsimd", writes=["ccg"], sem="cc", waits=sch.queue_totals("qg"))
        g.op(lambda e: e.collective_compute("AllGather", ALU.bypass, replica_groups=RG,
                                            ins=[cc_gin.opt()], outs=[cc_gout.opt()]))
        g.done()
        g = sch.group("sync", reads=["ccg"], writes=["gath", "gse0"], dma_sem="qs")
        g.op(lambda e: e.dma_start(out=gath.ap(), in_=cc_gout))
        g.done()
        for half in range(2):
            g = sch.group("tensor", reads=["gath", "const"], writes=["ps_bc"])
            g.op(lambda e, half=half: e.matmul(
                psb[3].ap()[0:1, 0:512], mask_t.ap(),
                gath.ap()[:, half * 512:half * 512 + 512], start=True, stop=True))
            g.done()
            g = sch.group("scalar", reads=["ps_bc"], writes=["offrow", "stv"])
            g.op(lambda e, half=half: e.copy(m1row.ap()[0:1, half * 512:half * 512 + 512],
                                             psb[3].ap()[0:1, 0:512]))
            g.done()
        g = sch.group("gpsimd", reads=["offrow"], dma_sem="qg")
        g.op(lambda e: e.dma_start(out=off_hbm[None, :], in_=m1row.ap()[0:1, 0:1024]))
        g.done()
        g = sch.group("sync", writes=["off"], dma_sem="qs", waits=sch.queue_totals("qg"))
        for m in range(CT):
            g.op(lambda e, m=m: e.dma_start(out=off_t.ap()[:, m:m + 1],
                                            in_=off_hbm[m * 128:m * 128 + 128][:, None]))
        g.done()

        snapAB = dict(sch.sem_count)
        for e in ENGINES:
            sch.wait_only(e, [(s, v) for s, v in snapAB.items()])
        p1a.close()

        p1b = ExitStack()
        _b = p1b.enter_context
        hhB = [_b(nc.sbuf_tensor(f"hhB{p}", [128, CT * NB], BF16)) for p in range(2)]
        hlB = [_b(nc.sbuf_tensor(f"hlB{p}", [128, CT * NB], BF16)) for p in range(2)]
        x1B = [_b(nc.sbuf_tensor(f"x1B{p}", [128, CT * NB], F32)) for p in range(2)]
        y0B = [_b(nc.sbuf_tensor(f"y0B{p}", [128, CT * NB], F32)) for p in range(2)]
        gseB = [_b(nc.sbuf_tensor(f"gseB{p}", [128, CT * (NB + 1)], F32)) for p in range(2)]
        xahB = _b(nc.sbuf_tensor("xahB", [128, CT * NB], BF16))
        xalB = _b(nc.sbuf_tensor("xalB", [128, CT * NB], BF16))
        h32B = _b(nc.sbuf_tensor("h32B", [128, CT * NB], F32))
        scAB = _b(nc.sbuf_tensor("scAB", [128, NB], F32))
        scBB = _b(nc.sbuf_tensor("scBB", [128, NB], F32))
        sresB = ("h32B0", "h32B1")

        load_split(WqT, wAh, wAl, "wAh", "wAl", stage=h32B, stage_res=sresB)
        load_split(WgT, wBh, wBl, "wBh", "wBl", stage=h32B, stage_res=sresB,
                   colscale=ginv_t, gacc=True)
        load_split(WoT, wCh, wCl, "wCh", "wCl", stage=h32B, stage_res=sresB)
        g = sch.group("scalar", reads=["ps_cg"], writes=["offrow", "stv"])
        for half in range(2):
            g.op(lambda e, half=half: e.mul(
                m1row.ap()[0:1, half * 512:half * 512 + 512],
                psb[6].ap()[32 * half:32 * half + 1, 0:512], -1.0))
        g.done()
        g = sch.group("gpsimd", reads=["offrow"], dma_sem="qg")
        g.op(lambda e: e.dma_start(out=cg_hbm[None, :], in_=m1row.ap()[0:1, 0:1024]))
        g.done()
        g = sch.group("sync", writes=["cgt"], dma_sem="qs", waits=sch.queue_totals("qg"))
        for m in range(CT):
            g.op(lambda e, m=m: e.dma_start(out=ngate_t.ap()[:, m:m + 1],
                                            in_=cg_hbm[m * 128:m * 128 + 128][:, None]))
        g.done()

        def b_loads(j):
            px = j % 2
            for dst, res_, srcp in [(hhB[px], f"hh{px}", hh_sp), (hlB[px], f"hl{px}", hl_sp),
                                    (y0B[px], f"y0{px}", y0_sp)]:
                g = sch.group("sync", writes=[res_], dma_sem="qs")
                g.op(lambda e, j=j, dst=dst, srcp=srcp: e.dma_start(
                    out=sbuf3(dst, NB), in_=hbm3(srcp, j * NB, NB)))
                g.done()
            g = sch.group("sync", writes=[f"gse{px}"], dma_sem="qs")
            g.op(lambda e, j=j, px=px: e.dma_start(
                out=gseB[px].ap().rearrange("p (k n) -> p k n", k=CT)[:, :, 0:NB],
                in_=hbm3(gs_sp, j * NB, NB)))
            g.done()
            psl = SV("psc") if px == 0 else SV("hc2")
            g = sch.group("sync", writes=[f"psc{px}"], dma_sem="qs")
            g.op(lambda e, j=j, psl=psl: e.dma_start(out=psl[:, 0:NB],
                                                     in_=ps_row_in[0:1, j * NB:j * NB + NB]))
            g.done()

        def b_main(i):
            px = i % 2
            psl = SV("psc") if px == 0 else SV("hc2")
            g = sch.group("tensor", reads=[f"psc{px}", "const2"], writes=["ps_ps"])
            g.op(lambda e, psl=psl: e.matmul(psb[7].ap()[:, 0:NB], onesk1.ap(), psl[:, 0:NB],
                                             start=True, stop=True))
            g.done()
            for m in range(CT):
                pm = f"psm{m % 2}"
                pgm = f"psg{m % 2}"
                g = sch.group("tensor", reads=[f"hh{px}", f"hl{px}", "wAh", "wAl"], writes=[pm])
                mm3(g, m % 2, wAh, wAl, hhB[px], hlB[px], m, NB)
                g.done()
                g = sch.group("tensor", reads=[f"hh{px}", f"hl{px}", "wBh", "wBl"], writes=[pgm])
                mm3(g, 4 + m % 2, wBh, wBl, hhB[px], hlB[px], m, NB)
                g.done()
                g = sch.group("scalar", reads=[pgm, "cgt"], writes=["h32B"])
                g.op(lambda e, m=m: e.activation(xs(h32B, m, NB), psb[4 + m % 2].ap()[:, 0:NB],
                                                 AF.Sigmoid, bias=ngate_t.ap()[:, m:m + 1]))
                g.done()
                g = sch.group("vector",
                              reads=[f"gse{px}", "off", "ps_ps", pm, f"y0{px}", "h32B", "lr"],
                              writes=["h32B"])
                base = m * (NB + 1)
                g.op(lambda e, base=base, m=m, px=px: e.scalar_tensor_tensor(
                    scAB.ap(), gseB[px].ap()[:, base:base + NB], off_t.ap()[:, m:m + 1],
                    psb[7].ap()[:, 0:NB], ALU.add, ALU.mult))
                g.op(lambda e, m=m: e.scalar_tensor_tensor(
                    scBB.ap(), psb[m % 2].ap()[:, 0:NB], lr_t.ap()[:, m:m + 1], scAB.ap(),
                    ALU.mult, ALU.mult))
                g.op(lambda e, m=m, px=px: e.tensor_sub(scAB.ap(), xs(y0B[px], m, NB),
                                                        scBB.ap()))
                g.op(lambda e, m=m: e.tensor_mul(xs(h32B, m, NB), scAB.ap(), xs(h32B, m, NB)))
                g.done()
            g = sch.group("scalar", reads=["h32B"], writes=["xah"])
            g.op(lambda e: e.copy(xahB.ap(), h32B.ap()))
            g.done()
            g = sch.group("gpsimd", reads=["h32B", "xah"], writes=["xal"])
            g.op(lambda e: e.tensor_sub(xalB.ap(), h32B.ap(), xahB.ap()))
            g.done()
            g = sch.group("vector", reads=[f"hh{px}", f"hl{px}", "gder", "const"],
                          writes=[f"x1{px}"])
            for kk in range(CT):
                g.op(lambda e, kk=kk, px=px: e.tensor_add(
                    scAB.ap(), xs(hhB[px], kk, NB), xs(hlB[px], kk, NB)))
                g.op(lambda e, kk=kk: e.tensor_scalar(
                    scAB.ap(), scAB.ap(), nb_t.ap()[:, kk:kk + 1],
                    ginv_t.ap()[:, kk:kk + 1], ALU.add, ALU.mult))
                g.op(lambda e, kk=kk, px=px: e.tensor_add(
                    xs(x1B[px], kk, NB), xs(x1B[px], kk, NB), scAB.ap()))
            g.done()
            for m in range(CT):
                pm = f"psm{m % 2}"
                g = sch.group("tensor", reads=["xah", "xal", "wCh", "wCl"], writes=[pm])
                mm3(g, m % 2, wCh, wCl, xahB, xalB, m, NB)
                g.done()
                g = sch.group("vector", reads=[pm, "const"], writes=[f"x1{px}"])
                g.op(lambda e, m=m, px=px: e.scalar_tensor_tensor(
                    xs(x1B[px], m, NB), psb[m % 2].ap()[:, 0:NB], ls_t.ap()[:, m:m + 1],
                    xs(x1B[px], m, NB), ALU.mult, ALU.add))
                g.done()
            g = sch.group("gpsimd", reads=[f"x1{px}"], dma_sem="qg")
            g.op(lambda e, i=i, px=px: e.dma_start(out=hbm3(x1_sp, i * NB, NB),
                                                   in_=sbuf3(x1B[px], NB)))
            g.done()
            stats_mm(x1B[px], f"x1{px}", NB, xq_buf=h32B, xq_res="h32B", xq_w=NB)
            stats_math(NB, "r1b", "mrnb", second=False)

        def b_y(i):
            px = i % 2
            g = sch.group("tensor", reads=["stv", "const2"], writes=["ps_bc"])
            g.op(lambda e: e.matmul(psb[3].ap()[:, 0:2 * NB], onesk1.ap(),
                                    stv_buf.ap()[:, 0:2 * NB], start=True, stop=True))
            g.done()
            g = sch.group("vector", reads=["ps_bc", f"x1{px}"], writes=["h32B", "rm"])
            for kk in range(CT):
                g.op(lambda e, kk=kk, px=px: e.tensor_mul(
                    xs(h32B, kk, NB), xs(x1B[px], kk, NB), psb[3].ap()[:, 0:NB]))
            for kk in range(CT):
                g.op(lambda e, kk=kk: e.tensor_add(xs(h32B, kk, NB), xs(h32B, kk, NB),
                                                   psb[3].ap()[:, NB:2 * NB]))
            for kk in range(CT):
                g.op(lambda e, kk=kk, i=i: e.tensor_reduce(
                    rmax.ap()[:, i * CT + kk:i * CT + kk + 1], xs(h32B, kk, NB), AX.X,
                    ALU.max, apply_absolute_value=True))
            g.done()
            g = sch.group("gpsimd", reads=["h32B"], dma_sem="qg")
            g.op(lambda e, i=i: e.dma_start(out=hbm3(y_sp, i * NB, NB), in_=sbuf3(h32B, NB)))
            g.done()

        def b_loadx(j):
            px = j % 2
            g = sch.group("sync", writes=[f"x1{px}"], dma_sem="qs")
            g.op(lambda e, j=j, px=px: e.dma_start(out=sbuf3(x1B[px], NB),
                                                   in_=hbm3(xT, j * NB, NB)))
            g.done()

        nCBB = T // NB
        b_loadx(0)
        for s in range(nCBB + 2):
            if s >= 2:
                b_y(s - 2)
            if s < nCBB:
                b_loads(s)
                if s >= 1:
                    b_loadx(s)
            if s >= 1 and s - 1 < nCBB:
                b_main(s - 1)

        part1 = sch.mark()
        with nc.Block() as blk:
            for e in ENGINES:
                def closure(eng, e=e):
                    sch.replay(e, eng, sems, 0, part1)
                getattr(blk, e)(closure)
        p1b.close()

    with ExitStack() as p2stack:
        _e2 = p2stack.enter_context
        wqgT = _e2(nc.sbuf_tensor("wqgT", [128, CT * I_FFN], BF16))
        wquT = _e2(nc.sbuf_tensor("wquT", [128, CT * I_FFN], BF16))
        wqdT = _e2(nc.sbuf_tensor("wqdT", [128, IT * C], BF16))
        y_bf0 = _e2(nc.sbuf_tensor("y_bf0", [128, CT * NF], F32))
        y_bf1 = y_bf0
        xq_b = _e2(nc.sbuf_tensor("xq_b", [128, CT * NF], BF16))
        itq = _e2(nc.sbuf_tensor("itq", [128, NF], I32))
        silu0 = _e2(nc.sbuf_tensor("silu0", [128, NF], F32))
        silu1 = _e2(nc.sbuf_tensor("silu1", [128, NF], F32))
        z_t0 = _e2(nc.sbuf_tensor("z_t0", [128, NF], F32))
        z_t1 = _e2(nc.sbuf_tensor("z_t1", [128, NF], F32))
        z_t2 = _e2(nc.sbuf_tensor("z_t2", [128, NF], F32))
        z_t3 = _e2(nc.sbuf_tensor("z_t3", [128, NF], F32))
        zq_t0 = _e2(nc.sbuf_tensor("zq_t0", [128, NF], BF16))
        zq_t1 = _e2(nc.sbuf_tensor("zq_t1", [128, NF], BF16))
        zq_t2 = _e2(nc.sbuf_tensor("zq_t2", [128, NF], BF16))
        zq_t3 = _e2(nc.sbuf_tensor("zq_t3", [128, NF], BF16))
        out_b = _e2(nc.sbuf_tensor("out_b", [128, CT * NF], F32))

        class _V2:
            def __init__(v, ap):
                v._ap = ap
            def ap(v):
                return v._ap
        m2row = _V2(out_b.ap()[0:1, 0:1024])
        dbgbufS = _e2(nc.sbuf_tensor("dbgbufS", [128, 64], F32))

        y_bufs = [y_bf0, y_bf1]
        silu_t = [silu0, silu1]
        z_t = [z_t0, z_t1]
        zD_t = [z_t0, z_t1, z_t2, z_t3]
        zqD_t = [zq_t0, zq_t1, zq_t2, zq_t3]

        snap = dict(sch.sem_count)
        for e in ENGINES:
            sch.wait_only(e, [(s, v) for s, v in snap.items()])

        g = sch.group("sync", writes=["wqg"], dma_sem="qs")
        for kk in range(CT):
            g.op(lambda e, kk=kk: e.dma_start_transpose(
                wqgT.ap()[:, kk * I_FFN:(kk + 1) * I_FFN], wqg_sp[:, 128 * kk:128 * kk + 128]))
        g.done()
        g = sch.group("sync", writes=["wqu"], dma_sem="qs")
        for kk in range(CT):
            g.op(lambda e, kk=kk: e.dma_start_transpose(
                wquT.ap()[:, kk * I_FFN:(kk + 1) * I_FFN], wqu_sp[:, 128 * kk:128 * kk + 128]))
        g.done()
        g = sch.group("sync", writes=["wqd"], dma_sem="qs")
        for kk in range(IT):
            g.op(lambda e, kk=kk: e.dma_start_transpose(
                wqdT.ap()[:, kk * C:(kk + 1) * C], wqd_sp[:, 128 * kk:128 * kk + 128]))
        g.done()

        g = sch.group("vector", reads=["rm"], writes=["redm"])
        g.op(lambda e: e.tensor_reduce(redt.ap(), rmax.ap()[:, 0:(T // NB) * CT], AX.X, ALU.max))
        g.done()
        g = sch.group("gpsimd", reads=["redm"], dma_sem="qg")
        g.op(lambda e: e.dma_start(out=cc_m1in[:, None], in_=redt.ap()))
        g.done()
        g = sch.group("gpsimd", writes=["ccm1"], sem="cc", waits=sch.queue_totals("qg"))
        g.op(lambda e: e.collective_compute("AllGather", ALU.bypass, replica_groups=RG,
                                            ins=[cc_m1in.opt()], outs=[cc_m1out.opt()]))
        g.done()
        g = sch.group("sync", reads=["ccm1"], writes=["m1row"], dma_sem="qs")
        g.op(lambda e: e.dma_start(out=m2row.ap(), in_=cc_m1out[None, :]))
        g.done()
        g = sch.group("scalar", reads=["const"], writes=["sig1"])
        g.op(lambda e: e.activation(sc1x1.ap()[0:1, 0:1], clipt.ap()[0:1, 0:1], AF.Sigmoid))
        g.done()
        g = sch.group("vector", reads=["m1row", "sig1"], writes=["s1s"], serialize=True)
        g.op(lambda e: e.tensor_reduce(sc1x1.ap()[0:1, 1:2], m2row.ap(), AX.X, ALU.max))
        g.op(lambda e: e.tensor_mul(sc1x1.ap()[0:1, 2:3], sc1x1.ap()[0:1, 1:2], sc1x1.ap()[0:1, 0:1]))
        g.op(lambda e: e.tensor_scalar_mul(sc1x1.ap()[0:1, 2:3], sc1x1.ap()[0:1, 2:3], 1.0 / QMAX))
        g.op(lambda e: e.reciprocal(sc1x1.ap()[0:1, 3:4], sc1x1.ap()[0:1, 2:3]))
        g.done()
        g = sch.group("tensor", reads=["s1s", "const2"], writes=["ps_bc"])
        g.op(lambda e: e.matmul(psb[3].ap()[:, 0:2], onesk1.ap(), sc1x1.ap()[0:1, 2:4],
                                start=True, stop=True))
        g.done()
        g = sch.group("scalar", reads=["ps_bc"], writes=["s1v"])
        g.op(lambda e: e.copy(s1p.ap(), psb[3].ap()[:, 0:1]))
        g.op(lambda e: e.copy(s1v.ap(), psb[3].ap()[:, 1:2]))
        g.done()
        g = sch.group("vector", reads=["s1v", "wscg", "wscu"], writes=["sq1"])
        g.op(lambda e: e.tensor_scalar(sgq.ap(), scg.ap(), s1p.ap(), None, ALU.mult))
        g.op(lambda e: e.tensor_scalar(suq.ap(), scu.ap(), s1p.ap(), None, ALU.mult))
        g.done()

        for i in range(nCF):
            t0 = i * NF
            p = i % 2
            ybr = "yb"
            g = sch.group("sync", writes=[ybr], dma_sem="qs")
            g.op(lambda e, t0=t0, p=p: e.dma_start(out=sbuf3(y_bufs[p], NF), in_=hbm3(y_sp, t0, NF)))
            g.done()
            g = sch.group("vector", reads=[ybr, "s1v"], writes=["xq"])
            for kk in range(CT):
                g.op(lambda e, kk=kk, p=p: e.tensor_scalar(
                    itq.ap(), y_bufs[p].ap()[:, kk * NF:(kk + 1) * NF], s1v.ap(), None, ALU.mult))
                g.op(lambda e, kk=kk: e.tensor_scalar(
                    xq_b.ap()[:, kk * NF:(kk + 1) * NF], itq.ap(), -128.0, 127.0, ALU.max, ALU.min))
            g.done()
            for kb in range(IT):
                pp = kb % 2
                g = sch.group("tensor", reads=["xq", "wqg"], writes=[f"psf{pp}"])
                for kk in range(CT):
                    g.op(lambda e, kb=kb, kk=kk, pp=pp: e.matmul(
                        psb[pp].ap()[:, 0:NF],
                        wqgT.ap()[:, kk * I_FFN + kb * 128: kk * I_FFN + kb * 128 + 128],
                        xq_b.ap()[:, kk * NF:(kk + 1) * NF], start=(kk == 0), stop=(kk == CT - 1)))
                g.done()
                g = sch.group("scalar", reads=[f"psf{pp}", "sq1"], writes=[f"sil{pp}"])
                g.op(lambda e, kb=kb, pp=pp: e.activation(
                    silu_t[pp].ap(), psb[pp].ap()[:, 0:NF], AF.Silu, scale=sgq.ap()[:, kb:kb + 1]))
                g.done()
                g = sch.group("tensor", reads=["xq", "wqu"], writes=[f"psf{2 + pp}"])
                for kk in range(CT):
                    g.op(lambda e, kb=kb, kk=kk, pp=pp: e.matmul(
                        psb[2 + pp].ap()[:, 0:NF],
                        wquT.ap()[:, kk * I_FFN + kb * 128: kk * I_FFN + kb * 128 + 128],
                        xq_b.ap()[:, kk * NF:(kk + 1) * NF], start=(kk == 0), stop=(kk == CT - 1)))
                g.done()
                g = sch.group("vector", reads=[f"psf{2 + pp}", f"sil{pp}", "sq1"],
                              writes=[f"zt{pp}", "rm2"])
                g.op(lambda e, kb=kb, pp=pp: e.scalar_tensor_tensor(
                    z_t[pp].ap(), silu_t[pp].ap(), suq.ap()[:, kb:kb + 1], psb[2 + pp].ap()[:, 0:NF],
                    ALU.mult, ALU.mult))
                g.op(lambda e, kb=kb, pp=pp, i=i: e.tensor_reduce(
                    rmax2.ap()[:, i * IT + kb:i * IT + kb + 1], z_t[pp].ap(), AX.X, ALU.max,
                    apply_absolute_value=True))
                g.done()
                g = sch.group("gpsimd", reads=[f"zt{pp}"], dma_sem="qg")
                g.op(lambda e, t0=t0, kb=kb, pp=pp: e.dma_start(
                    out=z_sp[kb * 128:kb * 128 + 128, t0:t0 + NF], in_=z_t[pp].ap()))
                g.done()

        g = sch.group("vector", reads=["rm2"], writes=["redm2"])
        g.op(lambda e, nCF=nCF: e.tensor_reduce(redt.ap(), rmax2.ap()[:, 0:nCF * IT], AX.X, ALU.max))
        g.done()
        g = sch.group("gpsimd", reads=["redm2"], dma_sem="qg")
        g.op(lambda e: e.dma_start(out=cc_m2in[:, None], in_=redt.ap()))
        g.done()
        g = sch.group("gpsimd", writes=["ccm2"], sem="cc", waits=sch.queue_totals("qg"))
        g.op(lambda e: e.collective_compute("AllGather", ALU.bypass, replica_groups=RG,
                                            ins=[cc_m2in.opt()], outs=[cc_m2out.opt()]))
        g.done()
        g = sch.group("sync", reads=["ccm2"], writes=["m2row"], dma_sem="qs")
        g.op(lambda e: e.dma_start(out=m2row.ap(), in_=cc_m2out[None, :]))
        g.done()
        g = sch.group("scalar", reads=["const"], writes=["sig2"])
        g.op(lambda e: e.activation(sc1x1.ap()[0:1, 4:5], clipt.ap()[0:1, 1:2], AF.Sigmoid))
        g.done()
        g = sch.group("vector", reads=["m2row", "sig2"], writes=["s2s"], serialize=True)
        g.op(lambda e: e.tensor_reduce(sc1x1.ap()[0:1, 5:6], m2row.ap(), AX.X, ALU.max))
        g.op(lambda e: e.tensor_mul(sc1x1.ap()[0:1, 6:7], sc1x1.ap()[0:1, 5:6], sc1x1.ap()[0:1, 4:5]))
        g.op(lambda e: e.tensor_scalar_mul(sc1x1.ap()[0:1, 6:7], sc1x1.ap()[0:1, 6:7], 1.0 / QMAX))
        g.op(lambda e: e.reciprocal(sc1x1.ap()[0:1, 7:8], sc1x1.ap()[0:1, 6:7]))
        g.done()
        g = sch.group("tensor", reads=["s2s", "const2"], writes=["ps_bc2"])
        g.op(lambda e: e.matmul(psb[6].ap()[:, 0:2], onesk1.ap(), sc1x1.ap()[0:1, 6:8],
                                start=True, stop=True))
        g.done()
        g = sch.group("scalar", reads=["ps_bc2"], writes=["s2v"])
        g.op(lambda e: e.copy(s2p.ap(), psb[6].ap()[:, 0:1]))
        g.op(lambda e: e.copy(s2v.ap(), psb[6].ap()[:, 1:2]))
        g.done()
        g = sch.group("vector", reads=["s2v", "wscd"], writes=["sqd"])
        g.op(lambda e: e.tensor_scalar(sdq.ap(), scd.ap(), s2p.ap(), None, ALU.mult))
        g.done()

        zg = 0
        for i in range(nCF):
            t0 = i * NF
            p = i % 2
            ybr = "yb"
            g = sch.group("sync", writes=[ybr], dma_sem="qs")
            g.op(lambda e, t0=t0, p=p: e.dma_start(out=sbuf3(y_bufs[p], NF), in_=hbm3(x1_sp, t0, NF)))
            g.done()
            for kb in range(IT):
                pp = zg % 4
                zg += 1
                g = sch.group("sync", writes=[f"zt{pp}"], dma_sem="qs")
                g.op(lambda e, t0=t0, kb=kb, pp=pp: e.dma_start(
                    out=zD_t[pp].ap(), in_=z_sp[kb * 128:kb * 128 + 128, t0:t0 + NF]))
                g.done()
                g = sch.group("vector", reads=[f"zt{pp}", "s2v"], writes=[f"zq{pp}"])
                g.op(lambda e, pp=pp: e.tensor_scalar(itq.ap(), zD_t[pp].ap(), s2v.ap(), None, ALU.mult))
                g.op(lambda e, pp=pp: e.tensor_scalar(zqD_t[pp].ap(), itq.ap(), -128.0, 127.0,
                                                      ALU.max, ALU.min))
                g.done()
                psw = ["psd"] if not (i == 0 and kb == 0) else [
                    "psd", "psf0", "psf1", "psf2", "psf3", "ps_bc2"]
                g = sch.group("tensor", reads=[f"zq{pp}", "wqd"], writes=psw)
                for mm in range(CT):
                    g.op(lambda e, kb=kb, mm=mm, pp=pp: e.matmul(
                        psb[mm].ap()[:, 0:NF],
                        wqdT.ap()[:, kb * C + mm * 128: kb * C + mm * 128 + 128],
                        zqD_t[pp].ap(), start=(kb == 0), stop=(kb == IT - 1)))
                g.done()
            g = sch.group("scalar", reads=["psd", "sqd"], writes=["outb"])
            for mm in range(CT):
                g.op(lambda e, mm=mm: e.activation(
                    out_b.ap()[:, mm * NF:(mm + 1) * NF], psb[mm].ap()[:, 0:NF],
                    AF.Copy, scale=sdq.ap()[:, mm:mm + 1]))
            g.done()
            g = sch.group("vector", reads=["outb", ybr], writes=["outb"])
            for kk in range(CT):
                g.op(lambda e, kk=kk, p=p: e.tensor_add(
                    out_b.ap()[:, kk * NF:(kk + 1) * NF], out_b.ap()[:, kk * NF:(kk + 1) * NF],
                    y_bufs[p].ap()[:, kk * NF:(kk + 1) * NF]))
            g.done()
            g = sch.group("sync", reads=["outb"], dma_sem="qs")
            g.op(lambda e, t0=t0: e.dma_start(out=hbm3(outT, t0, NF), in_=sbuf3(out_b, NF)))
            g.done()

        if DEBUG_SCALES:
            dbg_sc = dint("dbg_sc", [128, 64])
            g = sch.group("vector", reads=["wscg", "wscu", "wscd", "sq1", "sqd", "s1v", "s2v"],
                          writes=["dbgsc"])
            g.op(lambda e: e.tensor_copy(dbgbufS.ap()[:, 0:IT], scg.ap()))
            g.op(lambda e: e.tensor_copy(dbgbufS.ap()[:, 22:22 + IT], scu.ap()))
            g.op(lambda e: e.tensor_copy(dbgbufS.ap()[:, 44:44 + CT], scd.ap()))
            g.op(lambda e: e.tensor_copy(dbgbufS.ap()[:, 52:53], s1v.ap()))
            g.op(lambda e: e.tensor_copy(dbgbufS.ap()[:, 53:54], s2v.ap()))
            g.op(lambda e: e.tensor_copy(dbgbufS.ap()[:, 54:55], s1p.ap()))
            g.op(lambda e: e.tensor_copy(dbgbufS.ap()[:, 55:56], s2p.ap()))
            g.done()
            g = sch.group("gpsimd", reads=["dbgsc"], dma_sem="qg")
            g.op(lambda e: e.dma_start(out=dbg_sc, in_=dbgbufS.ap()))
            g.done()
        sch.wait_only("sync", sch.queue_totals("qs"))
        sch.wait_only("gpsimd", sch.queue_totals("qg"))

        part2 = sch.mark()
        with nc.Block() as blk2:
            for e in ENGINES:
                def closure2(eng, e=e):
                    sch.replay(e, eng, sems, part1, part2)
                getattr(blk2, e)(closure2)

    estack.close()
    return nc


_T = 4096


def _prep_core_inputs(core, inputs, T):
    b, q = core // 4, core % 4
    t0 = q * T
    d = {}
    d["xT"] = np.ascontiguousarray(inputs["L0_attn_x"][b, t0:t0 + T, :].T)
    d["WqT"] = np.ascontiguousarray(inputs["L0_attn_Wq"].T)
    d["WkT"] = np.ascontiguousarray(inputs["L0_attn_Wk"].T)
    d["WvT"] = np.ascontiguousarray(inputs["L0_attn_Wv"].T)
    d["WgT"] = np.ascontiguousarray(inputs["L0_attn_Wg"].T)
    d["WoT"] = np.ascontiguousarray(inputs["L0_attn_Wo"].T)
    d["W0"] = np.ascontiguousarray(inputs["L0_attn_W0"])
    d["AT"] = np.ascontiguousarray(inputs["L0_attn_A_init"].T)
    d["Bm"] = np.ascontiguousarray(inputs["L0_attn_B_init"])
    d["log_lr"] = inputs["L0_attn_log_lr"]
    d["ln_g"] = inputs["L0_attn_ln_g"]
    d["ln_b"] = inputs["L0_attn_ln_b"]
    d["ls_g"] = inputs["L0_attn_ls_g"]
    d["clip_x"] = inputs["L0_ffn_clip_x"]
    d["clip_down"] = inputs["L0_ffn_clip_down"]
    d["wg_r"] = inputs["L0_ffn_w_gate"]
    d["wu_r"] = inputs["L0_ffn_w_up"]
    d["wd_r"] = inputs["L0_ffn_w_down"]
    pos = np.arange(t0 + 1, t0 + T + 1, dtype=np.float64)
    d["ps_row"] = (1.0 / (1.0 + 0.1 * np.log(pos))).astype(np.float32)[None, :]
    m = np.zeros(8, dtype=np.float32)
    m[b * 4:b * 4 + q] = 1.0
    d["mask8"] = m
    return d


def kernel(**inputs):
    T = _T
    nc = build_program(T=T)
    in_maps = [_prep_core_inputs(c, inputs, T) for c in range(8)]
    res = run_bass_kernel_spmd(nc, in_maps, core_ids=list(range(8)))
    B, Tfull, _ = inputs["L0_attn_x"].shape
    out = np.empty((B, Tfull, C), dtype=np.float32)
    for c in range(8):
        b, q = c // 4, c % 4
        out[b, q * T:(q + 1) * T, :] = np.asarray(res.results[c]["outT"]).T
    return out


# version 19
# speedup vs baseline: 1.0807x; 1.0154x over previous
import numpy as np
import ml_dtypes
from contextlib import ExitStack

import concourse.bass as bass
import concourse.mybir as mybir
from concourse.bass_utils import run_bass_kernel_spmd

F32 = mybir.dt.float32
BF16 = mybir.dt.bfloat16
I32 = mybir.dt.int32
AF = mybir.ActivationFunctionType
ALU = mybir.AluOpType
AX = mybir.AxisListType

C = 1024
CT = 8
I_FFN = 2816
IT = 22
EPS = 1e-5
QMAX = 127.0

ENGINES = ("sync", "tensor", "vector", "scalar", "gpsimd")
DMA_RR = 4
DEBUG_OUTPUTS = ()
DEBUG_SCALES = False


class Sched:

    def __init__(self):
        self.groups = []
        self.sem_count = {}
        self.res = {}
        self.floor = {e: {} for e in ENGINES}
        self.max_waited = {}
        self.dma_rr = {}

    def _rec(self, name):
        return self.res.setdefault(name, {"w": None, "r": []})

    def _bump(self, sem, delta):
        self.sem_count[sem] = self.sem_count.get(sem, 0) + delta
        return self.sem_count[sem]

    def group(self, engine, reads=(), writes=(), sem=None, dma_sem=None, waits=(),
              serialize=False):
        sch = self
        want = {}
        if dma_sem is not None:
            idx = self.dma_rr.get(dma_sem, 0)
            self.dma_rr[dma_sem] = (idx + 1) % DMA_RR
            dma_sem = f"{dma_sem}{idx}"

        def add(sw):
            if sw is None:
                return
            s, v = sw
            if s == engine:
                return
            if want.get(s, 0) < v:
                want[s] = v

        for r in reads:
            add(self._rec(r)["w"])
        for w in writes:
            rec = self._rec(w)
            add(rec["w"])
            for sw in rec["r"]:
                add(sw)
        for sw in waits:
            add(sw)
        if dma_sem is not None:
            add((dma_sem, self.sem_count.get(dma_sem, 0)))
        elif sem and sem != engine:
            add((sem, self.sem_count.get(sem, 0)))
        for s, v in want.items():
            if self.max_waited.get(s, 0) < v:
                self.max_waited[s] = v
        fl = self.floor[engine]
        pruned = {s: v for s, v in want.items() if fl.get(s, 0) < v}
        for s, v in pruned.items():
            fl[s] = v

        class G:
            __slots__ = ("ops",)

            def __init__(g):
                g.ops = []

            def op(g, fn):
                g.ops.append(fn)
                return g

            def done(g):
                if dma_sem is not None:
                    incs = [(dma_sem, 16)] * len(g.ops)
                    val = sch._bump(dma_sem, 16 * len(g.ops))
                    mysem = dma_sem
                    mode = "dma"
                else:
                    s = sem or engine
                    n = len(g.ops) if serialize else 1
                    val = sch._bump(s, n)
                    incs = [(s, 1)] * n
                    mysem = s
                    mode = ("ser", val - n) if serialize else "eng"
                    if s == engine:
                        fl2 = sch.floor[engine]
                        fl2[s] = max(fl2.get(s, 0), val)
                sch.groups.append((engine, pruned, list(g.ops), incs, mode))
                for w in writes:
                    rec = sch._rec(w)
                    rec["w"] = (mysem, val)
                    rec["r"] = []
                for r in reads:
                    sch._rec(r)["r"].append((mysem, val))
                return (mysem, val)

        return G()

    def wait_only(self, engine, waits):
        fl = self.floor[engine]
        pruned = {}
        for s, v in waits:
            if s == engine or fl.get(s, 0) >= v:
                continue
            pruned[s] = max(pruned.get(s, 0), v)
            fl[s] = pruned[s]
            if self.max_waited.get(s, 0) < v:
                self.max_waited[s] = v
        if pruned:
            self.groups.append((engine, pruned, [], [], False))

    def queue_totals(self, base):
        return [(s, v) for s, v in self.sem_count.items()
                if s.startswith(base) and s[len(base):].isdigit()]

    def mark(self):
        return len(self.groups)

    def replay(self, engine, eng, sems, lo, hi):
        for (e, waits, ops, incs, mode) in self.groups[lo:hi]:
            if e != engine:
                continue
            for s, v in sorted(waits.items()):
                eng.wait_ge(sems[s], v)
            if not ops:
                continue
            if mode == "dma":
                for fn, (s, d) in zip(ops, incs):
                    fn(eng).then_inc(sems[s], d)
            elif isinstance(mode, tuple):
                s = incs[0][0]
                start = mode[1]
                for k, fn in enumerate(ops):
                    if k > 0:
                        eng.wait_ge(sems[s], start + k)
                    fn(eng).then_inc(sems[s], 1)
            else:
                last = None
                for fn in ops:
                    last = fn(eng)
                s, d = incs[0]
                last.then_inc(sems[s], d)


def build_program(T=4096, NA=256, NF=512):
    nCA = T // NA
    nCB = nCA
    nCF = T // NF
    nc = bass.Bass("TRN2", target_bir_lowering=False, detect_race_conditions=False)

    dram = {}

    def din(name, shape, dt=F32):
        dram[name] = nc.dram_tensor(name, shape, dt, kind="ExternalInput").ap()
        return dram[name]

    def dint(name, shape, dt=F32):
        kind = "ExternalOutput" if name in DEBUG_OUTPUTS else "Internal"
        dram[name] = nc.dram_tensor(name, shape, dt, kind=kind).ap()
        return dram[name]

    xT = din("xT", [C, T])
    WqT = din("WqT", [C, C]); WkT = din("WkT", [C, C]); WvT = din("WvT", [C, C])
    WgT = din("WgT", [C, C]); WoT = din("WoT", [C, C])
    W0 = din("W0", [C, C]); ATd = din("AT", [8, C]); Bmd = din("Bm", [8, C])
    log_lr = din("log_lr", [C]); ln_g = din("ln_g", [C]); ln_b = din("ln_b", [C]); ls_g = din("ls_g", [C])
    clip_x = din("clip_x", [1]); clip_down = din("clip_down", [1])
    wg_r = din("wg_r", [I_FFN, C]); wu_r = din("wu_r", [I_FFN, C]); wd_r = din("wd_r", [C, I_FFN])
    ps_row_in = din("ps_row", [1, T])
    mask8 = din("mask8", [8])
    outT = nc.dram_tensor("outT", [C, T], F32, kind="ExternalOutput").ap()

    xa_sp = dint("xa_sp", [C, T])
    hh_sp = dint("hh_sp", [C, T], BF16); hl_sp = dint("hl_sp", [C, T], BF16)
    y0_sp = dint("y0_sp", [C, T]); gs_sp = dint("gs_sp", [C, T])
    x1_sp = dint("x1_sp", [C, T]); y_sp = dint("y_sp", [C, T])
    z_sp = dint("z_sp", [I_FFN, T])
    wqg_sp = dint("wqg_sp", [I_FFN, C], BF16)
    wqu_sp = dint("wqu_sp", [I_FFN, C], BF16)
    wqd_sp = dint("wqd_sp", [C, I_FFN], BF16)
    cc_gin = dint("cc_gin", [1024]); cc_gout = dint("cc_gout", [8, 1024])
    cc_m1in = dint("cc_m1in", [128]); cc_m1out = dint("cc_m1out", [1024])
    cc_m2in = dint("cc_m2in", [128]); cc_m2out = dint("cc_m2out", [1024])
    off_hbm = dint("off_hbm", [1024]); cg_hbm = dint("cg_hbm", [1024])

    sch = Sched()
    estack = ExitStack()
    sem_names = list(ENGINES) + ["cc"] + [f"{b}{i}" for b in ("qs", "qg", "qs2")
                                           for i in range(DMA_RR)]
    sems = {n: estack.enter_context(nc.semaphore(n)) for n in sem_names}

    def sb(name, shape, dt=F32):
        return nc.alloc_sbuf_tensor(name, shape, dt)

    lr_t = sb("lr_t", [128, CT])
    g_t = sb("g_t", [128, CT]); b_t = sb("b_t", [128, CT]); ls_t = sb("ls_t", [128, CT])
    off_t = sb("off_t", [128, CT])
    scg = sb("scg", [128, IT]); scu = sb("scu", [128, IT]); scd = sb("scd", [128, CT])
    sgq = sb("sgq", [128, IT]); suq = sb("suq", [128, IT]); sdq = sb("sdq", [128, CT])
    s1v = sb("s1v", [128, 1]); s2v = sb("s2v", [128, 1])
    s1p = sb("s1p", [128, 1]); s2p = sb("s2p", [128, 1])
    rmax = sb("rmax", [128, 16 * CT])
    rmax2 = sb("rmax2", [128, 8 * IT])

    ones1 = sb("ones1", [128, 1]); onesk1 = sb("onesk1", [1, 128])
    ginv_t = sb("ginv_t", [128, CT]); nb_t = sb("nb_t", [128, CT])
    bg_t = sb("bg_t", [128, CT]); ngate_t = sb("ngate_t", [128, CT])
    mask_t = sb("mask_t", [8, 1])
    sc1x1 = sb("sc1x1", [1, 8])
    clipt = sb("clipt", [1, 2])
    redt = sb("redt", [128, 1])
    wmax = sb("wmax", [128, 1])
    NB = 256
    NS = NB
    stv_buf = sb("stv", [1, 13 * NS])
    _SLOT = {"r1": 0, "mrn": 1, "r12": 2, "hc2": 3,
             "r1b": 0, "mrnb": 1, "psc": 2,
             "mu": 4, "var": 5, "te": 6, "te2": 7, "s0": 8, "s2": 9, "r0": 12,
             "r2": 10, "tmp1": 11, "tmp2": 12,
             "e2": 12, "a": 11, "b2": 11, "cc2": 11, "rte": 12, "v1r": 12}

    def SV(n):
        i = _SLOT[n]
        return stv_buf.ap()[:, i * NS:(i + 1) * NS]
    psb = [nc.alloc_psum_tensor(f"ps{i}", [128, 512], F32) for i in range(8)]
    RG = [list(range(8))]

    def hbm3(t, t0, N):
        return t.rearrange("(k p) t -> p k t", p=128)[:, :, t0:t0 + N]

    def sbuf3(buf, N, nt=CT):
        return buf.ap().rearrange("p (k n) -> p k n", k=nt)

    with ExitStack() as p1stack:
        _w = p1stack.enter_context
        wAh = _w(nc.sbuf_tensor("wAh", [128, 8 * C], BF16))
        wAl = _w(nc.sbuf_tensor("wAl", [128, 8 * C], BF16))
        wBh = _w(nc.sbuf_tensor("wBh", [128, 8 * C], BF16))
        wBl = _w(nc.sbuf_tensor("wBl", [128, 8 * C], BF16))
        wCh = _w(nc.sbuf_tensor("wCh", [128, 8 * C], BF16))
        wCl = _w(nc.sbuf_tensor("wCl", [128, 8 * C], BF16))
        p1a = ExitStack()
        _e = p1a.enter_context
        m1row = stv_buf
        x_b0 = _e(nc.sbuf_tensor("x_b0", [128, CT * NA], F32))
        x_b1 = _e(nc.sbuf_tensor("x_b1", [128, CT * NA], F32))
        h32_b = _e(nc.sbuf_tensor("h32_b", [128, CT * NA], F32))
        hh0 = _e(nc.sbuf_tensor("hh0", [128, CT * NA], BF16))
        hh1 = _e(nc.sbuf_tensor("hh1", [128, CT * NA], BF16))
        hl0 = _e(nc.sbuf_tensor("hl0", [128, CT * NA], BF16))
        hl1 = _e(nc.sbuf_tensor("hl1", [128, CT * NA], BF16))
        Kh_b = _e(nc.sbuf_tensor("Kh_b", [128, CT * NA], BF16))
        Kl_b = _e(nc.sbuf_tensor("Kl_b", [128, CT * NA], BF16))
        y0_b0 = _e(nc.sbuf_tensor("y0_b0", [128, CT * NA], F32))
        K_b = _e(nc.sbuf_tensor("K_b", [128, CT * NA], F32))
        G_b = K_b
        gse_b = _e(nc.sbuf_tensor("gse_b", [128, CT * (NA + 1)], F32))
        gse_c = _e(nc.sbuf_tensor("gse_c", [128, CT * (NA + 1)], F32))
        xsq_b = h32_b
        x_b = [x_b0, x_b1]
        hh_p = [hh0, hh1]
        hl_p = [hl0, hl1]
        y0_p = [y0_b0, y0_b0]
        gse_p = [gse_b, gse_c]
        scB = _e(nc.sbuf_tensor("scB", [128, NA], F32))
        wstr0 = _e(nc.sbuf_tensor("wstr0", [128, I_FFN], F32))
        wfst = wstr0
        wit = _e(nc.sbuf_tensor("wit", [128, I_FFN // 8], I32))
        wqb0 = _e(nc.sbuf_tensor("wqb0", [128, I_FFN // 8], BF16))
        wqb1 = _e(nc.sbuf_tensor("wqb1", [128, I_FFN // 8], BF16))
        winv = _e(nc.sbuf_tensor("winv", [128, 1], F32))

        class _V:
            def __init__(v, ap):
                v._ap = ap
            def ap(v):
                return v._ap
        gath = _V(gse_b.ap()[0:8, 0:1024])
        AT_s = _V(gse_b.ap()[0:8, 0:C])
        Bm_s = _V(h32_b.ap()[0:8, 0:C])
        wstr = [wstr0, wstr0]

        def xs(buf, m, N=NA, dt=4):
            return buf.ap()[:, m * N:(m + 1) * N]

        def wsl(w, kk, m):
            return w.ap()[:, kk * C + m * 128: kk * C + m * 128 + 128]

        g = sch.group("sync", writes=["const", "gath", "h32"], dma_sem="qs")
        for kk in range(CT):
            g.op(lambda e, kk=kk: e.dma_start(out=lr_t.ap()[:, kk:kk + 1], in_=log_lr[128 * kk:128 * kk + 128][:, None]))
            g.op(lambda e, kk=kk: e.dma_start(out=g_t.ap()[:, kk:kk + 1], in_=ln_g[128 * kk:128 * kk + 128][:, None]))
            g.op(lambda e, kk=kk: e.dma_start(out=b_t.ap()[:, kk:kk + 1], in_=ln_b[128 * kk:128 * kk + 128][:, None]))
            g.op(lambda e, kk=kk: e.dma_start(out=ls_t.ap()[:, kk:kk + 1], in_=ls_g[128 * kk:128 * kk + 128][:, None]))
        g.op(lambda e: e.dma_start(out=mask_t.ap(), in_=mask8[:, None]))
        g.op(lambda e: e.dma_start(out=clipt.ap()[0:1, 0:1], in_=clip_x[:, None]))
        g.op(lambda e: e.dma_start(out=clipt.ap()[0:1, 1:2], in_=clip_down[:, None]))
        g.op(lambda e: e.dma_start(out=AT_s.ap(), in_=ATd))
        g.op(lambda e: e.dma_start(out=Bm_s.ap(), in_=Bmd))
        g.done()
        g = sch.group("gpsimd", writes=["const2", "off", "rm", "rm2"])
        g.op(lambda e: e.memset(ones1.ap(), 1.0))
        g.op(lambda e: e.memset(onesk1.ap(), 1.0))
        g.op(lambda e: e.memset(off_t.ap(), 0.0))
        g.done()
        g = sch.group("scalar", reads=["const"], writes=["lr1"])
        g.op(lambda e: e.activation(lr_t.ap(), lr_t.ap(), AF.Exp))
        g.done()
        g = sch.group("vector", reads=["lr1", "const"], writes=["lr", "gder"])
        g.op(lambda e: e.tensor_scalar_min(lr_t.ap(), lr_t.ap(), 1.0))
        g.op(lambda e: e.reciprocal(ginv_t.ap(), g_t.ap()))
        g.op(lambda e: e.tensor_scalar_mul(nb_t.ap(), b_t.ap(), -1.0))
        g.op(lambda e: e.tensor_mul(bg_t.ap(), b_t.ap(), ginv_t.ap()))
        g.done()

        def load_split(src_hbm, dsth, dstl, hres, lres, negate=False, weff=False,
                       stage=None, stage_res=None, colscale=None, gacc=False):
            stg = stage if stage is not None else wfst
            sres = stage_res if stage_res is not None else ("wf0", "wf1")
            for kk in range(CT):
                rg = kk % 2
                r0, r1 = rg * C, rg * C + C
                g = sch.group("sync", writes=[sres[rg]], dma_sem="qs")
                g.op(lambda e, kk=kk, r0=r0, r1=r1, src_hbm=src_hbm: e.dma_start(
                    out=stg.ap()[:, r0:r1], in_=src_hbm[128 * kk:128 * kk + 128, :]))
                g.done()
                if colscale is not None:
                    g = sch.group("vector", reads=[sres[rg], "gder"], writes=[sres[rg]])
                    g.op(lambda e, kk=kk, r0=r0, r1=r1: e.tensor_scalar(
                        stg.ap()[:, r0:r1], stg.ap()[:, r0:r1],
                        colscale.ap()[:, kk:kk + 1], None, ALU.mult))
                    g.done()
                if gacc:
                    g = sch.group("tensor", reads=[sres[rg], "gder"], writes=["ps_cg"])
                    for half in range(2):
                        g.op(lambda e, kk=kk, r0=r0, half=half: e.matmul(
                            psb[6].ap()[32 * half:32 * half + 1, 0:512],
                            bg_t.ap()[:, kk:kk + 1],
                            stg.ap()[:, r0 + half * 512:r0 + half * 512 + 512],
                            start=(kk == 0), stop=(kk == CT - 1)))
                    g.done()
                if weff:
                    g = sch.group("tensor", reads=["gath", "h32"], writes=["psm0", "psm1"])
                    for half in range(2):
                        g.op(lambda e, kk=kk, half=half: e.matmul(
                            psb[half].ap()[:, 0:512], AT_s.ap()[:, kk * 128:kk * 128 + 128],
                            Bm_s.ap()[:, half * 512:half * 512 + 512], start=True, stop=True))
                    g.done()
                    g = sch.group("vector", reads=["psm0", "psm1", sres[rg]], writes=[sres[rg]])
                    for half in range(2):
                        g.op(lambda e, r0=r0, half=half: e.tensor_add(
                            stg.ap()[:, r0 + half * 512:r0 + half * 512 + 512],
                            stg.ap()[:, r0 + half * 512:r0 + half * 512 + 512],
                            psb[half].ap()[:, 0:512]))
                    g.done()
                if negate:
                    g = sch.group("scalar", reads=[sres[rg]], writes=[sres[rg]])
                    g.op(lambda e, r0=r0, r1=r1: e.mul(stg.ap()[:, r0:r1], stg.ap()[:, r0:r1], -1.0))
                    g.done()
                g = sch.group("scalar", reads=[sres[rg]], writes=[hres])
                g.op(lambda e, kk=kk, r0=r0, r1=r1, dsth=dsth: e.copy(
                    xs(dsth, kk, C), stg.ap()[:, r0:r1]))
                g.done()
                g = sch.group("gpsimd", reads=[sres[rg], hres], writes=[lres])
                g.op(lambda e, kk=kk, r0=r0, r1=r1, dsth=dsth, dstl=dstl: e.tensor_sub(
                    xs(dstl, kk, C), stg.ap()[:, r0:r1], xs(dsth, kk, C)))
                g.done()

        load_split(WkT, wAh, wAl, "wAh", "wAl")
        load_split(W0, wBh, wBl, "wBh", "wBl", weff=True)
        load_split(WvT, wCh, wCl, "wCh", "wCl", negate=True)
        g = sch.group("gpsimd", reads=["gath", "h32"], writes=["gse0", "gse1"])
        g.op(lambda e: e.memset(gse_b.ap(), 0.0))
        g.op(lambda e: e.memset(gse_c.ap(), 0.0))
        g.done()


        def quant_strip(w_in, wq_out, sc_tile, sc_res, s_i, cols, half=0):
            SEG = I_FFN // 8
            wqbs = [wqb0, wqb1]
            base = 1024 * half if cols <= 1024 else 0
            sres_q = [f"wf{half}"] if cols <= 1024 else ["wf0", "wf1"]
            g = sch.group("sync", writes=sres_q, dma_sem="qs2")
            g.op(lambda e, s_i=s_i, base=base, cols=cols, w_in=w_in: e.dma_start(
                out=wstr0.ap()[:, base:base + cols], in_=w_in[128 * s_i:128 * s_i + 128, :]))
            g.done()
            segs = [(c0, min(SEG, cols - c0)) for c0 in range(0, cols, SEG)]
            for pi in range(0, len(segs), 2):
                pair = segs[pi:pi + 2]
                if pi == 0:
                    g = sch.group("vector", reads=sres_q,
                                  writes=[sc_res, "winv", "wmax", "wit"], serialize=True)
                    g.op(lambda e, base=base, cols=cols: e.tensor_reduce(
                        wmax.ap(), wstr0.ap()[:, base:base + cols], AX.X, ALU.max,
                        apply_absolute_value=True))
                    g.op(lambda e, s_i=s_i, sc_tile=sc_tile: e.tensor_scalar_mul(
                        sc_tile.ap()[:, s_i:s_i + 1], wmax.ap(), 1.0 / QMAX))
                    g.op(lambda e, s_i=s_i, sc_tile=sc_tile: e.reciprocal(
                        winv.ap(), sc_tile.ap()[:, s_i:s_i + 1]))
                    g.op(lambda e, base=base, cw=pair[0][1]: e.tensor_scalar(
                        wit.ap()[:, 0:cw], wstr0.ap()[:, base:base + cw], winv.ap(),
                        None, ALU.mult))
                    g.done()
                g = sch.group("vector", reads=sres_q + ["winv"],
                              writes=["wit", "wqb0", "wqb1"])
                for bi, (c0, cw) in enumerate(pair):
                    if not (pi == 0 and bi == 0):
                        g.op(lambda e, base=base, c0=c0, cw=cw: e.tensor_scalar(
                            wit.ap()[:, 0:cw], wstr0.ap()[:, base + c0:base + c0 + cw],
                            winv.ap(), None, ALU.mult))
                    g.op(lambda e, bi=bi, cw=cw, wqbs=wqbs: e.tensor_scalar(
                        wqbs[bi].ap()[:, 0:cw], wit.ap()[:, 0:cw], -128.0, 127.0,
                        ALU.max, ALU.min))
                g.done()
                for bi, (c0, cw) in enumerate(pair):
                    g = sch.group("gpsimd", reads=[f"wqb{bi}"], dma_sem="qg")
                    g.op(lambda e, s_i=s_i, c0=c0, cw=cw, bi=bi, wq_out=wq_out,
                         wqbs=wqbs: e.dma_start(
                        out=wq_out[128 * s_i:128 * s_i + 128, c0:c0 + cw],
                        in_=wqbs[bi].ap()[:, 0:cw]))
                    g.done()

        def strip_gen():
            n = 0
            for s_i in range(IT):
                yield (wg_r, wqg_sp, scg, "wscg", s_i, C, n % 2); n += 1
            for s_i in range(IT):
                yield (wu_r, wqu_sp, scu, "wscu", s_i, C, n % 2); n += 1
            for s_i in range(CT):
                yield (wd_r, wqd_sp, scd, "wscd", s_i, I_FFN, 0)

        strips = strip_gen()

        def pump_strips(n):
            for _ in range(n):
                try:
                    quant_strip(*next(strips))
                except StopIteration:
                    return

        def mm3(g, bank, wh, wl, rhit, rlot, m, N):
            passes = [(wh, rhit), (wh, rlot), (wl, rhit)]
            for pi, (ww, rr) in enumerate(passes):
                for kk in range(CT):
                    g.op(lambda e, ww=ww, rr=rr, m=m, kk=kk, bank=bank, N=N,
                         st=(pi == 0 and kk == 0), sp=(pi == 2 and kk == CT - 1): e.matmul(
                        psb[bank].ap()[:, 0:N], wsl(ww, kk, m), xs(rr, kk, N, dt=2),
                        start=st, stop=sp))

        def stats_mm(buf, res, N, xq_buf=None, xq_res="h32", xq_w=None):
            xqb = xq_buf if xq_buf is not None else xsq_b
            xqw = xq_w if xq_w is not None else NA
            g = sch.group("scalar", reads=[res], writes=[xq_res])
            for kk in range(CT):
                g.op(lambda e, kk=kk, buf=buf, N=N: e.activation(
                    xs(xqb, kk, xqw)[:, 0:N], xs(buf, kk, N), AF.Square))
            g.done()
            g = sch.group("tensor", reads=[res, xq_res, "const2"], writes=["ps_st"])
            for kk in range(CT):
                g.op(lambda e, kk=kk, buf=buf, N=N: e.matmul(
                    psb[2].ap()[0:1, 0:N], ones1.ap(), xs(buf, kk, N),
                    start=(kk == 0), stop=(kk == CT - 1)))
            for kk in range(CT):
                g.op(lambda e, kk=kk, N=N: e.matmul(
                    psb[2].ap()[32:33, 0:N], ones1.ap(), xs(xqb, kk, xqw)[:, 0:N],
                    start=(kk == 0), stop=(kk == CT - 1)))
            g.done()

        def stats_math(N, r1_slot, mrn_slot, second=True):
            V = lambda n: SV(n)[:, 0:N]
            g = sch.group("vector", reads=["ps_st"], writes=["stv"])
            g.op(lambda e: e.tensor_scalar_mul(V("mu"), psb[2].ap()[0:1, 0:N], 1.0 / C))
            g.op(lambda e: e.tensor_scalar_mul(V("e2"), psb[2].ap()[32:33, 0:N], 1.0 / C))
            g.op(lambda e: e.tensor_mul(V("a"), V("mu"), V("mu")))
            g.op(lambda e: e.tensor_sub(V("var"), V("e2"), V("a")))
            g.op(lambda e: e.tensor_scalar_add(V("te"), V("var"), EPS))
            if second:
                g.op(lambda e: e.reciprocal(V("rte"), V("te")))
                g.op(lambda e: e.tensor_mul(V("v1r"), V("var"), V("rte")))
                g.op(lambda e: e.tensor_scalar_add(V("te2"), V("v1r"), EPS))
            g.done()
            g = sch.group("scalar", reads=["stv"], writes=["stvs"])
            g.op(lambda e: e.activation(V("s0"), V("te"), AF.Sqrt))
            if second:
                g.op(lambda e: e.activation(V("s2"), V("te2"), AF.Sqrt))
            g.done()
            g = sch.group("vector", reads=["stvs"], writes=["stv"])

            def newton(s_in, t_in, r_out):
                g.op(lambda e, s_in=s_in: e.reciprocal(V("r0"), V(s_in)))
                for srcn, dstn in (("r0", r_out), (r_out, r_out)):
                    g.op(lambda e, srcn=srcn: e.tensor_mul(V("a"), V(srcn), V(srcn)))
                    g.op(lambda e, t_in=t_in: e.tensor_mul(V("b2"), V("a"), V(t_in)))
                    g.op(lambda e: e.tensor_scalar(V("cc2"), V("b2"), -0.5, 1.5, ALU.mult, ALU.add))
                    g.op(lambda e, srcn=srcn, dstn=dstn: e.tensor_mul(V(dstn), V(srcn), V("cc2")))

            newton("s0", "te", r1_slot)
            if second:
                newton("s2", "te2", "r2")
                g.op(lambda e, r1_slot=r1_slot: e.tensor_mul(V("r12"), V(r1_slot), V("r2")))
            g.op(lambda e, r1_slot=r1_slot, mrn_slot=mrn_slot: e.scalar_tensor_tensor(
                V(mrn_slot), V("mu"), -1.0, V(r1_slot), ALU.mult, ALU.mult))
            if second:
                g.op(lambda e, mrn_slot=mrn_slot: e.tensor_mul(V("hc2"), V(mrn_slot), V("r2")))
            g.done()

        def a_front(j):
            px = j % 2
            g = sch.group("sync", writes=[f"x{px}"], dma_sem="qs")
            g.op(lambda e, j=j, px=px: e.dma_start(out=sbuf3(x_b[px], NA),
                                                   in_=hbm3(xT, j * NA, NA)))
            g.done()
            stats_mm(x_b[px], f"x{px}", NA)

        def a_mid(j):
            stats_math(NA, "r1", "mrn", second=True)

        def a_bcast(j):
            g = sch.group("tensor", reads=["stv", "const2"], writes=["ps_bc"])
            for qq, sl in enumerate(("r12", "hc2")):
                g.op(lambda e, qq=qq, sl=sl: e.matmul(
                    psb[3].ap()[:, qq * NA:(qq + 1) * NA], onesk1.ap(),
                    SV(sl)[:, 0:NA], start=True, stop=True))
            g.done()

        def a_xah(j):
            px = j % 2
            g = sch.group("vector", reads=["ps_bc", f"x{px}", "const"],
                          writes=["h32"])
            for kk in range(CT):
                g.op(lambda e, kk=kk, px=px: e.tensor_mul(
                    xs(h32_b, kk), xs(x_b[px], kk), psb[3].ap()[:, 0:NA]))
            for kk in range(CT):
                g.op(lambda e, kk=kk: e.tensor_add(
                    xs(h32_b, kk), xs(h32_b, kk), psb[3].ap()[:, NA:2 * NA]))
            for kk in range(CT):
                g.op(lambda e, kk=kk: e.tensor_scalar(
                    xs(h32_b, kk), xs(h32_b, kk), g_t.ap()[:, kk:kk + 1],
                    b_t.ap()[:, kk:kk + 1], ALU.mult, ALU.add))
            g.done()
            g = sch.group("scalar", reads=["h32"], writes=[f"hh{px}"])
            g.op(lambda e, px=px: e.copy(hh_p[px].ap(), h32_b.ap()))
            g.done()
            g = sch.group("gpsimd", reads=["h32", f"hh{px}"], writes=[f"hl{px}"])
            g.op(lambda e, px=px: e.tensor_sub(hl_p[px].ap(), h32_b.ap(), hh_p[px].ap()))
            g.done()
            g = sch.group("gpsimd", reads=[f"hh{px}", f"hl{px}"], dma_sem="qg")
            g.op(lambda e, j=j, px=px: e.dma_start(out=hbm3(hh_sp, j * NA, NA),
                                                   in_=sbuf3(hh_p[px], NA)))
            g.op(lambda e, j=j, px=px: e.dma_start(out=hbm3(hl_sp, j * NA, NA),
                                                   in_=sbuf3(hl_p[px], NA)))
            g.done()

        def a_K(i):
            px = i % 2
            for m in range(CT):
                pm = f"psm{m % 2}"
                g = sch.group("tensor", reads=[f"hh{px}", f"hl{px}", "wAh", "wAl"], writes=[pm])
                mm3(g, m % 2, wAh, wAl, hh_p[px], hl_p[px], m, NA)
                g.done()
                g = sch.group("scalar", reads=[pm], writes=["K"])
                g.op(lambda e, m=m: e.copy(xs(K_b, m), psb[m % 2].ap()[:, 0:NA]))
                g.done()
            g = sch.group("scalar", reads=["K"], writes=["Kh"])
            g.op(lambda e: e.copy(Kh_b.ap(), K_b.ap()))
            g.done()
            g = sch.group("gpsimd", reads=["K", "Kh"], writes=["Kl"])
            g.op(lambda e: e.tensor_sub(Kl_b.ap(), K_b.ap(), Kh_b.ap()))
            g.done()

        def a_tail(i):
            px = i % 2
            for m in range(CT):
                pm = f"psm{m % 2}"
                g = sch.group("tensor", reads=["Kh", "Kl", "wBh", "wBl"], writes=[pm])
                mm3(g, m % 2, wBh, wBl, Kh_b, Kl_b, m, NA)
                g.done()
                g = sch.group("scalar", reads=[pm], writes=["y0A"])
                g.op(lambda e, m=m, px=px: e.copy(xs(y0_p[px], m), psb[m % 2].ap()[:, 0:NA]))
                g.done()
                g = sch.group("tensor", reads=[f"hh{px}", f"hl{px}", "wCh", "wCl", "y0A"],
                              writes=[pm])
                mm3(g, m % 2, wCh, wCl, hh_p[px], hl_p[px], m, NA)
                g.done()
                g = sch.group("vector", reads=[pm, "y0A"], writes=["K"])
                g.op(lambda e, m=m, px=px: e.tensor_add(scB.ap()[:, 0:NA], xs(y0_p[px], m),
                                                        psb[m % 2].ap()[:, 0:NA]))
                g.op(lambda e, m=m: e.tensor_mul(xs(G_b, m), xs(K_b, m), scB.ap()[:, 0:NA]))
                g.done()
            g = sch.group("gpsimd", reads=["y0A"], dma_sem="qg")
            g.op(lambda e, i=i, px=px: e.dma_start(out=hbm3(y0_sp, i * NA, NA),
                                                   in_=sbuf3(y0_p[px], NA)))
            g.done()
            pg = i % 2
            gcur, gprv = gse_p[pg], gse_p[1 - pg]
            g = sch.group("vector", reads=["K", f"gse{1 - pg}"], writes=[f"gse{pg}"])
            for m in range(CT):
                base = m * (NA + 1)
                g.op(lambda e, base=base, gcur=gcur, gprv=gprv: e.tensor_copy(
                    gcur.ap()[:, base:base + 1], gprv.ap()[:, base + NA:base + NA + 1]))
                g.op(lambda e, base=base, m=m, gcur=gcur, gprv=gprv: e.tensor_tensor_scan(
                    gcur.ap()[:, base + 1:base + 1 + NA], xs(G_b, m), xs(G_b, m),
                    gprv.ap()[:, base + NA:base + NA + 1], ALU.add, ALU.bypass))
            g.done()
            g = sch.group("gpsimd", reads=[f"gse{pg}"], dma_sem="qg")
            g.op(lambda e, i=i, gcur=gcur: e.dma_start(
                out=hbm3(gs_sp, i * NA, NA),
                in_=gcur.ap().rearrange("p (k n) -> p k n", k=CT)[:, :, 0:NA]))
            g.done()

        for s in range(nCA + 1):
            if s >= 1:
                a_K(s - 1)
            if s < nCA:
                a_front(s)
                a_mid(s)
            if s >= 1:
                a_tail(s - 1)
            if s < nCA:
                a_bcast(s)
                a_xah(s)
            pump_strips(4 if s > 0 else 0)
        pump_strips(64)

        load_split(WqT, wAh, wAl, "wAh", "wAl")
        load_split(WgT, wBh, wBl, "wBh", "wBl", colscale=ginv_t, gacc=True)
        load_split(WoT, wCh, wCl, "wCh", "wCl")
        g = sch.group("scalar", reads=["ps_cg"], writes=["offrow", "stv"])
        for half in range(2):
            g.op(lambda e, half=half: e.mul(
                m1row.ap()[0:1, half * 512:half * 512 + 512],
                psb[6].ap()[32 * half:32 * half + 1, 0:512], -1.0))
        g.done()
        g = sch.group("gpsimd", reads=["offrow"], dma_sem="qg")
        g.op(lambda e: e.dma_start(out=cg_hbm[None, :], in_=m1row.ap()[0:1, 0:1024]))
        g.done()
        g = sch.group("sync", writes=["cgt"], dma_sem="qs", waits=sch.queue_totals("qg"))
        for m in range(CT):
            g.op(lambda e, m=m: e.dma_start(out=ngate_t.ap()[:, m:m + 1],
                                            in_=cg_hbm[m * 128:m * 128 + 128][:, None]))
        g.done()

        pg_last = (nCA - 1) % 2
        glast = gse_b if pg_last == 0 else gse_c
        g = sch.group("gpsimd", reads=[f"gse{pg_last}"], dma_sem="qg")
        for m in range(CT):
            g.op(lambda e, m=m, glast=glast: e.dma_start(
                out=cc_gin[m * 128:m * 128 + 128][:, None],
                in_=glast.ap()[:, m * (NA + 1) + NA:m * (NA + 1) + NA + 1]))
        g.done()
        g = sch.group("gpsimd", writes=["ccg"], sem="cc", waits=sch.queue_totals("qg"))
        g.op(lambda e: e.collective_compute("AllGather", ALU.bypass, replica_groups=RG,
                                            ins=[cc_gin.opt()], outs=[cc_gout.opt()]))
        g.done()
        g = sch.group("sync", reads=["ccg"], writes=["gath", "gse0"], dma_sem="qs")
        g.op(lambda e: e.dma_start(out=gath.ap(), in_=cc_gout))
        g.done()
        for half in range(2):
            g = sch.group("tensor", reads=["gath", "const"], writes=["ps_bc"])
            g.op(lambda e, half=half: e.matmul(
                psb[3].ap()[0:1, 0:512], mask_t.ap(),
                gath.ap()[:, half * 512:half * 512 + 512], start=True, stop=True))
            g.done()
            g = sch.group("scalar", reads=["ps_bc"], writes=["offrow", "stv"])
            g.op(lambda e, half=half: e.copy(m1row.ap()[0:1, half * 512:half * 512 + 512],
                                             psb[3].ap()[0:1, 0:512]))
            g.done()
        g = sch.group("gpsimd", reads=["offrow"], dma_sem="qg")
        g.op(lambda e: e.dma_start(out=off_hbm[None, :], in_=m1row.ap()[0:1, 0:1024]))
        g.done()
        g = sch.group("sync", writes=["off"], dma_sem="qs", waits=sch.queue_totals("qg"))
        for m in range(CT):
            g.op(lambda e, m=m: e.dma_start(out=off_t.ap()[:, m:m + 1],
                                            in_=off_hbm[m * 128:m * 128 + 128][:, None]))
        g.done()

        snapAB = dict(sch.sem_count)
        for e in ENGINES:
            sch.wait_only(e, [(s, v) for s, v in snapAB.items()])
        p1a.close()

        p1b = ExitStack()
        _b = p1b.enter_context
        hhB = [_b(nc.sbuf_tensor(f"hhB{p}", [128, CT * NB], BF16)) for p in range(2)]
        hlB = [_b(nc.sbuf_tensor(f"hlB{p}", [128, CT * NB], BF16)) for p in range(2)]
        x1B = [_b(nc.sbuf_tensor(f"x1B{p}", [128, CT * NB], F32)) for p in range(2)]
        y0B = [_b(nc.sbuf_tensor(f"y0B{p}", [128, CT * NB], F32)) for p in range(2)]
        gseB = [_b(nc.sbuf_tensor(f"gseB{p}", [128, CT * (NB + 1)], F32)) for p in range(2)]
        xahB = _b(nc.sbuf_tensor("xahB", [128, CT * NB], BF16))
        xalB = _b(nc.sbuf_tensor("xalB", [128, CT * NB], BF16))
        h32B = _b(nc.sbuf_tensor("h32B", [128, CT * NB], F32))
        scAB = _b(nc.sbuf_tensor("scAB", [128, NB], F32))
        scBB = _b(nc.sbuf_tensor("scBB", [128, NB], F32))
        def b_loads(j):
            px = j % 2
            for dst, res_, srcp in [(hhB[px], f"hh{px}", hh_sp), (hlB[px], f"hl{px}", hl_sp),
                                    (y0B[px], f"y0{px}", y0_sp)]:
                g = sch.group("sync", writes=[res_], dma_sem="qs")
                g.op(lambda e, j=j, dst=dst, srcp=srcp: e.dma_start(
                    out=sbuf3(dst, NB), in_=hbm3(srcp, j * NB, NB)))
                g.done()
            g = sch.group("sync", writes=[f"gse{px}"], dma_sem="qs")
            g.op(lambda e, j=j, px=px: e.dma_start(
                out=gseB[px].ap().rearrange("p (k n) -> p k n", k=CT)[:, :, 0:NB],
                in_=hbm3(gs_sp, j * NB, NB)))
            g.done()
            psl = SV("psc") if px == 0 else SV("hc2")
            g = sch.group("sync", writes=[f"psc{px}"], dma_sem="qs")
            g.op(lambda e, j=j, psl=psl: e.dma_start(out=psl[:, 0:NB],
                                                     in_=ps_row_in[0:1, j * NB:j * NB + NB]))
            g.done()

        def b_main(i):
            px = i % 2
            psl = SV("psc") if px == 0 else SV("hc2")
            g = sch.group("tensor", reads=[f"psc{px}", "const2"], writes=["ps_ps"])
            g.op(lambda e, psl=psl: e.matmul(psb[7].ap()[:, 0:NB], onesk1.ap(), psl[:, 0:NB],
                                             start=True, stop=True))
            g.done()
            for m in range(CT):
                pm = f"psm{m % 2}"
                pgm = f"psg{m % 2}"
                g = sch.group("tensor", reads=[f"hh{px}", f"hl{px}", "wAh", "wAl"], writes=[pm])
                mm3(g, m % 2, wAh, wAl, hhB[px], hlB[px], m, NB)
                g.done()
                g = sch.group("tensor", reads=[f"hh{px}", f"hl{px}", "wBh", "wBl"], writes=[pgm])
                mm3(g, 4 + m % 2, wBh, wBl, hhB[px], hlB[px], m, NB)
                g.done()
                g = sch.group("scalar", reads=[pgm, "cgt"], writes=["h32B"])
                g.op(lambda e, m=m: e.activation(xs(h32B, m, NB), psb[4 + m % 2].ap()[:, 0:NB],
                                                 AF.Sigmoid, bias=ngate_t.ap()[:, m:m + 1]))
                g.done()
                g = sch.group("vector",
                              reads=[f"gse{px}", "off", "ps_ps", pm, f"y0{px}", "h32B", "lr"],
                              writes=["h32B"])
                base = m * (NB + 1)
                g.op(lambda e, base=base, m=m, px=px: e.scalar_tensor_tensor(
                    scAB.ap(), gseB[px].ap()[:, base:base + NB], off_t.ap()[:, m:m + 1],
                    psb[7].ap()[:, 0:NB], ALU.add, ALU.mult))
                g.op(lambda e, m=m: e.scalar_tensor_tensor(
                    scBB.ap(), psb[m % 2].ap()[:, 0:NB], lr_t.ap()[:, m:m + 1], scAB.ap(),
                    ALU.mult, ALU.mult))
                g.op(lambda e, m=m, px=px: e.tensor_sub(scAB.ap(), xs(y0B[px], m, NB),
                                                        scBB.ap()))
                g.op(lambda e, m=m: e.tensor_mul(xs(h32B, m, NB), scAB.ap(), xs(h32B, m, NB)))
                g.done()
            g = sch.group("scalar", reads=["h32B"], writes=["xah"])
            g.op(lambda e: e.copy(xahB.ap(), h32B.ap()))
            g.done()
            g = sch.group("gpsimd", reads=["h32B", "xah"], writes=["xal"])
            g.op(lambda e: e.tensor_sub(xalB.ap(), h32B.ap(), xahB.ap()))
            g.done()
            g = sch.group("vector", reads=[f"hh{px}", f"hl{px}", "gder", "const"],
                          writes=[f"x1{px}"])
            for kk in range(CT):
                g.op(lambda e, kk=kk, px=px: e.tensor_add(
                    scAB.ap(), xs(hhB[px], kk, NB), xs(hlB[px], kk, NB)))
                g.op(lambda e, kk=kk: e.tensor_scalar(
                    scAB.ap(), scAB.ap(), nb_t.ap()[:, kk:kk + 1],
                    ginv_t.ap()[:, kk:kk + 1], ALU.add, ALU.mult))
                g.op(lambda e, kk=kk, px=px: e.tensor_add(
                    xs(x1B[px], kk, NB), xs(x1B[px], kk, NB), scAB.ap()))
            g.done()
            for m in range(CT):
                pm = f"psm{m % 2}"
                g = sch.group("tensor", reads=["xah", "xal", "wCh", "wCl"], writes=[pm])
                mm3(g, m % 2, wCh, wCl, xahB, xalB, m, NB)
                g.done()
                g = sch.group("vector", reads=[pm, "const"], writes=[f"x1{px}"])
                g.op(lambda e, m=m, px=px: e.scalar_tensor_tensor(
                    xs(x1B[px], m, NB), psb[m % 2].ap()[:, 0:NB], ls_t.ap()[:, m:m + 1],
                    xs(x1B[px], m, NB), ALU.mult, ALU.add))
                g.done()
            g = sch.group("gpsimd", reads=[f"x1{px}"], dma_sem="qg")
            g.op(lambda e, i=i, px=px: e.dma_start(out=hbm3(x1_sp, i * NB, NB),
                                                   in_=sbuf3(x1B[px], NB)))
            g.done()
            stats_mm(x1B[px], f"x1{px}", NB, xq_buf=h32B, xq_res="h32B", xq_w=NB)
            stats_math(NB, "r1b", "mrnb", second=False)

        def b_y(i):
            px = i % 2
            g = sch.group("tensor", reads=["stv", "const2"], writes=["ps_bc"])
            g.op(lambda e: e.matmul(psb[3].ap()[:, 0:2 * NB], onesk1.ap(),
                                    stv_buf.ap()[:, 0:2 * NB], start=True, stop=True))
            g.done()
            g = sch.group("vector", reads=["ps_bc", f"x1{px}"], writes=["h32B", "rm"])
            for kk in range(CT):
                g.op(lambda e, kk=kk, px=px: e.tensor_mul(
                    xs(h32B, kk, NB), xs(x1B[px], kk, NB), psb[3].ap()[:, 0:NB]))
            for kk in range(CT):
                g.op(lambda e, kk=kk: e.tensor_add(xs(h32B, kk, NB), xs(h32B, kk, NB),
                                                   psb[3].ap()[:, NB:2 * NB]))
            for kk in range(CT):
                g.op(lambda e, kk=kk, i=i: e.tensor_reduce(
                    rmax.ap()[:, i * CT + kk:i * CT + kk + 1], xs(h32B, kk, NB), AX.X,
                    ALU.max, apply_absolute_value=True))
            g.done()
            g = sch.group("gpsimd", reads=["h32B"], dma_sem="qg")
            g.op(lambda e, i=i: e.dma_start(out=hbm3(y_sp, i * NB, NB), in_=sbuf3(h32B, NB)))
            g.done()

        def b_loadx(j):
            px = j % 2
            g = sch.group("sync", writes=[f"x1{px}"], dma_sem="qs")
            g.op(lambda e, j=j, px=px: e.dma_start(out=sbuf3(x1B[px], NB),
                                                   in_=hbm3(xT, j * NB, NB)))
            g.done()

        nCBB = T // NB
        b_loadx(0)
        for s in range(nCBB + 2):
            if s >= 2:
                b_y(s - 2)
            if s < nCBB:
                b_loads(s)
                if s >= 1:
                    b_loadx(s)
            if s >= 1 and s - 1 < nCBB:
                b_main(s - 1)

        part1 = sch.mark()
        with nc.Block() as blk:
            for e in ENGINES:
                def closure(eng, e=e):
                    sch.replay(e, eng, sems, 0, part1)
                getattr(blk, e)(closure)
        p1b.close()

    with ExitStack() as p2stack:
        _e2 = p2stack.enter_context
        wqgT = _e2(nc.sbuf_tensor("wqgT", [128, CT * I_FFN], BF16))
        wquT = _e2(nc.sbuf_tensor("wquT", [128, CT * I_FFN], BF16))
        wqdT = _e2(nc.sbuf_tensor("wqdT", [128, IT * C], BF16))
        y_bf0 = _e2(nc.sbuf_tensor("y_bf0", [128, CT * NF], F32))
        y_bf1 = y_bf0
        xq_b = _e2(nc.sbuf_tensor("xq_b", [128, CT * NF], BF16))
        itq = _e2(nc.sbuf_tensor("itq", [128, NF], I32))
        silu0 = _e2(nc.sbuf_tensor("silu0", [128, NF], F32))
        silu1 = _e2(nc.sbuf_tensor("silu1", [128, NF], F32))
        z_t0 = _e2(nc.sbuf_tensor("z_t0", [128, NF], F32))
        z_t1 = _e2(nc.sbuf_tensor("z_t1", [128, NF], F32))
        z_t2 = _e2(nc.sbuf_tensor("z_t2", [128, NF], F32))
        z_t3 = _e2(nc.sbuf_tensor("z_t3", [128, NF], F32))
        zq_t0 = _e2(nc.sbuf_tensor("zq_t0", [128, NF], BF16))
        zq_t1 = _e2(nc.sbuf_tensor("zq_t1", [128, NF], BF16))
        zq_t2 = _e2(nc.sbuf_tensor("zq_t2", [128, NF], BF16))
        zq_t3 = _e2(nc.sbuf_tensor("zq_t3", [128, NF], BF16))
        out_b = _e2(nc.sbuf_tensor("out_b", [128, CT * NF], F32))

        class _V2:
            def __init__(v, ap):
                v._ap = ap
            def ap(v):
                return v._ap
        m2row = _V2(out_b.ap()[0:1, 0:1024])
        dbgbufS = _e2(nc.sbuf_tensor("dbgbufS", [128, 64], F32))

        y_bufs = [y_bf0, y_bf1]
        silu_t = [silu0, silu1]
        z_t = [z_t0, z_t1]
        zD_t = [z_t0, z_t1, z_t2, z_t3]
        zqD_t = [zq_t0, zq_t1, zq_t2, zq_t3]

        snap = dict(sch.sem_count)
        for e in ENGINES:
            sch.wait_only(e, [(s, v) for s, v in snap.items()])

        g = sch.group("sync", writes=["wqg"], dma_sem="qs")
        for kk in range(CT):
            g.op(lambda e, kk=kk: e.dma_start_transpose(
                wqgT.ap()[:, kk * I_FFN:(kk + 1) * I_FFN], wqg_sp[:, 128 * kk:128 * kk + 128]))
        g.done()
        g = sch.group("sync", writes=["wqu"], dma_sem="qs")
        for kk in range(CT):
            g.op(lambda e, kk=kk: e.dma_start_transpose(
                wquT.ap()[:, kk * I_FFN:(kk + 1) * I_FFN], wqu_sp[:, 128 * kk:128 * kk + 128]))
        g.done()
        g = sch.group("sync", writes=["wqd"], dma_sem="qs")
        for kk in range(IT):
            g.op(lambda e, kk=kk: e.dma_start_transpose(
                wqdT.ap()[:, kk * C:(kk + 1) * C], wqd_sp[:, 128 * kk:128 * kk + 128]))
        g.done()

        g = sch.group("vector", reads=["rm"], writes=["redm"])
        g.op(lambda e: e.tensor_reduce(redt.ap(), rmax.ap()[:, 0:(T // NB) * CT], AX.X, ALU.max))
        g.done()
        g = sch.group("gpsimd", reads=["redm"], dma_sem="qg")
        g.op(lambda e: e.dma_start(out=cc_m1in[:, None], in_=redt.ap()))
        g.done()
        g = sch.group("gpsimd", writes=["ccm1"], sem="cc", waits=sch.queue_totals("qg"))
        g.op(lambda e: e.collective_compute("AllGather", ALU.bypass, replica_groups=RG,
                                            ins=[cc_m1in.opt()], outs=[cc_m1out.opt()]))
        g.done()
        g = sch.group("sync", reads=["ccm1"], writes=["m1row"], dma_sem="qs")
        g.op(lambda e: e.dma_start(out=m2row.ap(), in_=cc_m1out[None, :]))
        g.done()
        g = sch.group("scalar", reads=["const"], writes=["sig1"])
        g.op(lambda e: e.activation(sc1x1.ap()[0:1, 0:1], clipt.ap()[0:1, 0:1], AF.Sigmoid))
        g.done()
        g = sch.group("vector", reads=["m1row", "sig1"], writes=["s1s"], serialize=True)
        g.op(lambda e: e.tensor_reduce(sc1x1.ap()[0:1, 1:2], m2row.ap(), AX.X, ALU.max))
        g.op(lambda e: e.tensor_mul(sc1x1.ap()[0:1, 2:3], sc1x1.ap()[0:1, 1:2], sc1x1.ap()[0:1, 0:1]))
        g.op(lambda e: e.tensor_scalar_mul(sc1x1.ap()[0:1, 2:3], sc1x1.ap()[0:1, 2:3], 1.0 / QMAX))
        g.op(lambda e: e.reciprocal(sc1x1.ap()[0:1, 3:4], sc1x1.ap()[0:1, 2:3]))
        g.done()
        g = sch.group("tensor", reads=["s1s", "const2"], writes=["ps_bc"])
        g.op(lambda e: e.matmul(psb[3].ap()[:, 0:2], onesk1.ap(), sc1x1.ap()[0:1, 2:4],
                                start=True, stop=True))
        g.done()
        g = sch.group("scalar", reads=["ps_bc"], writes=["s1v"])
        g.op(lambda e: e.copy(s1p.ap(), psb[3].ap()[:, 0:1]))
        g.op(lambda e: e.copy(s1v.ap(), psb[3].ap()[:, 1:2]))
        g.done()
        g = sch.group("vector", reads=["s1v", "wscg", "wscu"], writes=["sq1"])
        g.op(lambda e: e.tensor_scalar(sgq.ap(), scg.ap(), s1p.ap(), None, ALU.mult))
        g.op(lambda e: e.tensor_scalar(suq.ap(), scu.ap(), s1p.ap(), None, ALU.mult))
        g.done()

        for i in range(nCF):
            t0 = i * NF
            p = i % 2
            ybr = "yb"
            g = sch.group("sync", writes=[ybr], dma_sem="qs")
            g.op(lambda e, t0=t0, p=p: e.dma_start(out=sbuf3(y_bufs[p], NF), in_=hbm3(y_sp, t0, NF)))
            g.done()
            g = sch.group("vector", reads=[ybr, "s1v"], writes=["xq"])
            for kk in range(CT):
                g.op(lambda e, kk=kk, p=p: e.tensor_scalar(
                    itq.ap(), y_bufs[p].ap()[:, kk * NF:(kk + 1) * NF], s1v.ap(), None, ALU.mult))
                g.op(lambda e, kk=kk: e.tensor_scalar(
                    xq_b.ap()[:, kk * NF:(kk + 1) * NF], itq.ap(), -128.0, 127.0, ALU.max, ALU.min))
            g.done()
            for kb in range(IT):
                pp = kb % 2
                g = sch.group("tensor", reads=["xq", "wqg"], writes=[f"psf{pp}"])
                for kk in range(CT):
                    g.op(lambda e, kb=kb, kk=kk, pp=pp: e.matmul(
                        psb[pp].ap()[:, 0:NF],
                        wqgT.ap()[:, kk * I_FFN + kb * 128: kk * I_FFN + kb * 128 + 128],
                        xq_b.ap()[:, kk * NF:(kk + 1) * NF], start=(kk == 0), stop=(kk == CT - 1)))
                g.done()
                g = sch.group("scalar", reads=[f"psf{pp}", "sq1"], writes=[f"sil{pp}"])
                g.op(lambda e, kb=kb, pp=pp: e.activation(
                    silu_t[pp].ap(), psb[pp].ap()[:, 0:NF], AF.Silu, scale=sgq.ap()[:, kb:kb + 1]))
                g.done()
                g = sch.group("tensor", reads=["xq", "wqu"], writes=[f"psf{2 + pp}"])
                for kk in range(CT):
                    g.op(lambda e, kb=kb, kk=kk, pp=pp: e.matmul(
                        psb[2 + pp].ap()[:, 0:NF],
                        wquT.ap()[:, kk * I_FFN + kb * 128: kk * I_FFN + kb * 128 + 128],
                        xq_b.ap()[:, kk * NF:(kk + 1) * NF], start=(kk == 0), stop=(kk == CT - 1)))
                g.done()
                g = sch.group("vector", reads=[f"psf{2 + pp}", f"sil{pp}", "sq1"],
                              writes=[f"zt{pp}", "rm2"])
                g.op(lambda e, kb=kb, pp=pp: e.scalar_tensor_tensor(
                    z_t[pp].ap(), silu_t[pp].ap(), suq.ap()[:, kb:kb + 1], psb[2 + pp].ap()[:, 0:NF],
                    ALU.mult, ALU.mult))
                g.op(lambda e, kb=kb, pp=pp, i=i: e.tensor_reduce(
                    rmax2.ap()[:, i * IT + kb:i * IT + kb + 1], z_t[pp].ap(), AX.X, ALU.max,
                    apply_absolute_value=True))
                g.done()
                g = sch.group("gpsimd", reads=[f"zt{pp}"], dma_sem="qg")
                g.op(lambda e, t0=t0, kb=kb, pp=pp: e.dma_start(
                    out=z_sp[kb * 128:kb * 128 + 128, t0:t0 + NF], in_=z_t[pp].ap()))
                g.done()

        g = sch.group("vector", reads=["rm2"], writes=["redm2"])
        g.op(lambda e, nCF=nCF: e.tensor_reduce(redt.ap(), rmax2.ap()[:, 0:nCF * IT], AX.X, ALU.max))
        g.done()
        g = sch.group("gpsimd", reads=["redm2"], dma_sem="qg")
        g.op(lambda e: e.dma_start(out=cc_m2in[:, None], in_=redt.ap()))
        g.done()
        g = sch.group("gpsimd", writes=["ccm2"], sem="cc", waits=sch.queue_totals("qg"))
        g.op(lambda e: e.collective_compute("AllGather", ALU.bypass, replica_groups=RG,
                                            ins=[cc_m2in.opt()], outs=[cc_m2out.opt()]))
        g.done()
        g = sch.group("sync", reads=["ccm2"], writes=["m2row"], dma_sem="qs")
        g.op(lambda e: e.dma_start(out=m2row.ap(), in_=cc_m2out[None, :]))
        g.done()
        g = sch.group("scalar", reads=["const"], writes=["sig2"])
        g.op(lambda e: e.activation(sc1x1.ap()[0:1, 4:5], clipt.ap()[0:1, 1:2], AF.Sigmoid))
        g.done()
        g = sch.group("vector", reads=["m2row", "sig2"], writes=["s2s"], serialize=True)
        g.op(lambda e: e.tensor_reduce(sc1x1.ap()[0:1, 5:6], m2row.ap(), AX.X, ALU.max))
        g.op(lambda e: e.tensor_mul(sc1x1.ap()[0:1, 6:7], sc1x1.ap()[0:1, 5:6], sc1x1.ap()[0:1, 4:5]))
        g.op(lambda e: e.tensor_scalar_mul(sc1x1.ap()[0:1, 6:7], sc1x1.ap()[0:1, 6:7], 1.0 / QMAX))
        g.op(lambda e: e.reciprocal(sc1x1.ap()[0:1, 7:8], sc1x1.ap()[0:1, 6:7]))
        g.done()
        g = sch.group("tensor", reads=["s2s", "const2"], writes=["ps_bc2"])
        g.op(lambda e: e.matmul(psb[6].ap()[:, 0:2], onesk1.ap(), sc1x1.ap()[0:1, 6:8],
                                start=True, stop=True))
        g.done()
        g = sch.group("scalar", reads=["ps_bc2"], writes=["s2v"])
        g.op(lambda e: e.copy(s2p.ap(), psb[6].ap()[:, 0:1]))
        g.op(lambda e: e.copy(s2v.ap(), psb[6].ap()[:, 1:2]))
        g.done()
        g = sch.group("vector", reads=["s2v", "wscd"], writes=["sqd"])
        g.op(lambda e: e.tensor_scalar(sdq.ap(), scd.ap(), s2p.ap(), None, ALU.mult))
        g.done()

        zg = 0
        for i in range(nCF):
            t0 = i * NF
            p = i % 2
            ybr = "yb"
            g = sch.group("sync", writes=[ybr], dma_sem="qs")
            g.op(lambda e, t0=t0, p=p: e.dma_start(out=sbuf3(y_bufs[p], NF), in_=hbm3(x1_sp, t0, NF)))
            g.done()
            for kb in range(IT):
                pp = zg % 4
                zg += 1
                g = sch.group("sync", writes=[f"zt{pp}"], dma_sem="qs")
                g.op(lambda e, t0=t0, kb=kb, pp=pp: e.dma_start(
                    out=zD_t[pp].ap(), in_=z_sp[kb * 128:kb * 128 + 128, t0:t0 + NF]))
                g.done()
                g = sch.group("vector", reads=[f"zt{pp}", "s2v"], writes=[f"zq{pp}"])
                g.op(lambda e, pp=pp: e.tensor_scalar(itq.ap(), zD_t[pp].ap(), s2v.ap(), None, ALU.mult))
                g.op(lambda e, pp=pp: e.tensor_scalar(zqD_t[pp].ap(), itq.ap(), -128.0, 127.0,
                                                      ALU.max, ALU.min))
                g.done()
                psw = ["psd"] if not (i == 0 and kb == 0) else [
                    "psd", "psf0", "psf1", "psf2", "psf3", "ps_bc2"]
                g = sch.group("tensor", reads=[f"zq{pp}", "wqd"], writes=psw)
                for mm in range(CT):
                    g.op(lambda e, kb=kb, mm=mm, pp=pp: e.matmul(
                        psb[mm].ap()[:, 0:NF],
                        wqdT.ap()[:, kb * C + mm * 128: kb * C + mm * 128 + 128],
                        zqD_t[pp].ap(), start=(kb == 0), stop=(kb == IT - 1)))
                g.done()
            g = sch.group("scalar", reads=["psd", "sqd"], writes=["outb"])
            for mm in range(CT):
                g.op(lambda e, mm=mm: e.activation(
                    out_b.ap()[:, mm * NF:(mm + 1) * NF], psb[mm].ap()[:, 0:NF],
                    AF.Copy, scale=sdq.ap()[:, mm:mm + 1]))
            g.done()
            g = sch.group("vector", reads=["outb", ybr], writes=["outb"])
            for kk in range(CT):
                g.op(lambda e, kk=kk, p=p: e.tensor_add(
                    out_b.ap()[:, kk * NF:(kk + 1) * NF], out_b.ap()[:, kk * NF:(kk + 1) * NF],
                    y_bufs[p].ap()[:, kk * NF:(kk + 1) * NF]))
            g.done()
            g = sch.group("sync", reads=["outb"], dma_sem="qs")
            g.op(lambda e, t0=t0: e.dma_start(out=hbm3(outT, t0, NF), in_=sbuf3(out_b, NF)))
            g.done()

        if DEBUG_SCALES:
            dbg_sc = dint("dbg_sc", [128, 64])
            g = sch.group("vector", reads=["wscg", "wscu", "wscd", "sq1", "sqd", "s1v", "s2v"],
                          writes=["dbgsc"])
            g.op(lambda e: e.tensor_copy(dbgbufS.ap()[:, 0:IT], scg.ap()))
            g.op(lambda e: e.tensor_copy(dbgbufS.ap()[:, 22:22 + IT], scu.ap()))
            g.op(lambda e: e.tensor_copy(dbgbufS.ap()[:, 44:44 + CT], scd.ap()))
            g.op(lambda e: e.tensor_copy(dbgbufS.ap()[:, 52:53], s1v.ap()))
            g.op(lambda e: e.tensor_copy(dbgbufS.ap()[:, 53:54], s2v.ap()))
            g.op(lambda e: e.tensor_copy(dbgbufS.ap()[:, 54:55], s1p.ap()))
            g.op(lambda e: e.tensor_copy(dbgbufS.ap()[:, 55:56], s2p.ap()))
            g.done()
            g = sch.group("gpsimd", reads=["dbgsc"], dma_sem="qg")
            g.op(lambda e: e.dma_start(out=dbg_sc, in_=dbgbufS.ap()))
            g.done()
        sch.wait_only("sync", sch.queue_totals("qs"))
        sch.wait_only("gpsimd", sch.queue_totals("qg"))

        part2 = sch.mark()
        with nc.Block() as blk2:
            for e in ENGINES:
                def closure2(eng, e=e):
                    sch.replay(e, eng, sems, part1, part2)
                getattr(blk2, e)(closure2)

    estack.close()
    return nc


_T = 4096


def _prep_core_inputs(core, inputs, T):
    b, q = core // 4, core % 4
    t0 = q * T
    d = {}
    d["xT"] = np.ascontiguousarray(inputs["L0_attn_x"][b, t0:t0 + T, :].T)
    d["WqT"] = np.ascontiguousarray(inputs["L0_attn_Wq"].T)
    d["WkT"] = np.ascontiguousarray(inputs["L0_attn_Wk"].T)
    d["WvT"] = np.ascontiguousarray(inputs["L0_attn_Wv"].T)
    d["WgT"] = np.ascontiguousarray(inputs["L0_attn_Wg"].T)
    d["WoT"] = np.ascontiguousarray(inputs["L0_attn_Wo"].T)
    d["W0"] = np.ascontiguousarray(inputs["L0_attn_W0"])
    d["AT"] = np.ascontiguousarray(inputs["L0_attn_A_init"].T)
    d["Bm"] = np.ascontiguousarray(inputs["L0_attn_B_init"])
    d["log_lr"] = inputs["L0_attn_log_lr"]
    d["ln_g"] = inputs["L0_attn_ln_g"]
    d["ln_b"] = inputs["L0_attn_ln_b"]
    d["ls_g"] = inputs["L0_attn_ls_g"]
    d["clip_x"] = inputs["L0_ffn_clip_x"]
    d["clip_down"] = inputs["L0_ffn_clip_down"]
    d["wg_r"] = inputs["L0_ffn_w_gate"]
    d["wu_r"] = inputs["L0_ffn_w_up"]
    d["wd_r"] = inputs["L0_ffn_w_down"]
    pos = np.arange(t0 + 1, t0 + T + 1, dtype=np.float64)
    d["ps_row"] = (1.0 / (1.0 + 0.1 * np.log(pos))).astype(np.float32)[None, :]
    m = np.zeros(8, dtype=np.float32)
    m[b * 4:b * 4 + q] = 1.0
    d["mask8"] = m
    return d


def kernel(**inputs):
    T = _T
    nc = build_program(T=T)
    in_maps = [_prep_core_inputs(c, inputs, T) for c in range(8)]
    res = run_bass_kernel_spmd(nc, in_maps, core_ids=list(range(8)))
    B, Tfull, _ = inputs["L0_attn_x"].shape
    out = np.empty((B, Tfull, C), dtype=np.float32)
    for c in range(8):
        b, q = c // 4, c % 4
        out[b, q * T:(q + 1) * T, :] = np.asarray(res.results[c]["outT"]).T
    return out
